# version 05
# baseline (speedup 1.0000x reference)
import contextlib
import numpy as np
import concourse.bass as bass
import concourse.mybir as mybir

F32 = mybir.dt.float32
BF16 = mybir.dt.bfloat16
U32 = mybir.dt.uint32
AF = mybir.ActivationFunctionType
ALU = mybir.AluOpType
AX = mybir.AxisListType

B = 2
T = 2048
D = 1024
DX = D // 128
HID = 4096
HX = HID // 128
NH = 16
HD = 64
NE = 8
NT = B * T

ENGINES = ("tensor", "vector", "scalar", "sync", "gpsimd")

AB_COLS = 17856


class _Alloc:
    def __init__(self, arena, cols):
        self.arena = arena
        self.cols = cols
        self.off = 0

    def take(self, ncols):
        assert self.off + ncols <= self.cols, \
            f"arena overflow: {self.off}+{ncols} > {self.cols}"
        ap = self.arena[:, self.off:self.off + ncols]
        self.off += ncols
        return ap


class Sem:
    def __init__(self, h, name):
        self.h = h
        self.name = name
        self.n = 0


class Prog:

    def __init__(self, nc, ctx):
        self.nc = nc
        self.ctx = ctx
        self.q = {e: [] for e in ENGINES}
        self.sems = {}

    def sem(self, name):
        s = Sem(self.ctx.enter_context(self.nc.semaphore(name)), name)
        self.sems[name] = s
        return s

    def emit(self, eng, fn):
        self.q[eng].append(fn)

    def op(self, eng, name, *args, inc=None, incv=1, **kw):
        if inc is not None:
            inc.n += incv
        def f(e, name=name, args=args, kw=kw, inc=inc, incv=incv):
            ins = getattr(e, name)(*args, **kw)
            if inc is not None:
                ins.then_inc(inc.h, incv)
        self.emit(eng, f)

    def wait(self, eng, sem, val=None):
        v = sem.n if val is None else val
        if v <= 0:
            return
        self.emit(eng, lambda e, s=sem.h, v=v: e.wait_ge(s, v))

    def dma(self, eng, out, in_, sem, serialize=True):
        if serialize and sem.n > 0:
            self.wait(eng, sem, sem.n)
        self.op(eng, "dma_start", out=out, in_=in_, inc=sem, incv=16)

    def run(self, block):
        for eng in ENGINES:
            ops = self.q[eng]
            getattr(block, eng)(lambda e, ops=ops: [f(e) for f in ops])


ROPE_PERM = np.array(list(range(0, 16)) + list(range(32, 48))
                     + list(range(16, 32)) + list(range(48, 64)))
ROPE_SHUF_MASK = list(range(16, 32)) + list(range(16))


def host_constants():
    c = {}
    c["c_ident_f"] = np.eye(128, dtype=np.float32)
    c["c_eps"] = np.tile(np.array([[1e-5, 1e-6]], dtype=np.float32), (128, 1))
    c["c_ident_b"] = np.eye(128, dtype=np.float32)
    c["c_neg8ident"] = (-8.0 * np.eye(128)).astype(np.float32)
    c["c_ones_col"] = np.ones((128, 1), dtype=np.float32)
    c["c_ones_row"] = np.ones((1, 128), dtype=np.float32)
    c["c_ones_row512"] = np.ones((1, 512), dtype=np.float32)
    c["c_iota8"] = np.tile(np.arange(8, dtype=np.float32), (128, 1))
    tri = np.where(np.arange(128)[None, :] <= np.arange(128)[:, None],
                   np.float32(0), np.float32(-1e30))
    c["c_mask_st"] = tri.astype(np.float32)
    c["c_mask_ts"] = np.ascontiguousarray(tri.T).astype(np.float32)
    bo = np.zeros((128, 2), dtype=np.float32)
    bo[:64, 0] = 1.0
    bo[64:, 1] = 1.0
    c["c_blockones"] = bo
    c["c_blockonesT"] = np.ascontiguousarray(bo.T)
    hd = HD
    inv_freq = (1.0 / (10000.0 ** (np.arange(0, hd, 2, dtype=np.float64) / hd)))
    freqs = np.arange(T, dtype=np.float64)[:, None] * inv_freq[None, :]
    emb = np.concatenate([freqs, freqs], axis=1)
    cos = emb.T * 0 + np.cos(emb).T
    sin = np.sin(emb).T
    sign = np.where(np.arange(hd) < 32, -1.0, 1.0)
    cosP = cos[ROPE_PERM]
    sinS = (sin * sign[:, None])[ROPE_PERM]
    c["c_rope_cos"] = np.tile(cosP, (2, 1)).astype(np.float32)
    c["c_rope_sin"] = np.tile(sinS, (2, 1)).astype(np.float32)
    return c


WEIGHT_SHAPES = {
    "L0_attn_Wz": (D, D), "L0_attn_Wh": (D, D),
    "L0_attn_bz": (1, D), "L0_attn_bh": (1, D),
    "L0_ffn_w1_w": (HID, D), "L0_ffn_w1_s": (1, HID),
    "L0_ffn_w2_w": (HID, D), "L0_ffn_w2_s": (1, HID),
    "L0_ffn_h1_w": (HID, HID), "L0_ffn_h1_s": (1, HID),
    "L0_ffn_h2_w": (HID, HID), "L0_ffn_h2_s": (1, HID),
    "L0_ffn_w3_w": (D, HID), "L0_ffn_w3_s": (1, D),
    "L1_attn_Wqkv": (3 * D, D), "L1_attn_bqkv": (1, 3 * D),
    "L1_attn_Wproj": (D, D), "L1_attn_bproj": (1, D),
    "L1_attn_qn_w": (1, HD), "L1_attn_kn_w": (1, HD),
    "L1_ffn_attn_norm_w": (1, D),
    "L1_ffn_wq": (D, D), "L1_ffn_wk": (D, D),
    "L1_ffn_wv": (D, D), "L1_ffn_wo": (D, D),
    "L1_ffn_moe_norm_w": (1, D),
    "L1_ffn_gate_w": (NE, D),
    "L1_ffn_w1": (NE, HID, D), "L1_ffn_b1": (NE, HID),
    "L1_ffn_w2": (NE, D, HID), "L1_ffn_b2": (1, NE * D),
}

SLOT_WEIGHTS = {
    0: ["L0_attn_Wz", "L0_attn_Wh", "L0_attn_bz", "L0_attn_bh"],
    1: ["L0_ffn_w1_w", "L0_ffn_w1_s", "L0_ffn_w2_w", "L0_ffn_w2_s",
        "L0_ffn_h1_w", "L0_ffn_h1_s", "L0_ffn_h2_w", "L0_ffn_h2_s",
        "L0_ffn_w3_w", "L0_ffn_w3_s"],
    2: ["L1_attn_Wqkv", "L1_attn_bqkv", "L1_attn_Wproj", "L1_attn_bproj",
        "L1_attn_qn_w", "L1_attn_kn_w"],
    3: ["L1_ffn_attn_norm_w", "L1_ffn_wq", "L1_ffn_wk", "L1_ffn_wv",
        "L1_ffn_wo", "L1_ffn_moe_norm_w", "L1_ffn_gate_w",
        "L1_ffn_w1", "L1_ffn_b1", "L1_ffn_w2", "L1_ffn_b2"],
}

CONST_NAMES = {
    0: ["c_ident_f", "c_ident_b", "c_ones_col", "c_ones_row", "c_eps"],
    1: ["c_ident_f", "c_ident_b", "c_ones_col", "c_ones_row", "c_eps"],
    2: ["c_ident_f", "c_ident_b", "c_ones_col", "c_ones_row", "c_neg8ident",
        "c_blockones", "c_blockonesT", "c_eps", "c_mask_st", "c_mask_ts"],
    3: ["c_ident_f", "c_ident_b", "c_ones_col", "c_ones_row", "c_neg8ident",
        "c_blockones", "c_blockonesT", "c_iota8", "c_mask_st", "c_mask_ts",
        "c_rope_cos", "c_rope_sin", "c_eps", "c_ones_row512"],
}
CONST_DTYPES = {"c_ident_b": BF16, "c_blockones": BF16}
CONST_NOSBUF = {"c_rope_cos", "c_rope_sin"}
CONST_SHAPES = {
    "c_eps": (128, 2),
    "c_ident_f": (128, 128), "c_ident_b": (128, 128), "c_neg8ident": (128, 128),
    "c_ones_col": (128, 1), "c_ones_row": (1, 128), "c_iota8": (128, 8),
    "c_ones_row512": (1, 512),
    "c_mask_st": (128, 128), "c_mask_ts": (128, 128),
    "c_blockones": (128, 2), "c_blockonesT": (2, 128),
    "c_rope_cos": (128, T), "c_rope_sin": (128, T),
}


class MK:
    def __init__(self, nc, ctx, slots):
        self.nc = nc
        self.ctx = ctx
        self.slots = slots
        self.P = Prog(nc, ctx)
        self.io = {}
        self.sb = {}

    def dram_in(self, name, shape):
        ap = self.nc.dram_tensor(name, list(shape), F32, kind="ExternalInput").ap()
        self.io[name] = ap
        return ap

    def dram_out(self, name, shape):
        ap = self.nc.dram_tensor(name, list(shape), F32, kind="ExternalOutput").ap()
        self.io[name] = ap
        return ap

    def dram_scratch(self, name, shape, dtype=BF16):
        ap = self.nc.dram_tensor(name, list(shape), dtype, kind="Internal").ap()
        self.io[name] = ap
        return ap

    def sbuf(self, name, shape, dtype=F32):
        h = self.ctx.enter_context(
            self.nc.sbuf_tensor("sb_" + name, list(shape), dtype))
        self.sb[name] = h
        return h

    def psum(self, name, shape, dtype=F32):
        h = self.ctx.enter_context(self.nc.psum_tensor(name, list(shape), dtype))
        return h

    def arena_alloc(self):
        return _Alloc(self.arena, AB_COLS)

    def setup(self):
        P = self.P
        self.LD = P.sem("s_ld")
        self.LS = P.sem("s_ls")
        self.MM = P.sem("s_mm")
        self.EV = P.sem("s_ev")
        self.AC = P.sem("s_ac")
        self.PO = P.sem("s_po")
        self.ST = P.sem("s_st")
        self.STG = P.sem("s_stg")
        self.TICK = P.sem("s_tick")

        self.resid = self.sbuf("resid", [128, DX * NT], BF16)
        self.arena = self.sbuf("arena", [128, AB_COLS], BF16)
        self.hbuf = self.sbuf("hbuf", [128, DX * NT], BF16)

        self.dram_in("x_in", (B, T, D))
        self.dram_out("x_out", (B, T, D))
        cns = sorted({n for s in self.slots for n in CONST_NAMES[s]})
        for nm in cns:
            self.dram_in(nm, CONST_SHAPES[nm])
            if nm not in CONST_NOSBUF:
                self.sbuf(nm, CONST_SHAPES[nm], CONST_DTYPES.get(nm, F32))
        for s in self.slots:
            for nm in SLOT_WEIGHTS[s]:
                self.dram_in(nm, WEIGHT_SHAPES[nm])
        self.sbuf("c_ones_col_b", [128, 1], BF16)
        self.sbuf("c_ones_row_b", [1, 128], BF16)

        for nm in cns:
            if nm in CONST_NOSBUF:
                continue
            if CONST_DTYPES.get(nm, F32) == F32:
                P.dma("sync", self.sb[nm][:], self.io[nm], self.LS)
            else:
                P.dma("gpsimd", self.sb[nm][:], self.io[nm], self.LD)
        P.dma("gpsimd", self.sb["c_ones_col_b"][:], self.io["c_ones_col"], self.LD)
        P.dma("gpsimd", self.sb["c_ones_row_b"][:], self.io["c_ones_row"], self.LD)

        self.pA = self.psum("pA", [128, 2048], F32)
        self.pB = self.psum("pB", [128, 1024], F32)
        self.pT = self.psum("pT", [128, 1024], BF16)
        self.pR = self.psum("pR", [128, 512], F32)
        self.sbuf("wtmp", [128, 4096], BF16)
        self.sbuf("lnsum", [128, DX * 512], BF16)
        self.sbuf("rows", [1, 1024], F32)
        self.sbuf("rows2", [2, 2048], F32)
        self.sbuf("rowb", [1, 128], BF16)
        self.sbuf("rowd", [1, 512], BF16)
        self.sbuf("fcols", [128, 512], F32)
        self.sbuf("rowbuf", [1, 1024], F32)
        self.sbuf("t1f", [128, 512], F32)
        self.sbuf("otile", [128, D], BF16)
        self.barrier("sync")

    def barrier(self, owner="vector"):
        P = self.P
        sems = [s for s in P.sems.values()
                if s is not self.TICK and s.n > 0]
        for s in sems:
            P.wait(owner, s)
        self.TICK.n += 1
        P.emit(owner, lambda e, h=self.TICK.h: e.sem_inc(h, 1))
        for e in ENGINES:
            P.wait(e, self.TICK)

    def prep_transposed(self, dst, dst_col0, src_ap, rows, cols, row0=0,
                        dst_stride=None, scale_col=None):
        P = self.P
        if dst_stride is None:
            dst_stride = rows
        wtmp = self.sb["wtmp"]
        identb = self.sb["c_ident_b"]
        nk = cols // 128
        nr = rows // 128
        P.wait("vector", self.ST)
        for r in range(nr):
            P.wait("gpsimd", self.MM)
            P.wait("gpsimd", self.EV)
            P.dma("gpsimd", wtmp[:, 0:cols],
                  src_ap[row0 + r * 128: row0 + (r + 1) * 128, :], self.LD)
            for k0 in range(0, nk, 8):
                kn = min(8, nk - k0)
                P.wait("tensor", self.LD)
                P.wait("tensor", self.EV)
                for kk in range(kn):
                    kX = k0 + kk
                    P.op("tensor", "transpose",
                         self.pT[:, kk * 128:(kk + 1) * 128],
                         wtmp[:, kX * 128:(kX + 1) * 128], identb[:],
                         inc=self.MM if kk == kn - 1 else None)
                P.wait("vector", self.MM)
                for kk in range(kn):
                    kX = k0 + kk
                    c0 = dst_col0 + kX * dst_stride + r * 128
                    if scale_col is None:
                        P.op("vector", "tensor_copy", dst[:, c0:c0 + 128],
                             self.pT[:, kk * 128:(kk + 1) * 128],
                             inc=self.EV if kk == kn - 1 else None)
                    else:
                        P.op("vector", "tensor_scalar_mul", dst[:, c0:c0 + 128],
                             self.pT[:, kk * 128:(kk + 1) * 128],
                             scale_col(kX),
                             inc=self.EV if kk == kn - 1 else None)
        P.wait("tensor", self.EV)

    def load_bias_cols(self, dst_ap, src_row_ap, n):
        P = self.P
        onesr = self.sb["c_ones_row"]
        rowbuf = self.sb["rowbuf"]
        for j0 in range(0, n, 8):
            jn = min(8, n - j0)
            P.wait("sync", self.MM)
            P.dma("sync", rowbuf[0:1, 0:jn * 128],
                  src_row_ap[0:1, j0 * 128:(j0 + jn) * 128], self.LS)
            P.wait("tensor", self.LS)
            P.wait("tensor", self.EV)
            for j in range(jn):
                P.op("tensor", "matmul", self.pR[:, j:j + 1],
                     rowbuf[0:1, j * 128:(j + 1) * 128], onesr[0:1, 0:1],
                     start=True, stop=True,
                     inc=self.MM if j == jn - 1 else None)
            P.wait("vector", self.MM)
            P.op("vector", "tensor_copy", dst_ap[:, j0:j0 + jn],
                 self.pR[:, 0:jn], inc=self.EV)

    def load_resid(self):
        P = self.P
        src = self.io["x_in"]
        xtm = self.sb["wtmp"]
        identb = self.sb["c_ident_b"]
        for b in range(B):
            for tt in range(T // 128):
                P.wait("gpsimd", self.MM)
                P.dma("gpsimd", xtm[:, 0:D],
                      src[b, tt * 128:(tt + 1) * 128, :], self.LD)
                P.wait("tensor", self.LD)
                P.wait("tensor", self.EV)
                for dXi in range(DX):
                    P.op("tensor", "transpose",
                         self.pT[:, dXi * 128:(dXi + 1) * 128],
                         xtm[:, dXi * 128:(dXi + 1) * 128], identb[:],
                         inc=self.MM if dXi == DX - 1 else None)
                P.wait("vector", self.MM)
                for dXi in range(DX):
                    c0 = dXi * NT + b * T + tt * 128
                    P.op("vector", "tensor_copy",
                         self.resid[:, c0:c0 + 128],
                         self.pT[:, dXi * 128:(dXi + 1) * 128],
                         inc=self.EV if dXi == DX - 1 else None)
        self.barrier()

    def store_out(self, src_name=None):
        P = self.P
        dst = self.io["x_out"]
        identb = self.sb["c_ident_b"]
        otile = self.sb["otile"]
        src = self.sb[src_name] if src_name else self.resid
        for b in range(B):
            for tt in range(T // 128):
                P.wait("tensor", self.STG)
                P.wait("tensor", self.EV)
                for dXi in range(DX):
                    c0 = dXi * NT + b * T + tt * 128
                    P.op("tensor", "transpose",
                         self.pT[:, dXi * 128:(dXi + 1) * 128],
                         src[:, c0:c0 + 128], identb[:],
                         inc=self.MM if dXi == DX - 1 else None)

                P.wait("vector", self.MM)
                P.wait("vector", self.STG)
                P.op("vector", "tensor_copy", otile[:, 0:D],
                     self.pT[:, 0:1024], inc=self.EV)
                P.wait("gpsimd", self.EV)
                P.dma("gpsimd", dst[b, tt * 128:(tt + 1) * 128, :],
                      otile[:, 0:D], self.STG)
        P.wait("gpsimd", self.STG)
        self.barrier("gpsimd")

    def postln_chunk(self, rcol0, slot_fn, bias_col=None, add_resid=True,
                     extra_fn=None, presummed=False):
        P = self.P
        lnsum, lnsq = self.sb["lnsum"], self.sb["wtmp"]
        onesb = self.sb["c_ones_col_b"]
        ones1f = self.sb["c_ones_row"]
        rows = self.sb["rows"]
        t1f = self.sb["t1f"]
        for dXi in range(DX if not presummed else 0):
            rc = rcol0 + dXi * NT
            d_ap = lnsum[:, dXi * 512:(dXi + 1) * 512]
            s_ap = slot_fn(dXi)
            last = (dXi == DX - 1) and extra_fn is None
            if add_resid:
                if bias_col is not None:
                    P.op("vector", "scalar_tensor_tensor", d_ap, s_ap,
                         bias_col, self.resid[:, rc:rc + 512], ALU.add,
                         ALU.add, inc=self.EV if last else None)
                else:
                    P.op("vector", "tensor_add", d_ap, s_ap,
                         self.resid[:, rc:rc + 512],
                         inc=self.EV if last else None)
            else:
                P.op("vector", "tensor_copy", d_ap, s_ap,
                     inc=self.EV if last else None)
        if extra_fn is not None:
            P.wait("vector", self.EV)
            for dXi in range(DX):
                d_ap = lnsum[:, dXi * 512:(dXi + 1) * 512]
                P.op("vector", "tensor_add", d_ap, d_ap, extra_fn(dXi),
                     inc=self.EV if dXi == DX - 1 else None)
        P.wait("scalar", self.EV)
        for dXi in range(DX):
            P.op("scalar", "activation", lnsq[:, dXi * 512:(dXi + 1) * 512],
                 lnsum[:, dXi * 512:(dXi + 1) * 512], AF.Square,
                 inc=self.AC if dXi == DX - 1 else None)
        P.wait("tensor", self.EV)
        P.wait("tensor", self.AC)
        for dXi in range(DX):
            P.op("tensor", "matmul", self.pB[0:1, 0:512], onesb[:],
                 lnsum[:, dXi * 512:(dXi + 1) * 512],
                 start=(dXi == 0), stop=(dXi == DX - 1))
        for dXi in range(DX):
            P.op("tensor", "matmul", self.pB[0:1, 512:1024], onesb[:],
                 lnsq[:, dXi * 512:(dXi + 1) * 512],
                 start=(dXi == 0), stop=(dXi == DX - 1),
                 inc=self.MM if dXi == DX - 1 else None)
        P.wait("vector", self.MM)
        P.op("vector", "tensor_scalar_mul", rows[0:1, 0:512],
             self.pB[0:1, 0:512], 1.0 / D, inc=self.EV)
        P.wait("scalar", self.EV)
        P.op("scalar", "activation", rows[0:1, 512:1024], rows[0:1, 0:512],
             AF.Square, inc=self.AC)
        P.wait("vector", self.AC)
        P.op("vector", "scalar_tensor_tensor", rows[0:1, 512:1024],
             self.pB[0:1, 512:1024], 1.0 / D, rows[0:1, 512:1024],
             ALU.mult, ALU.subtract, inc=self.EV)
        P.wait("scalar", self.EV)
        P.op("scalar", "activation", rows[0:1, 512:1024],
             rows[0:1, 512:1024], AF.Ln,
             bias=self.sb["c_eps"][0:1, 0:1], inc=self.AC)
        P.wait("scalar", self.AC, self.AC.n)
        P.op("scalar", "activation", rows[0:1, 512:1024],
             rows[0:1, 512:1024], AF.Exp, scale=-0.5, inc=self.AC)
        P.wait("tensor", self.AC)
        P.op("tensor", "matmul", self.pA[:, 0:512], ones1f[0:1, :],
             rows[0:1, 0:512], start=True, stop=True)
        P.op("tensor", "matmul", self.pA[:, 512:1024], ones1f[0:1, :],
             rows[0:1, 512:1024], start=True, stop=True, inc=self.MM)
        P.wait("vector", self.MM)
        for dXi in range(DX):
            rc = rcol0 + dXi * NT
            P.op("vector", "tensor_sub", t1f[:],
                 lnsum[:, dXi * 512:(dXi + 1) * 512], self.pA[:, 0:512],
                 inc=self.EV)
            P.wait("vector", self.EV)
            P.op("vector", "tensor_mul", self.resid[:, rc:rc + 512],
                 t1f[:], self.pA[:, 512:1024], inc=self.EV)
            P.wait("vector", self.EV)

    def slot0(self):
        P = self.P
        if True:
            al = self.arena_alloc()
            wzT = al.take(DX * D)
            whT = al.take(DX * D)
            bcol = self.sb["fcols"][:, 0:2 * DX]
            nbz = self.sb["fcols"][:, 2 * DX:3 * DX]
            SC = 512
            hb = self.hbuf
            Ab = hb[:, 0:DX * SC]
            Bb = hb[:, DX * SC:2 * DX * SC]
            XM = [hb[:, 2 * DX * SC:3 * DX * SC],
                  hb[:, 3 * DX * SC:4 * DX * SC]]
            zt = hb[:, 4 * DX * SC:4 * DX * SC + SC]
            sg = hb[:, 4 * DX * SC + SC:4 * DX * SC + 2 * SC]
            hp5 = hb[:, 4 * DX * SC + 2 * SC:4 * DX * SC + 3 * SC]
            gt = hb[:, 4 * DX * SC + 3 * SC:4 * DX * SC + 4 * SC]

            self.prep_transposed(wzT, 0, self.io["L0_attn_Wz"], D, D)
            self.prep_transposed(whT, 0, self.io["L0_attn_Wh"], D, D)
            self.load_bias_cols(bcol[:, 0:DX], self.io["L0_attn_bz"], DX)
            self.load_bias_cols(bcol[:, DX:2 * DX], self.io["L0_attn_bh"], DX)
            P.wait("vector", self.EV)
            P.op("vector", "tensor_scalar_mul", nbz[:], bcol[:, 0:DX], -1.0,
                 inc=self.EV)
            self.barrier()

            for b in range(B):
                for ci in range(T // SC):
                    c0 = ci * SC
                    par = ci % 2
                    for hXi in range(DX):
                        pk = self.pA[:, (hXi % 2) * 512:(hXi % 2) * 512 + 512]
                        ph = self.pB[:, (hXi % 2) * 512:(hXi % 2) * 512 + 512]
                        P.wait("tensor", self.EV)
                        P.wait("tensor", self.AC)
                        for (w, p) in ((wzT, pk), (whT, ph)):
                            for dXi in range(DX):
                                rhs = self.resid[:, dXi * NT + b * T + c0:
                                                 dXi * NT + b * T + c0 + SC]
                                lhsT = w[:, dXi * D + hXi * 128:
                                         dXi * D + (hXi + 1) * 128]
                                for n in range(SC // 512):
                                    P.op("tensor", "matmul",
                                         p[:, n * 512:(n + 1) * 512], lhsT,
                                         rhs[:, n * 512:(n + 1) * 512],
                                         start=(dXi == 0), stop=(dXi == DX - 1),
                                         inc=self.MM if (w is whT
                                                         and dXi == DX - 1
                                                         and n == SC // 512 - 1)
                                         else None)
                        P.wait("scalar", self.MM)
                        hcol = slice(hXi * SC, (hXi + 1) * SC)
                        P.op("scalar", "activation", Ab[:, hcol], pk[:],
                             AF.Sigmoid, bias=nbz[:, hXi:hXi + 1], scale=-1.0,
                             inc=self.AC)
                        P.op("scalar", "activation", zt[:], pk[:], AF.Sigmoid,
                             bias=bcol[:, hXi:hXi + 1], inc=self.AC)
                        P.op("scalar", "activation", sg[:], ph[:], AF.Sigmoid,
                             bias=bcol[:, DX + hXi:DX + hXi + 1], inc=self.AC)
                        P.wait("vector", self.MM)
                        P.op("vector", "tensor_scalar", hp5[:], ph[:],
                             bcol[:, DX + hXi:DX + hXi + 1], 0.5, ALU.add,
                             ALU.add, inc=self.EV)
                        P.wait("vector", self.EV)
                        P.wait("vector", self.AC)
                        P.op("vector", "tensor_max", gt[:], hp5[:], sg[:],
                             inc=self.EV)
                        P.wait("vector", self.EV)
                        P.op("vector", "tensor_mul", Bb[:, hcol], zt[:], gt[:],
                             inc=self.EV)
                        P.wait("vector", self.EV)
                        init = 0.0 if ci == 0 else \
                            XM[1 - par][:, hXi * SC + SC - 1: hXi * SC + SC]
                        P.op("vector", "tensor_tensor_scan", XM[par][:, hcol],
                             Ab[:, hcol], Bb[:, hcol], init, ALU.mult, ALU.add,
                             inc=self.EV)
                        P.wait("vector", self.EV)
                    self.barrier()
                    self.postln_chunk(
                        b * T + c0,
                        lambda dXi, par=par:
                            XM[par][:, dXi * SC: dXi * SC + 512])
                    self.barrier()


    def slot1(self):
        P = self.P
        al = self.arena_alloc()
        h1 = al.take(HX * 512)
        h2 = self.hbuf[:, 0:HX * 512]
        wb = [self.hbuf[:, HX * 512:HX * 512 + 8192],
              self.hbuf[:, HX * 512 + 8192:HX * 512 + 16384]]
        silu_s = al.take(512)
        fc = self.sb["fcols"]
        s1c, s2c = fc[:, 0:32], fc[:, 32:64]
        sh1c, sh2c = fc[:, 64:96], fc[:, 96:128]
        s3c = fc[:, 128:136]

        scr = {
            "w1": self.dram_scratch("scr_w1", (HX, 128, 1024)),
            "w2": self.dram_scratch("scr_w2", (HX, 128, 1024)),
            "h1": self.dram_scratch("scr_h1", (HX, 128, 4096)),
            "h2": self.dram_scratch("scr_h2", (HX, 128, 4096)),
            "w3": self.dram_scratch("scr_w3", (DX, 128, 4096)),
        }
        self.load_bias_cols(s1c, self.io["L0_ffn_w1_s"], 32)
        self.load_bias_cols(s2c, self.io["L0_ffn_w2_s"], 32)
        self.load_bias_cols(sh1c, self.io["L0_ffn_h1_s"], 32)
        self.load_bias_cols(sh2c, self.io["L0_ffn_h2_s"], 32)
        self.load_bias_cols(s3c, self.io["L0_ffn_w3_s"], DX)
        self.barrier()

        lastmm = {}

        def get_block(g, bufs, par, tag, src_name, m, nk, blkcols):
            wap = bufs[par][:, 0:blkcols]
            if g == 0:
                P.wait("sync", self.MM, lastmm.get((tag, par), 0))
                self.prep_transposed(wap, 0, self.io[src_name], 128,
                                     nk * 128, row0=m * 128, dst_stride=128)
                P.wait("sync", self.EV)
                P.dma("sync", scr[tag][m], wap, self.ST)
            else:
                P.wait("sync", self.MM, lastmm.get((tag, par), 0))
                P.dma("sync", wap, scr[tag][m], self.LS)
            return wap

        def stageA(g, gc):
            for m in range(HX):
                par = m % 2
                w1ap = get_block(g, [wb[0][:, 0:1024], wb[1][:, 0:1024]],
                                 par, "w1", "L0_ffn_w1_w", m, DX, 1024)
                w2ap = get_block(g, [wb[0][:, 1024:2048], wb[1][:, 1024:2048]],
                                 par, "w2", "L0_ffn_w2_w", m, DX, 1024)
                pa = self.pA[:, par * 1024: par * 1024 + 512]
                pb = self.pA[:, par * 1024 + 512: par * 1024 + 1024]
                P.wait("tensor", self.LS)
                P.wait("tensor", self.EV, lastmm.get(("psA", par), 0))
                P.wait("tensor", self.AC, lastmm.get(("psAa", par), 0))
                for k in range(DX):
                    rhs = self.resid[:, k * NT + gc: k * NT + gc + 512]
                    P.op("tensor", "matmul", pa, w1ap[:, k * 128:(k + 1) * 128],
                         rhs, start=(k == 0), stop=(k == DX - 1))
                for k in range(DX):
                    rhs = self.resid[:, k * NT + gc: k * NT + gc + 512]
                    P.op("tensor", "matmul", pb, w2ap[:, k * 128:(k + 1) * 128],
                         rhs, start=(k == 0), stop=(k == DX - 1),
                         inc=self.MM if k == DX - 1 else None)
                lastmm[("w1", par)] = self.MM.n
                lastmm[("w2", par)] = self.MM.n
                P.wait("scalar", self.MM)
                P.wait("scalar", self.EV)
                P.op("scalar", "activation", silu_s[:, 0:512], pa, AF.Silu,
                     scale=s1c[:, m:m + 1], inc=self.AC)
                lastmm[("psAa", par)] = self.AC.n
                P.wait("vector", self.AC)
                P.op("vector", "scalar_tensor_tensor",
                     h1[:, m * 512:(m + 1) * 512], pb, s2c[:, m:m + 1],
                     silu_s[:, 0:512], ALU.mult, ALU.mult, inc=self.EV)
                lastmm[("psA", par)] = self.EV.n

        def stageMM(g, gc, tag, src_name, src_buf, dst_buf, nm, nk, scol,
                    act):
            for mg in range(nm // 2):
                par = mg % 2
                if g == 0:
                    P.wait("vector", self.MM, lastmm.get((tag, par), 0))
                    for mm_ in range(2):
                        self.prep_transposed(
                            wap_sub(wb[par], mm_, nk), 0, self.io[src_name],
                            128, nk * 128, row0=(mg * 2 + mm_) * 128,
                            dst_stride=128)
                    P.wait("sync", self.EV)
                    P.dma("sync", scr[tag][mg * 2:(mg + 1) * 2].transpose(
                        [1, 0, 2]),
                        wb[par][:, 0:2 * nk * 128].rearrange(
                            "p (m c) -> p m c", m=2), self.ST)
                else:
                    P.wait("sync", self.MM, lastmm.get((tag, par), 0))
                    P.dma("sync", wb[par][:, 0:2 * nk * 128].rearrange(
                        "p (m c) -> p m c", m=2),
                        scr[tag][mg * 2:(mg + 1) * 2].transpose([1, 0, 2]),
                        self.LS)
                P.wait("tensor", self.LS)
                for mm_ in range(2):
                    m = mg * 2 + mm_
                    p2 = m % 2
                    pc = self.pB[:, p2 * 512: p2 * 512 + 512]
                    P.wait("tensor", self.AC,
                           lastmm.get(("psB" + act.name, p2), 0))
                    for k in range(nk):
                        P.op("tensor", "matmul", pc,
                             wb[par][:, mm_ * nk * 128 + k * 128:
                                     mm_ * nk * 128 + (k + 1) * 128],
                             src_buf[:, k * 512:(k + 1) * 512],
                             start=(k == 0), stop=(k == nk - 1),
                             inc=self.MM if k == nk - 1 else None)
                    P.wait("scalar", self.MM)
                    P.op("scalar", "activation",
                         dst_buf[:, m * 512:(m + 1) * 512],
                         pc, act, scale=scol[:, m:m + 1], inc=self.AC)
                    lastmm[("psB" + act.name, p2)] = self.AC.n
                lastmm[(tag, par)] = self.MM.n

        def wap_sub(wbuf, mm_, nk):
            return wbuf[:, mm_ * nk * 128:(mm_ + 1) * nk * 128]

        for g in range(NT // 512):
            gc = g * 512
            stageA(g, gc)
            self.barrier()
            stageMM(g, gc, "h1", "L0_ffn_h1_w", h1, h2, HX, HX, sh1c, AF.Gelu)
            self.barrier()
            stageMM(g, gc, "h2", "L0_ffn_h2_w", h2, h1, HX, HX, sh2c, AF.Gelu)
            self.barrier()
            for m in range(DX):
                par = m % 2
                wap = get_block(g, wb, par, "w3", "L0_ffn_w3_w", m, HX, 4096)
                pc = self.pB[:, par * 512: par * 512 + 512]
                P.wait("tensor", self.LS)
                P.wait("tensor", self.EV, lastmm.get(("psD", par), 0))
                for k in range(HX):
                    P.op("tensor", "matmul", pc, wap[:, k * 128:(k + 1) * 128],
                         h1[:, k * 512:(k + 1) * 512],
                         start=(k == 0), stop=(k == HX - 1),
                         inc=self.MM if k == HX - 1 else None)
                lastmm[("w3", par)] = self.MM.n
                P.wait("vector", self.MM)
                P.op("vector", "scalar_tensor_tensor",
                     self.sb["lnsum"][:, m * 512:(m + 1) * 512], pc,
                     s3c[:, m:m + 1], self.resid[:, m * NT + gc:
                                                 m * NT + gc + 512],
                     ALU.mult, ALU.add, inc=self.EV)
                lastmm[("psD", par)] = self.EV.n
            self.barrier()
            self.postln_chunk(gc, None, presummed=True)
            self.barrier()


    def col64_to_128(self, dst_col, src_row_ap, n64=64):
        P = self.P
        rowbuf = self.sb["rowbuf"]
        onesr = self.sb["c_ones_row"]
        P.wait("sync", self.MM)
        P.dma("sync", rowbuf[0:1, 0:n64], src_row_ap[0:1, 0:n64], self.LS)
        P.wait("tensor", self.LS)
        P.wait("tensor", self.EV)
        P.op("tensor", "matmul", self.pR[0:n64, 0:1], rowbuf[0:1, 0:n64],
             onesr[0:1, 0:1], start=True, stop=True, inc=self.MM)
        P.wait("vector", self.MM)
        P.op("vector", "tensor_copy", dst_col[0:64, :], self.pR[0:64, 0:1],
             inc=self.EV)
        P.wait("vector", self.EV)
        P.op("vector", "tensor_copy", dst_col[64:128, :], self.pR[0:64, 0:1],
             inc=self.EV)

    def attn_heads(self, Qb, Kb, Vb2, causal, mrow, out_fn, post_fn=None,
                   bound_cols=None):
        P = self.P
        fc = self.sb["fcols"]
        m_all = fc[:, 0:16]
        mh = fc[:, 16:20]
        t1f = self.sb["t1f"]
        ones1f = self.sb["c_ones_row"]
        n8i = self.sb["c_neg8ident"]
        mask_st = self.sb["c_mask_st"]
        mask_ts = self.sb["c_mask_ts"]
        lu = self._attn_lu = {}

        def wl(eng, key):
            v = lu.get(key)
            if v is None:
                return
            if isinstance(v, list):
                for sem, val in v:
                    P.wait(eng, sem, val)
            else:
                P.wait(eng, v[0], v[1])

        lu[("pa", 0)] = (self.EV, self.EV.n)
        lu[("pa", 1)] = (self.EV, self.EV.n)
        for tg_ in ("s", 0, 1):
            lu[("pst", tg_, 0)] = [(self.AC, self.AC.n), (self.EV, self.EV.n)]
            lu[("pst", tg_, 1)] = [(self.AC, self.AC.n), (self.EV, self.EV.n)]
        lu["mrow_free"] = (self.MM, self.MM.n)

        def passA_items(h):
            hb = 64 * h
            items = []
            for i in range(16):
                def item(i=i, hb=hb, h=h):
                    ncols = (i + 1) * 128 if causal else 2048
                    sl = i % 2
                    base = sl * 1024
                    msl = mh[:, sl * 2:sl * 2 + 2]
                    nhalf = min(2, (ncols + 1023) // 1024)
                    wl("vector", ("mh", sl))
                    for hf in range(nhalf):
                        w = min(1024, ncols - hf * 1024)
                        wl("tensor", ("pa", sl))
                        nch = (w + 511) // 512
                        for cc in range(nch):
                            cw = min(512, w - cc * 512)
                            P.op("tensor", "matmul",
                                 self.pA[:, base + cc * 512:
                                         base + cc * 512 + cw],
                                 Qb[hb:hb + 64, i * 128:(i + 1) * 128],
                                 Kb[hb:hb + 64, hf * 1024 + cc * 512:
                                    hf * 1024 + cc * 512 + cw],
                                 start=True, stop=True,
                                 inc=self.MM if cc == nch - 1 else None)
                        mmc = self.MM.n
                        P.wait("vector", self.MM, mmc)
                        if causal and hf * 1024 <= i * 128 < hf * 1024 + w:
                            off = i * 128 - hf * 1024
                            P.op("vector", "tensor_add",
                                 self.pA[:, base + off:base + off + 128],
                                 self.pA[:, base + off:base + off + 128],
                                 mask_st[:], inc=self.EV)
                            P.wait("vector", self.EV, self.EV.n)
                        P.op("vector", "reduce_max", msl[:, hf:hf + 1],
                             self.pA[:, base:base + w], axis=AX.X,
                             negate=True, inc=self.EV)
                        lu[("pa", sl)] = (self.EV, self.EV.n)
                    P.wait("vector", self.EV, self.EV.n)
                    if nhalf == 2:
                        P.op("vector", "tensor_tensor", msl[:, 0:1],
                             msl[:, 0:1], msl[:, 1:2], ALU.min, inc=self.EV)
                        P.wait("vector", self.EV, self.EV.n)
                    P.op("vector", "tensor_scalar_mul", m_all[:, i:i + 1],
                         msl[:, 0:1], 0.125, inc=self.EV)
                    lu[("mh", sl)] = (self.EV, self.EV.n)
                    lu[("mcol", h, i)] = (self.EV, self.EV.n)
                items.append(item)
            return items

        def emit_mrows(h):
            wl("tensor", ("pa", 0))
            for i in range(16):
                wl("tensor", ("mcol", h, i))
            wl("tensor", "mrow_free")
            for i in range(16):
                P.op("tensor", "matmul",
                     self.pA[0:1, (i % 4) * 128:(i % 4 + 1) * 128],
                     m_all[:, i:i + 1], n8i[:], start=True, stop=True,
                     inc=self.MM if i % 4 == 3 else None)
                if i % 4 == 3:
                    mmc = self.MM.n
                    P.wait("vector", self.MM, mmc)
                    P.op("vector", "tensor_copy",
                         mrow[0:1, (i - 3) * 128:(i + 1) * 128],
                         self.pA[0:1, 0:512], inc=self.EV)
                    if i < 15:
                        P.wait("tensor", self.EV)
            lu[("pa", 0)] = (self.EV, self.EV.n)
            lu["mrow_ready"] = (self.EV, self.EV.n)

        def passB_items(h):
            hb = 64 * h
            res = (self._attn_res or {}).get(h) if getattr(
                self, "_attn_res", None) else None
            if res is None:
                res = dict(pst=(self.pB[:, 0:512], self.pB[:, 512:1024]),
                           av=self.pR[0:65, 0:512],
                           ptb=(self._attn_ptb[:, 0:512],
                                self._attn_ptb[:, 512:1024]),
                           rep=self.pB[:, 512:1024], repk=("pst", "s", 1),
                           avs=t1f[0:65, :], tag="s")
            else:
                res = dict(res)
                res["tag"] = h
            tg = res["tag"]
            items = []
            for c in range(4):
                jstop = min(15, (c * 512 + 511) // 128) if causal else 15
                jlist = list(range(jstop + 1))
                for j in jlist:
                    def item(j=j, c=c, hb=hb, h=h, res=res, tg=tg,
                             first=(j == 0), last=(j == jlist[-1])):
                        pj = j % 2
                        bank = res["pst"][pj]
                        sc0 = c * 512
                        wl("tensor", ("pst", tg, pj))
                        if bound_cols is None:
                            wl("tensor", "mrow_ready")
                            P.op("tensor", "matmul", bank,
                                 Kb[hb:hb + 64, j * 128:(j + 1) * 128],
                                 Qb[hb:hb + 64, sc0:sc0 + 512],
                                 start=True, stop=False)
                            P.op("tensor", "matmul", bank,
                                 self.sb["c_ones_row_b"][0:1, :],
                                 mrow[0:1, sc0:sc0 + 512],
                                 start=False, stop=True, inc=self.MM)
                            lu["mrow_free"] = (self.MM, self.MM.n)
                        else:
                            P.op("tensor", "matmul", bank,
                                 Kb[hb:hb + 64, j * 128:(j + 1) * 128],
                                 Qb[hb:hb + 64, sc0:sc0 + 512],
                                 start=True, stop=True, inc=self.MM)
                        mmc = self.MM.n
                        nev = None
                        if causal and sc0 <= j * 128 < sc0 + 512:
                            off = j * 128 - sc0
                            P.wait("vector", self.MM, mmc)
                            if off > 0:
                                P.op("vector", "tensor_scalar_add",
                                     bank[:, 0:off], bank[:, 0:off],
                                     -1e30, inc=self.EV)
                            P.op("vector", "tensor_add",
                                 bank[:, off:off + 128],
                                 bank[:, off:off + 128], mask_ts[:],
                                 inc=self.EV)
                            nev = self.EV.n
                        P.wait("scalar", self.MM, mmc)
                        if nev:
                            P.wait("scalar", self.EV, nev)
                        wl("scalar", ("ptb", tg, pj))
                        ptb_ap = res["ptb"][pj]
                        if bound_cols is None:
                            P.op("scalar", "activation", ptb_ap, bank,
                                 AF.Exp, scale=0.125, inc=self.AC)
                        else:
                            P.op("scalar", "activation", ptb_ap, bank,
                                 AF.Exp, scale=0.125, bias=bound_cols[h],
                                 inc=self.AC)
                        lu[("pst", tg, pj)] = (self.AC, self.AC.n)
                        acc = self.AC.n

                        def av_part(acc=acc, ptb_ap=ptb_ap, j=j, c=c, h=h,
                                    res=res, tg=tg, first=first, last=last):
                            P.wait("tensor", self.AC, acc)
                            if first:
                                wl("tensor", ("prav", tg))
                            P.op("tensor", "matmul", res["av"],
                                 Vb2[:, j * 130 + h * 65:
                                     j * 130 + h * 65 + 65],
                                 ptb_ap, start=first, stop=last,
                                 inc=self.MM if last else None)
                            lu[("ptb", tg, pj)] = (self.MM, self.MM.n)
                            if last:
                                self._finish_chunk(h, c, res, tg)
                        return av_part
                    items.append(item)
            return items

        def finish_chunk(h, c, res, tg):
            avs = res["avs"]
            mmc = self.MM.n
            P.wait("vector", self.MM, mmc)
            wl("vector", ("avsfree", tg))
            P.op("vector", "tensor_copy", avs, res["av"], inc=self.EV)
            lu[("prav", tg)] = (self.EV, self.EV.n)
            P.wait("scalar", self.EV, self.EV.n)
            wl("scalar", "rowd")
            P.op("scalar", "activation", avs[64:65, :], avs[64:65, :],
                 AF.Ln, inc=self.AC)
            P.wait("scalar", self.AC, self.AC.n)
            P.op("scalar", "activation", self.sb["rowd"][0:1, 0:512],
                 avs[64:65, :], AF.Exp, scale=-1.0, inc=self.AC)
            rc = self.AC.n
            P.wait("tensor", self.AC, rc)
            if res["repk"] is not None:
                wl("tensor", res["repk"])
            else:
                wl("tensor", ("repfree", tg))
            P.op("tensor", "matmul", res["rep"],
                 self.sb["c_ones_row_b"][0:1, :],
                 self.sb["rowd"][0:1, 0:512], start=True, stop=True,
                 inc=self.MM)
            lu["rowd"] = (self.MM, self.MM.n)
            repc = self.MM.n
            P.wait("vector", self.MM, repc)
            dst = out_fn(h, c)
            P.op("vector", "tensor_mul", dst, avs[0:64, :],
                 res["rep"][0:64, :], inc=self.EV)
            lu[("pst", tg, 1)] = (self.EV, self.EV.n)
            lu[("repfree", tg)] = (self.EV, self.EV.n)
            lu[("avsfree", tg)] = (self.EV, self.EV.n)
            if post_fn is not None:
                post_fn(h, c)
        self._finish_chunk = finish_chunk

        if bound_cols is not None:
            res_h = {
                0: dict(pst=(self.pB[:, 0:512], self.pB[:, 512:1024]),
                        av=self.pR[0:65, 0:512],
                        ptb=(self._attn_ptb[:, 0:512],
                             self._attn_ptb[:, 512:1024]),
                        rep=self.pB[:, 512:1024], repk=("pst", 0, 1),
                        avs=t1f[0:65, :]),
                1: dict(pst=(self.pA[:, 0:512], self.pA[:, 512:1024]),
                        av=self.pA[0:65, 1024:1536],
                        ptb=(self._attn_ptb[:, 1024:1536],
                             self._attn_ptb[:, 1536:2048]),
                        rep=self.pA[:, 1536:2048], repk=None,
                        avs=self.sb["wtmp"][0:65, 0:512]),
            }
            self._attn_res = res_h
            b0 = passB_items(0)
            b1 = passB_items(1)
            n = max(len(b0), len(b1))
            for i in range(n):
                avs_ = []
                if i < len(b0):
                    avs_.append(b0[i]())
                if i < len(b1):
                    avs_.append(b1[i]())
                for a_ in avs_:
                    a_()
            self._attn_res = None
        else:
            for it in passA_items(0):
                it()
            emit_mrows(0)
            b0 = passB_items(0)
            a1 = passA_items(1)
            k = 0
            prev_av = None
            for i, it in enumerate(b0):
                av_ = it()
                if i % 2 == 1 and k < len(a1):
                    a1[k]()
                    k += 1
                if prev_av is not None:
                    prev_av()
                prev_av = av_
            if prev_av is not None:
                prev_av()
            while k < len(a1):
                a1[k]()
                k += 1
            emit_mrows(1)
            prev_av = None
            for it in passB_items(1):
                av_ = it()
                if prev_av is not None:
                    prev_av()
                prev_av = av_
            if prev_av is not None:
                prev_av()

    def slot2(self):
        P = self.P
        al = self.arena_alloc()
        wqb = [al.take(3072), al.take(3072)]
        Qb = al.take(2048)
        Kb = al.take(2048)
        Vb2 = al.take(16 * 130)
        sqs = al.take(2048)
        ptb = al.take(1024)
        avT = self.hbuf[:, 0:DX * 2048]
        wpT = self.hbuf[:, DX * 2048:DX * 2048 + 8192]
        fc = self.sb["fcols"]
        m_all = fc[:, 0:16]
        den_all = fc[:, 16:32]
        g_all = fc[:, 32:48]
        nscal = fc[:, 48:49]
        bq_cols = fc[:, 50:58]
        bk_cols = fc[:, 58:66]
        bp_cols = fc[:, 66:74]
        qn_col = fc[:, 74:75]
        kn_col = fc[:, 75:76]
        rows2 = self.sb["rows2"]
        rowb = self.sb["rowb"]
        blockones = self.sb["c_blockones"]
        blockonesT = self.sb["c_blockonesT"]
        ones1f = self.sb["c_ones_row"]
        n8i = self.sb["c_neg8ident"]
        eps6 = self.sb["c_eps"][0:1, 1:2]
        eps6r = self.sb["c_eps"][0:2, 1:2]

        self.load_bias_cols(bq_cols, self.io["L1_attn_bqkv"][:, 0:1024], 8)
        self.load_bias_cols(bk_cols, self.io["L1_attn_bqkv"][:, 1024:2048], 8)
        self.load_bias_cols(bp_cols, self.io["L1_attn_bproj"], 8)
        self.col64_to_128(qn_col, self.io["L1_attn_qn_w"])
        self.col64_to_128(kn_col, self.io["L1_attn_kn_w"])
        self.prep_transposed(wpT, 0, self.io["L1_attn_Wproj"], D, D)
        P.wait("vector", self.LD)
        for tt in range(16):
            for hh in range(2):
                P.op("vector", "tensor_copy",
                     Vb2[:, tt * 130 + hh * 65 + 64:tt * 130 + hh * 65 + 65],
                     self.sb["c_ones_col_b"][:],
                     inc=self.EV if (tt == 15 and hh == 1) else None)
        self.barrier()

        lastu = {}
        for b in range(B):
            bc = b * T
            for p in range(NH // 2):
                par = p % 2
                wq = wqb[par][:, 0:1024]
                wk = wqb[par][:, 1024:2048]
                wv = wqb[par][:, 2048:3072]
                if b == 0:
                    P.wait("vector", self.MM, lastu.get(("wq", par), 0))
                    self.prep_transposed(wq, 0, self.io["L1_attn_Wqkv"],
                                         128, D, row0=p * 128, dst_stride=128)
                    self.prep_transposed(wk, 0, self.io["L1_attn_Wqkv"],
                                         128, D, row0=1024 + p * 128,
                                         dst_stride=128)
                    self.prep_transposed(wv, 0, self.io["L1_attn_Wqkv"],
                                         128, D, row0=2048 + p * 128,
                                         dst_stride=128)
                    if p == 0:
                        scr_qkv = self.dram_scratch("scr_qkv",
                                                    (NH // 2, 128, 3072))
                        self.scr_qkv = scr_qkv
                    P.wait("sync", self.EV)
                    P.dma("sync", self.scr_qkv[p], wqb[par][:, 0:3072],
                          self.ST)
                else:
                    P.wait("sync", self.MM, lastu.get(("wq", par), 0))
                    P.dma("sync", wqb[par][:, 0:3072], self.scr_qkv[p],
                          self.LS)
                for (wtile, dstb, bcols) in ((wq, Qb, bq_cols),
                                             (wk, Kb, bk_cols)):
                    P.wait("tensor", self.LS)
                    P.wait("tensor", self.ST)
                    P.wait("tensor", self.EV)
                    for k in range(DX):
                        for n in range(4):
                            P.op("tensor", "matmul",
                                 self.pA[:, n * 512:(n + 1) * 512],
                                 wtile[:, k * 128:(k + 1) * 128],
                                 self.resid[:, k * NT + bc + n * 512:
                                            k * NT + bc + n * 512 + 512],
                                 start=(k == 0), stop=(k == DX - 1),
                                 inc=self.MM if (k == DX - 1 and n == 3)
                                 else None)
                    lastu[("wq", par)] = self.MM.n
                    P.wait("vector", self.MM)
                    P.op("vector", "tensor_scalar_add", dstb[:, 0:2048],
                         self.pA[:, 0:2048], bcols[:, p:p + 1], inc=self.EV)
                P.wait("gpsimd", self.MM, lastu.get("rowb", 0))
                P.dma("gpsimd", rowb[0:1, 0:128],
                      self.io["L1_attn_bqkv"][:, 2048 + p * 128:
                                              2048 + (p + 1) * 128], self.LD)
                P.wait("tensor", self.EV)
                P.wait("tensor", self.LD)
                for tt in range(16):
                    pv = self.pB[:, (tt % 8) * 128:(tt % 8 + 1) * 128]
                    if tt % 8 == 0:
                        P.wait("tensor", self.EV, lastu.get("pv", 0))
                    for k in range(DX):
                        P.op("tensor", "matmul", pv,
                             self.resid[:, k * NT + bc + tt * 128:
                                        k * NT + bc + (tt + 1) * 128],
                             wv[:, k * 128:(k + 1) * 128],
                             start=(k == 0), stop=False)
                    P.op("tensor", "matmul", pv,
                         self.sb["c_ones_row_b"][0:1, :],
                         rowb[0:1, 0:128],
                         start=False, stop=True,
                         inc=self.MM if tt % 8 == 7 else None)
                    if tt == 15:
                        lastu["rowb"] = self.MM.n
                    if tt % 8 == 7:
                        P.wait("vector", self.MM)
                        t0c = tt - 7
                        P.op("vector", "tensor_copy",
                             Vb2[:, t0c * 130:(tt + 1) * 130].rearrange(
                                 "p (t c) -> p t c", t=8)[:, :, 0:64],
                             self.pB[:, 0:1024].rearrange(
                                 "p (t c) -> p t c", t=8)[:, :, 0:64],
                             inc=self.EV)
                        P.op("vector", "tensor_copy",
                             Vb2[:, t0c * 130:(tt + 1) * 130].rearrange(
                                 "p (t c) -> p t c", t=8)[:, :, 65:129],
                             self.pB[:, 0:1024].rearrange(
                                 "p (t c) -> p t c", t=8)[:, :, 64:128],
                             inc=self.EV)
                        lastu["pv"] = self.EV.n
                lastu[("wq", par)] = self.MM.n
                for (buf, ncol) in ((Qb, qn_col), (Kb, kn_col)):
                    P.wait("scalar", self.EV)
                    P.op("scalar", "activation", sqs[:, 0:2048], buf[:, 0:2048],
                         AF.Square, inc=self.AC)
                    P.wait("tensor", self.AC)
                    P.wait("tensor", self.EV)
                    for n in range(4):
                        P.op("tensor", "matmul",
                             self.pA[0:2, n * 512:(n + 1) * 512], blockones[:],
                             sqs[:, n * 512:(n + 1) * 512],
                             start=True, stop=True,
                             inc=self.MM if n == 3 else None)
                    P.wait("vector", self.MM)
                    P.op("vector", "tensor_scalar_mul", rows2[0:2, 0:2048],
                         self.pA[0:2, 0:2048], 1.0 / HD, inc=self.EV)
                    P.wait("scalar", self.EV)
                    P.op("scalar", "activation", rows2[0:2, 0:2048],
                         rows2[0:2, 0:2048], AF.Ln, bias=eps6r, inc=self.AC)
                    P.wait("scalar", self.AC, self.AC.n)
                    P.op("scalar", "activation", rows2[0:2, 0:2048],
                         rows2[0:2, 0:2048], AF.Exp, scale=-0.5, inc=self.AC)
                    P.wait("tensor", self.AC)
                    for n in range(4):
                        P.op("tensor", "matmul",
                             self.pA[:, n * 512:(n + 1) * 512],
                             blockonesT[0:2, :], rows2[0:2, n * 512:
                                                       (n + 1) * 512],
                             start=True, stop=True,
                             inc=self.MM if n == 3 else None)
                    P.wait("vector", self.MM)
                    P.op("vector", "scalar_tensor_tensor", buf[:, 0:2048],
                         buf[:, 0:2048], ncol, self.pA[:, 0:2048],
                         ALU.mult, ALU.mult, inc=self.EV)
                self.barrier()
                self._attn_ptb = ptb
                self.attn_heads(
                    Qb, Kb, Vb2, causal=False, mrow=sqs[0:1, 0:2048],
                    out_fn=lambda h, c, p=p: avT[64 * h:64 * h + 64,
                                                 p * 2048 + c * 512:
                                                 p * 2048 + c * 512 + 512])
                self.barrier()
            for n in range(4):
                for m in range(DX):
                    par = m % 2
                    pc = self.pB[:, par * 512:par * 512 + 512]
                    P.wait("tensor", self.EV, lastu.get(("pprj", par), 0))
                    for k in range(DX):
                        P.op("tensor", "matmul", pc,
                             wpT[:, k * 1024 + m * 128:k * 1024 + m * 128
                                 + 128],
                             avT[:, k * 2048 + n * 512:k * 2048 + n * 512
                                 + 512],
                             start=(k == 0), stop=(k == DX - 1),
                             inc=self.MM if k == DX - 1 else None)
                    P.wait("vector", self.MM)
                    P.op("vector", "scalar_tensor_tensor",
                         self.sb["lnsum"][:, m * 512:(m + 1) * 512], pc,
                         bp_cols[:, m:m + 1],
                         self.resid[:, m * NT + bc + n * 512:
                                    m * NT + bc + n * 512 + 512],
                         ALU.add, ALU.add, inc=self.EV)
                    lastu[("pprj", par)] = self.EV.n
                self.barrier()
                self.postln_chunk(bc + n * 512, None, presummed=True)
                self.barrier()


    def slot3(self):
        P = self.P
        al = self.arena_alloc()
        wqb = [al.take(3072)]
        Qb = al.take(2048)
        Kb = al.take(2048)
        Vb2 = al.take(16 * 130)
        shf = al.take(2048)
        sqs = shf
        ptb = al.take(2048)
        ropc = al.take(2048)
        rops = al.take(2048)
        avstg = ptb
        woT = self.arena[:, 0:8192]
        projstg = self.arena[:, 8192:12288]
        av_scr = self.dram_scratch("scr_av", (DX, 128, 2048))
        h_scr = self.dram_scratch("scr_h", (DX, 128, NT))
        fc = self.sb["fcols"]
        m_all = fc[:, 0:16]
        den_all = fc[:, 16:32]
        g_all = fc[:, 32:48]
        nscal = fc[:, 48:49]
        wn1c = fc[:, 49:57]
        wn2c = fc[:, 57:65]
        rstd_tt = fc[:, 65:97]
        rows2 = self.sb["rows2"]
        rows = self.sb["rows"]
        t1f = self.sb["t1f"]
        ones1f = self.sb["c_ones_row"]
        ones512 = self.sb["c_ones_row512"]
        onescb = self.sb["c_ones_col_b"]
        n8i = self.sb["c_neg8ident"]
        identf = self.sb["c_ident_f"]
        mask_st = self.sb["c_mask_st"]
        mask_ts = self.sb["c_mask_ts"]
        iota8 = self.sb["c_iota8"]
        eps6 = self.sb["c_eps"][0:1, 1:2]
        rowbuf = self.sb["rowbuf"]

        P.dma("gpsimd", ropc, self.io["c_rope_cos"], self.LD)
        P.dma("gpsimd", rops, self.io["c_rope_sin"], self.LD)
        self.load_bias_cols(wn1c, self.io["L1_ffn_attn_norm_w"], DX)
        self.load_bias_cols(wn2c, self.io["L1_ffn_moe_norm_w"], DX)
        scr_a = self.dram_scratch("scr_l1a", (NH // 2, 128, 3072))
        self.barrier()
        lastu = {}

        for b in range(B):
            bc = b * T
            for n in range(4):
                P.wait("scalar", self.EV)
                P.wait("scalar", self.MM)
                for dXi in range(DX):
                    P.op("scalar", "activation", sqs[:, 0:512],
                         self.resid[:, dXi * NT + bc + n * 512:
                                    dXi * NT + bc + n * 512 + 512],
                         AF.Square, inc=self.AC)
                    P.wait("tensor", self.AC)
                    P.op("tensor", "matmul", self.pR[0:1, 0:512], onescb[:],
                         sqs[:, 0:512], start=(dXi == 0), stop=(dXi == DX - 1),
                         inc=self.MM)
                    if dXi < DX - 1:
                        P.wait("scalar", self.MM)
                P.wait("vector", self.MM)
                P.op("vector", "tensor_scalar_mul", rows[0:1, 0:512],
                     self.pR[0:1, 0:512], 1.0 / D, inc=self.EV)
                P.wait("scalar", self.EV)
                P.op("scalar", "activation", rows[0:1, 0:512],
                     rows[0:1, 0:512], AF.Sqrt, bias=eps6, inc=self.AC)
                P.wait("vector", self.AC)
                P.op("vector", "reciprocal", rows[0:1, 0:512],
                     rows[0:1, 0:512], inc=self.EV)
                P.wait("tensor", self.EV)
                for i in range(4):
                    P.op("tensor", "matmul", self.pR[:, 4 + i:5 + i],
                         rows[0:1, i * 128:(i + 1) * 128], ones1f[0:1, 0:1],
                         start=True, stop=True,
                         inc=self.MM if i == 3 else None)
                P.wait("vector", self.MM)
                P.op("vector", "tensor_copy",
                     rstd_tt[:, b * 16 + n * 4:b * 16 + n * 4 + 4],
                     self.pR[:, 4:8], inc=self.EV)
                P.wait("tensor", self.EV)
            P.wait("vector", self.LD)
            for tt in range(16):
                for hh in range(2):
                    P.op("vector", "tensor_copy",
                         Vb2[:, tt * 130 + hh * 65 + 64:
                             tt * 130 + hh * 65 + 65],
                         self.sb["c_ones_col_b"][:],
                         inc=self.EV if (tt == 15 and hh == 1) else None)
            self.barrier()

            for p in range(NH // 2):
                par = 0
                wq = wqb[par][:, 0:1024]
                wk = wqb[par][:, 1024:2048]
                wv = wqb[par][:, 2048:3072]
                if b == 0:
                    P.wait("vector", self.MM, lastu.get(("w3a", par), 0))
                    for (wname, dst) in (("L1_ffn_wq", wq), ("L1_ffn_wk", wk),
                                         ("L1_ffn_wv", wv)):
                        if wname == "L1_ffn_wv":
                            self.prep_transposed(dst, 0, self.io[wname],
                                                 128, D, row0=p * 128,
                                                 dst_stride=128,
                                                 scale_col=lambda kX:
                                                 wn1c[:, kX:kX + 1])
                        else:
                            for blk in range(8):
                                hh = blk // 4
                                sub = ROPE_PERM[(blk % 4) * 16:
                                                (blk % 4) * 16 + 16]
                                r0 = int(sub[0])
                                self.prep_16rows(dst, blk * 16,
                                                 self.io[wname],
                                                 p * 128 + hh * 64 + r0,
                                                 wn1c)
                    P.wait("sync", self.EV)
                    P.dma("sync", scr_a[p], wqb[par][:, 0:3072], self.ST)
                else:
                    P.wait("sync", self.MM, lastu.get(("w3a", par), 0))
                    P.dma("sync", wqb[par][:, 0:3072], scr_a[p], self.LS)
                for (wtile, dstb) in ((wq, Qb), (wk, Kb)):
                    P.wait("tensor", self.LS)
                    P.wait("tensor", self.ST)
                    P.wait("tensor", self.EV)
                    for k in range(DX):
                        for n in range(4):
                            P.op("tensor", "matmul",
                                 self.pA[:, n * 512:(n + 1) * 512],
                                 wtile[:, k * 128:(k + 1) * 128],
                                 self.resid[:, k * NT + bc + n * 512:
                                            k * NT + bc + n * 512 + 512],
                                 start=(k == 0), stop=(k == DX - 1),
                                 inc=self.MM if (k == DX - 1 and n == 3)
                                 else None)
                    P.wait("vector", self.MM)
                    P.op("vector", "tensor_copy", dstb[:, 0:2048],
                         self.pA[:, 0:2048], inc=self.EV)
                for n in range(4):
                    P.wait("tensor", self.EV)
                    for i in range(4):
                        P.op("tensor", "matmul",
                             self.pR[0:1, i * 128:(i + 1) * 128],
                             rstd_tt[:, b * 16 + n * 4 + i:
                                     b * 16 + n * 4 + i + 1], identf[:],
                             start=True, stop=True,
                             inc=self.MM if i == 3 else None)
                    P.wait("vector", self.MM)
                    P.op("vector", "tensor_copy", rows[0:1, 512:1024],
                         self.pR[0:1, 0:512], inc=self.EV)
                    P.wait("tensor", self.EV)
                    P.op("tensor", "matmul",
                         self.pA[:, n * 512:(n + 1) * 512], ones1f[0:1, :],
                         rows[0:1, 512:1024],
                         start=True, stop=True,
                         inc=self.MM if n == 3 else None)
                P.wait("vector", self.MM)
                for dstb in (Qb, Kb):
                    P.op("vector", "tensor_mul", dstb[:, 0:2048],
                         dstb[:, 0:2048], self.pA[:, 0:2048], inc=self.EV)
                for dstb in (Qb, Kb):
                    P.wait("vector", self.EV)
                    P.op("vector", "stream_shuffle", shf[:, 0:2048],
                         dstb[:, 0:2048], ROPE_SHUF_MASK, inc=self.EV)
                    P.wait("vector", self.EV)
                    P.op("vector", "tensor_mul", shf[:, 0:2048], shf[:, 0:2048],
                         rops, inc=self.EV)
                    P.wait("vector", self.EV)
                    P.op("vector", "tensor_mul", dstb[:, 0:2048],
                         dstb[:, 0:2048], ropc, inc=self.EV)
                    P.wait("vector", self.EV)
                    P.op("vector", "tensor_add", dstb[:, 0:2048],
                         dstb[:, 0:2048], shf[:, 0:2048], inc=self.EV)
                P.wait("tensor", self.EV)
                for tt in range(16):
                    pv = self.pB[:, (tt % 8) * 128:(tt % 8 + 1) * 128]
                    if tt % 8 == 0:
                        P.wait("tensor", self.EV, lastu.get("pv3", 0))
                    for k in range(DX):
                        P.op("tensor", "matmul", pv,
                             self.resid[:, k * NT + bc + tt * 128:
                                        k * NT + bc + (tt + 1) * 128],
                             wv[:, k * 128:(k + 1) * 128],
                             start=(k == 0), stop=(k == DX - 1),
                             inc=self.MM if (k == DX - 1 and tt % 8 == 7)
                             else None)
                    if tt % 8 == 7:
                        P.wait("vector", self.MM)
                        for t2 in range(tt - 7, tt + 1):
                            P.op("vector", "tensor_scalar_mul",
                                 Vb2[:, t2 * 130:t2 * 130 + 64],
                                 self.pB[:, (t2 % 8) * 128:
                                         (t2 % 8) * 128 + 64],
                                 rstd_tt[:, b * 16 + t2:b * 16 + t2 + 1])
                            P.op("vector", "tensor_scalar_mul",
                                 Vb2[:, t2 * 130 + 65:t2 * 130 + 129],
                                 self.pB[:, (t2 % 8) * 128 + 64:
                                         (t2 % 8) * 128 + 128],
                                 rstd_tt[:, b * 16 + t2:b * 16 + t2 + 1],
                                 inc=self.EV if t2 == tt else None)
                        lastu["pv3"] = self.EV.n
                lastu[("w3a", par)] = self.MM.n
                self.barrier()
                fcb = self.sb["fcols"]
                rows_ = self.sb["rows"]
                blockones = self.sb["c_blockones"]
                for (buf, dst) in ((Qb, fcb[0:2, 22:23]), (Kb, fcb[0:2, 23:24])):
                    P.wait("scalar", self.EV)
                    P.wait("scalar", self.MM)
                    P.op("scalar", "activation", shf[:, 0:2048], buf[:, 0:2048],
                         AF.Square, inc=self.AC)
                    P.wait("tensor", self.AC)
                    P.wait("tensor", self.EV)
                    for n in range(4):
                        P.op("tensor", "matmul",
                             self.pA[0:2, n * 512:(n + 1) * 512], blockones[:],
                             shf[:, n * 512:(n + 1) * 512],
                             start=True, stop=True,
                             inc=self.MM if n == 3 else None)
                    P.wait("vector", self.MM)
                    P.op("vector", "reduce_max", dst, self.pA[0:2, 0:2048],
                         axis=AX.X, inc=self.EV)
                P.wait("vector", self.EV, self.EV.n)
                P.op("vector", "tensor_mul", fcb[0:2, 24:25], fcb[0:2, 22:23],
                     fcb[0:2, 23:24], inc=self.EV)
                P.wait("scalar", self.EV)
                P.op("scalar", "activation", fcb[0:2, 24:25], fcb[0:2, 24:25],
                     AF.Ln, inc=self.AC)
                P.wait("scalar", self.AC, self.AC.n)
                P.op("scalar", "activation", fcb[0:2, 24:25], fcb[0:2, 24:25],
                     AF.Exp, scale=0.5, inc=self.AC)
                P.wait("vector", self.AC)
                P.op("vector", "tensor_scalar_mul", fcb[0:2, 24:25],
                     fcb[0:2, 24:25], -0.125, inc=self.EV)
                P.wait("tensor", self.EV)
                P.op("tensor", "matmul", self.pR[0:1, 0:2], fcb[0:2, 24:25],
                     self.sb["c_ident_f"][0:2, 0:2], start=True, stop=True,
                     inc=self.MM)
                P.wait("vector", self.MM)
                P.op("vector", "tensor_copy", rows_[0:1, 0:2],
                     self.pR[0:1, 0:2], inc=self.EV)
                P.wait("tensor", self.EV)
                for hh in range(2):
                    P.op("tensor", "matmul", self.pR[:, 4 + hh:5 + hh],
                         self.sb["c_ones_row"][0:1, :],
                         rows_[0:1, hh:hh + 1], start=True, stop=True,
                         inc=self.MM if hh == 1 else None)
                P.wait("vector", self.MM)
                P.op("vector", "tensor_copy", fcb[:, 25:27], self.pR[:, 4:6],
                     inc=self.EV)
                P.wait("tensor", self.EV)
                P.wait("scalar", self.EV)
                self._attn_ptb = ptb

                def s3_out(h, c, p=p):
                    return shf[64 * h:64 * h + 64, c * 512:(c + 1) * 512]

                def s3_post(h, c, p=p, b=b):
                    lu = self._attn_lu
                    P.wait("sync", self.EV, self.EV.n)
                    P.dma("sync", av_scr[p, 64 * h:64 * h + 64,
                                         c * 512:(c + 1) * 512],
                          shf[64 * h:64 * h + 64, c * 512:(c + 1) * 512],
                          self.ST)
                    lu[("avsfree", h)] = (self.ST, self.ST.n)

                self.attn_heads(Qb, Kb, Vb2, causal=True, mrow=None,
                                out_fn=s3_out, post_fn=s3_post,
                                bound_cols=(fcb[:, 25:26], fcb[:, 26:27]))
                self.barrier()
            self.prep_transposed(woT, 0, self.io["L1_ffn_wo"], D, D)
            self.barrier()
            for n in range(4):
                P.wait("sync", self.MM, lastu.get("projstg", 0))
                for k in range(DX):
                    P.dma("sync", projstg[:, k * 512:(k + 1) * 512],
                          av_scr[k][:, n * 512:(n + 1) * 512], self.LS)
                for m in range(DX):
                    par = m % 2
                    pc = self.pB[:, par * 512:par * 512 + 512]
                    P.wait("tensor", self.LS)
                    P.wait("tensor", self.EV, lastu.get(("pprj3", par), 0))
                    for k in range(DX):
                        P.op("tensor", "matmul", pc,
                             woT[:, k * 1024 + m * 128:k * 1024 + m * 128
                                 + 128],
                             projstg[:, k * 512:(k + 1) * 512],
                             start=(k == 0), stop=(k == DX - 1),
                             inc=self.MM if k == DX - 1 else None)
                    if m == DX - 1:
                        lastu["projstg"] = self.MM.n
                    P.wait("vector", self.MM)
                    P.wait("vector", self.ST)
                    P.op("vector", "tensor_add",
                         self.sb["lnsum"][:, m * 512:(m + 1) * 512], pc,
                         self.resid[:, m * NT + bc + n * 512:
                                    m * NT + bc + n * 512 + 512],
                         inc=self.EV)
                    lastu[("pprj3", par)] = self.EV.n
                    P.wait("sync", self.EV)
                    P.dma("sync", h_scr[m][:, bc + n * 512:
                                           bc + n * 512 + 512],
                          self.sb["lnsum"][:, m * 512:(m + 1) * 512],
                          self.ST)
                    lastu[("hstg", m)] = self.ST.n
            self.barrier()

        al2 = self.arena_alloc()
        hid = self.hbuf[:, 0:HX * 512]
        tn2 = self.hbuf[:, HX * 512:HX * 512 + 4096]
        wb3 = [self.hbuf[:, HX * 512 + 4096:HX * 512 + 8192],
               self.hbuf[:, HX * 512 + 8192:HX * 512 + 12288]]
        hgb = self.hbuf[:, HX * 512 + 12288:HX * 512 + 16384]
        gateT = al2.take(64)
        b1a = fc[:, 256:512]
        logt = fc[:, 129:161]
        top2 = fc[:, 161:169]
        fidx = fc[:, 169:171]
        rw12 = fc[:, 171:173]
        msk = fc[:, 173:181]
        w8s = fc[:, 181:213]
        idxu = self.sbuf("idxu", [128, 8], U32)
        scr_m1 = self.dram_scratch("scr_m1", (NE, HX, 128, 1024))
        scr_m2 = self.dram_scratch("scr_m2", (NE, DX, 128, 4096))
        wtmp = self.sb["wtmp"]
        identb = self.sb["c_ident_b"]
        P.wait("vector", self.ST)
        P.wait("gpsimd", self.MM)
        P.dma("gpsimd", wtmp[0:NE, 0:D], self.io["L1_ffn_gate_w"], self.LD)
        P.wait("tensor", self.LD)
        P.wait("tensor", self.EV)
        for kX in range(DX):
            P.op("tensor", "transpose", self.pT[:, kX * 8:(kX + 1) * 8],
                 wtmp[0:NE, kX * 128:(kX + 1) * 128], identb[0:NE, 0:NE],
                 inc=self.MM if kX == DX - 1 else None)
        P.wait("vector", self.MM)
        for kX in range(DX):
            P.op("vector", "tensor_scalar_mul", gateT[:, kX * 8:(kX + 1) * 8],
                 self.pT[:, kX * 8:(kX + 1) * 8], wn2c[:, kX:kX + 1],
                 inc=self.EV if kX == DX - 1 else None)
        for e in range(NE):
            self.load_bias_cols(b1a[:, e * 32:(e + 1) * 32],
                                self.io["L1_ffn_b1"][e:e + 1, :], 32)
        self.barrier()

        for g in range(NT // 512):
            gc = g * 512
            P.wait("sync", self.EV, lastu.get("hgb", 0))
            P.wait("sync", self.AC, lastu.get("hgb_a", 0))
            for dXi in range(DX):
                P.dma("sync", hgb[:, dXi * 512:(dXi + 1) * 512],
                      h_scr[dXi][:, gc:gc + 512], self.LS)
            P.wait("scalar", self.LS)
            P.wait("vector", self.LS)
            for dXi in range(DX):
                P.wait("scalar", self.EV)
                P.wait("scalar", self.MM)
                P.op("scalar", "activation", sqs[:, 0:512],
                     hgb[:, dXi * 512:(dXi + 1) * 512],
                     AF.Square, inc=self.AC)
                P.wait("tensor", self.AC)
                P.op("tensor", "matmul", self.pR[0:1, 0:512], onescb[:],
                     sqs[:, 0:512], start=(dXi == 0), stop=(dXi == DX - 1),
                     inc=self.MM)
            P.wait("vector", self.MM)
            P.op("vector", "tensor_scalar_mul", rows[0:1, 0:512],
                 self.pR[0:1, 0:512], 1.0 / D, inc=self.EV)
            P.wait("scalar", self.EV)
            P.op("scalar", "activation", rows[0:1, 0:512], rows[0:1, 0:512],
                 AF.Sqrt, bias=eps6, inc=self.AC)
            P.wait("vector", self.AC)
            P.op("vector", "reciprocal", rows[0:1, 0:512], rows[0:1, 0:512],
                 inc=self.EV)
            P.wait("tensor", self.EV)
            P.op("tensor", "matmul", self.pA[:, 0:512], ones1f[0:1, :],
                 rows[0:1, 0:512], start=True, stop=True, inc=self.MM)
            P.wait("vector", self.MM)
            for dXi in range(DX):
                P.op("vector", "tensor_mul",
                     tn2[:, dXi * 512:(dXi + 1) * 512],
                     hgb[:, dXi * 512:(dXi + 1) * 512],
                     self.pA[:, 0:512], inc=self.EV)
            P.wait("tensor", self.EV)
            for k in range(DX):
                P.op("tensor", "matmul", self.pB[0:NE, 0:512],
                     gateT[:, k * NE:(k + 1) * NE],
                     tn2[:, k * 512:(k + 1) * 512],
                     start=(k == 0), stop=(k == DX - 1), inc=self.MM
                     if k == DX - 1 else None)
            P.wait("vector", self.MM)
            P.op("vector", "tensor_copy", t1f[0:NE, 0:512],
                 self.pB[0:NE, 0:512], inc=self.EV)
            P.wait("tensor", self.EV)
            for tt in range(4):
                P.op("tensor", "transpose", self.pA[:, tt * 8:(tt + 1) * 8],
                     t1f[0:NE, tt * 128:(tt + 1) * 128], identf[0:NE, 0:NE],
                     inc=self.MM if tt == 3 else None)
            P.wait("vector", self.MM)
            P.op("vector", "tensor_copy", logt, self.pA[:, 0:32], inc=self.EV)
            for tt in range(4):
                lg = logt[:, tt * 8:(tt + 1) * 8]
                P.wait("vector", self.EV)
                P.op("vector", "max", top2, lg, inc=self.EV)
                P.wait("vector", self.EV)
                P.op("vector", "max_index", idxu[:], top2, lg, inc=self.EV)
                P.wait("vector", self.EV)
                P.op("vector", "tensor_copy", fidx, idxu[:, 0:2], inc=self.EV)
                P.wait("vector", self.EV)
                P.op("vector", "tensor_sub", rw12[:, 0:1], top2[:, 1:2],
                     top2[:, 0:1], inc=self.EV)
                P.wait("scalar", self.EV)
                P.op("scalar", "activation", rw12[:, 0:1], rw12[:, 0:1],
                     AF.Sigmoid, scale=-1.0, inc=self.AC)
                P.wait("vector", self.AC)
                P.op("vector", "tensor_scalar", rw12[:, 1:2], rw12[:, 0:1],
                     -1.0, 1.0, ALU.mult, ALU.add, inc=self.EV)
                P.wait("vector", self.EV)
                P.op("vector", "tensor_scalar", msk, iota8[:],
                     fidx[:, 0:1], None, ALU.is_equal, inc=self.EV)
                P.wait("vector", self.EV)
                P.op("vector", "tensor_scalar_mul",
                     w8s[:, tt * 8:(tt + 1) * 8], msk, rw12[:, 0:1],
                     inc=self.EV)
                P.wait("vector", self.EV)
                P.op("vector", "tensor_scalar", msk, iota8[:],
                     fidx[:, 1:2], None, ALU.is_equal, inc=self.EV)
                P.wait("vector", self.EV)
                P.op("vector", "scalar_tensor_tensor",
                     w8s[:, tt * 8:(tt + 1) * 8], msk, rw12[:, 1:2],
                     w8s[:, tt * 8:(tt + 1) * 8], ALU.mult, ALU.add,
                     inc=self.EV)
            self.barrier()
            for e in range(NE):
                par = e % 2
                P.wait("tensor", self.EV)
                for tt in range(4):
                    P.op("tensor", "matmul",
                         self.pR[0:1, tt * 128:(tt + 1) * 128],
                         w8s[:, tt * 8 + e:tt * 8 + e + 1], identf[:],
                         start=True, stop=True,
                         inc=self.MM if tt == 3 else None)
                P.wait("vector", self.MM)
                P.op("vector", "tensor_copy", rows[0:1, 512:1024],
                     self.pR[0:1, 0:512], inc=self.EV)
                P.wait("tensor", self.EV)
                P.op("tensor", "matmul", self.pA[:, 1536:2048],
                     ones1f[0:1, :], rows[0:1, 512:1024],
                     start=True, stop=True, inc=self.MM)
                P.wait("vector", self.MM)
                P.op("vector", "tensor_copy", t1f[:], self.pA[:, 1536:2048],
                     inc=self.EV)
                P.wait("sync", self.MM)
                P.dma("sync", rowbuf[0:1, 0:1024],
                      self.io["L1_ffn_b2"][0:1, e * 1024:(e + 1) * 1024],
                      self.LS)
                P.wait("scalar", self.EV)
                P.wait("vector", self.EV)
                for mg in range(HX // 4):
                    wap = wb3[par][:, 0:4096]
                    if g == 0:
                        P.wait("vector", self.MM,
                               lastu.get(("wm1", par), 0))
                        for mm_ in range(4):
                            self.prep_transposed(
                                wap[:, mm_ * 1024:(mm_ + 1) * 1024], 0,
                                self.io["L1_ffn_w1"][e],
                                128, D, row0=(mg * 4 + mm_) * 128,
                                dst_stride=128,
                                scale_col=lambda kX: wn2c[:, kX:kX + 1])
                        P.wait("sync", self.EV)
                        P.dma("sync",
                              scr_m1[e, mg * 4:(mg + 1) * 4].transpose([1, 0, 2]),
                              wap.rearrange("p (m c) -> p m c", m=4), self.ST)
                    else:
                        P.wait("sync", self.MM, lastu.get(("wm1", par), 0))
                        P.dma("sync",
                              wap.rearrange("p (m c) -> p m c", m=4),
                              scr_m1[e, mg * 4:(mg + 1) * 4].transpose([1, 0, 2]),
                              self.LS)
                    P.wait("tensor", self.LS)
                    for mm_ in range(4):
                        m = mg * 4 + mm_
                        ph = self.pB[:, (m % 2) * 512:(m % 2) * 512 + 512]
                        P.wait("tensor", self.AC,
                               lastu.get(("ph3", m % 2), 0))
                        for k in range(DX):
                            P.op("tensor", "matmul", ph,
                                 wap[:, mm_ * 1024 + k * 128:
                                     mm_ * 1024 + (k + 1) * 128],
                                 tn2[:, k * 512:(k + 1) * 512],
                                 start=(k == 0), stop=(k == DX - 1),
                                 inc=self.MM if k == DX - 1 else None)
                        P.wait("scalar", self.MM)
                        P.op("scalar", "activation",
                             hid[:, m * 512:(m + 1) * 512], ph, AF.Gelu,
                             bias=b1a[:, e * 32 + m:e * 32 + m + 1],
                             inc=self.AC)
                        lastu[("ph3", m % 2)] = self.AC.n
                    lastu[("wm1", par)] = self.MM.n
                    par = (par + 1) % 2
                P.wait("tensor", self.LS)
                P.wait("tensor", self.AC)
                for dXi in range(DX):
                    wap = wb3[par][:, 0:4096]
                    if g == 0:
                        P.wait("vector", self.MM,
                               lastu.get(("wm1", par), 0))
                        self.prep_transposed(wap, 0,
                                             self.io["L1_ffn_w2"][e],
                                             128, HID, row0=dXi * 128,
                                             dst_stride=128)
                        P.wait("sync", self.EV)
                        P.dma("sync", scr_m2[e, dXi], wap, self.ST)
                    else:
                        P.wait("sync", self.MM, lastu.get(("wm1", par), 0))
                        P.dma("sync", wap, scr_m2[e, dXi], self.LS)
                    pz = self.pA[:, (dXi % 2) * 512:(dXi % 2) * 512 + 512]
                    P.wait("tensor", self.LS)
                    P.wait("tensor", self.EV, lastu.get(("pz3", dXi % 2), 0))
                    for k in range(HX):
                        P.op("tensor", "matmul", pz,
                             wap[:, k * 128:(k + 1) * 128],
                             hid[:, k * 512:(k + 1) * 512],
                             start=(k == 0), stop=False)
                    P.op("tensor", "matmul", pz,
                         rowbuf[0:1, dXi * 128:(dXi + 1) * 128], ones512[:],
                         start=False, stop=True, inc=self.MM)
                    lastu[("wm1", par)] = self.MM.n
                    P.wait("vector", self.MM)
                    P.op("vector", "tensor_mul",
                         self.sb["lnsum"][:, 0:512], pz, t1f[:], inc=self.EV)
                    lastu[("pz3", dXi % 2)] = self.EV.n
                    P.wait("vector", self.EV)
                    P.op("vector", "tensor_add",
                         self.resid[:, dXi * NT + gc:dXi * NT + gc + 512],
                         self.resid[:, dXi * NT + gc:dXi * NT + gc + 512],
                         self.sb["lnsum"][:, 0:512], inc=self.EV)
                    P.wait("vector", self.EV)
                    par = (par + 1) % 2
                self.barrier()
            self.postln_chunk(gc, lambda dXi: hgb[:, dXi * 512:
                                                  (dXi + 1) * 512])
            lastu["hgb"] = self.EV.n
            lastu["hgb_a"] = self.AC.n
            self.barrier()

    def prep_16rows(self, dst, dst_sub0, src_ap, row0, wcol):
        P = self.P
        wtmp = self.sb["wtmp"]
        identb = self.sb["c_ident_b"]
        P.wait("vector", self.ST)
        P.wait("gpsimd", self.MM)
        P.wait("gpsimd", self.EV)
        P.dma("gpsimd", wtmp[0:16, 0:D], src_ap[row0:row0 + 16, :], self.LD)
        P.wait("tensor", self.LD)
        P.wait("tensor", self.EV)
        for kX in range(DX):
            P.op("tensor", "transpose", self.pT[:, kX * 16:(kX + 1) * 16],
                 wtmp[0:16, kX * 128:(kX + 1) * 128], identb[0:16, 0:16],
                 inc=self.MM if kX == DX - 1 else None)
        P.wait("vector", self.MM)
        for kX in range(DX):
            P.op("vector", "tensor_scalar_mul",
                 dst[:, kX * 128 + dst_sub0:kX * 128 + dst_sub0 + 16],
                 self.pT[:, kX * 16:(kX + 1) * 16], wcol[:, kX:kX + 1],
                 inc=self.EV if kX == DX - 1 else None)
        P.wait("tensor", self.EV)


def build_program(slots):
    nc = bass.Bass("TRN2", target_bir_lowering=False)
    with contextlib.ExitStack() as ctx:
        mk = MK(nc, ctx, slots)
        mk.setup()
        mk.load_resid()
        for s in slots:
            getattr(mk, f"slot{s}")()
        mk.store_out()
        block = ctx.enter_context(nc.Block())
        mk.P.run(block)
    return nc


def shard_inputs(inputs, core):
    im = {}
    im["x_in"] = np.ascontiguousarray(inputs["L0_attn_x"][2 * core:2 * core + 2])
    return im


def kernel(**inputs):
    from concourse.bass_utils import run_bass_kernel_spmd
    consts = host_constants()
    slots_chain = [[0, 1, 2, 3]]
    x = np.asarray(inputs["L0_attn_x"], dtype=np.float32)
    cur = x
    for slots in slots_chain:
        nc = build_program(slots)
        in_maps = []
        for core in range(8):
            im = {"x_in": np.ascontiguousarray(cur[2 * core:2 * core + 2])}
            for s in slots:
                for nm in SLOT_WEIGHTS[s]:
                    im[nm] = np.ascontiguousarray(
                        np.asarray(inputs[nm], dtype=np.float32)
                        .reshape(WEIGHT_SHAPES[nm]))
            cns = sorted({n for s in slots for n in CONST_NAMES[s]})
            for nm in cns:
                im[nm] = consts[nm]
            in_maps.append(im)
        res = run_bass_kernel_spmd(nc, in_maps, core_ids=list(range(8)))
        cur = np.concatenate([np.asarray(r["x_out"]) for r in res.results],
                             axis=0)
    return cur


# version 06
# speedup vs baseline: 1.1692x; 1.1692x over previous
import contextlib
import numpy as np
import concourse.bass as bass
import concourse.mybir as mybir

F32 = mybir.dt.float32
BF16 = mybir.dt.bfloat16
U32 = mybir.dt.uint32
AF = mybir.ActivationFunctionType
ALU = mybir.AluOpType
AX = mybir.AxisListType

B = 2
T = 2048
D = 1024
DX = D // 128
HID = 4096
HX = HID // 128
NH = 16
HD = 64
NE = 8
NT = B * T

ENGINES = ("tensor", "vector", "scalar", "sync", "gpsimd")

AB_COLS = 17856


class _Alloc:
    def __init__(self, arena, cols):
        self.arena = arena
        self.cols = cols
        self.off = 0

    def take(self, ncols):
        assert self.off + ncols <= self.cols, \
            f"arena overflow: {self.off}+{ncols} > {self.cols}"
        ap = self.arena[:, self.off:self.off + ncols]
        self.off += ncols
        return ap


class Sem:
    def __init__(self, h, name):
        self.h = h
        self.name = name
        self.n = 0


class Prog:

    def __init__(self, nc, ctx):
        self.nc = nc
        self.ctx = ctx
        self.q = {e: [] for e in ENGINES}
        self.sems = {}

    def sem(self, name):
        s = Sem(self.ctx.enter_context(self.nc.semaphore(name)), name)
        self.sems[name] = s
        return s

    def emit(self, eng, fn):
        self.q[eng].append(fn)

    def op(self, eng, name, *args, inc=None, incv=1, **kw):
        if inc is not None:
            inc.n += incv
        def f(e, name=name, args=args, kw=kw, inc=inc, incv=incv):
            ins = getattr(e, name)(*args, **kw)
            if inc is not None:
                ins.then_inc(inc.h, incv)
        self.emit(eng, f)

    def wait(self, eng, sem, val=None):
        v = sem.n if val is None else val
        if v <= 0:
            return
        self.emit(eng, lambda e, s=sem.h, v=v: e.wait_ge(s, v))

    def dma(self, eng, out, in_, sem, serialize=True):
        if serialize and sem.n > 0:
            self.wait(eng, sem, sem.n)
        self.op(eng, "dma_start", out=out, in_=in_, inc=sem, incv=16)

    def run(self, block):
        for eng in ENGINES:
            ops = self.q[eng]
            getattr(block, eng)(lambda e, ops=ops: [f(e) for f in ops])


ROPE_PERM = np.array(list(range(0, 16)) + list(range(32, 48))
                     + list(range(16, 32)) + list(range(48, 64)))
ROPE_SHUF_MASK = list(range(16, 32)) + list(range(16))


def host_constants():
    c = {}
    c["c_ident_f"] = np.eye(128, dtype=np.float32)
    c["c_eps"] = np.tile(np.array([[1e-5, 1e-6]], dtype=np.float32), (128, 1))
    c["c_ident_b"] = np.eye(128, dtype=np.float32)
    c["c_neg8ident"] = (-8.0 * np.eye(128)).astype(np.float32)
    c["c_ones_col"] = np.ones((128, 1), dtype=np.float32)
    c["c_ones_row"] = np.ones((1, 128), dtype=np.float32)
    c["c_ones_row512"] = np.ones((1, 512), dtype=np.float32)
    c["c_ones512b"] = np.ones((1, 512), dtype=np.float32)
    c["c_iota8"] = np.tile(np.arange(8, dtype=np.float32), (128, 1))
    tri = np.where(np.arange(128)[None, :] <= np.arange(128)[:, None],
                   np.float32(0), np.float32(-1e30))
    c["c_mask_st"] = tri.astype(np.float32)
    c["c_mask_ts"] = np.ascontiguousarray(tri.T).astype(np.float32)
    bo = np.zeros((128, 2), dtype=np.float32)
    bo[:64, 0] = 1.0
    bo[64:, 1] = 1.0
    c["c_blockones"] = bo
    c["c_blockonesT"] = np.ascontiguousarray(bo.T)
    hd = HD
    inv_freq = (1.0 / (10000.0 ** (np.arange(0, hd, 2, dtype=np.float64) / hd)))
    freqs = np.arange(T, dtype=np.float64)[:, None] * inv_freq[None, :]
    emb = np.concatenate([freqs, freqs], axis=1)
    cos = emb.T * 0 + np.cos(emb).T
    sin = np.sin(emb).T
    sign = np.where(np.arange(hd) < 32, -1.0, 1.0)
    cosP = cos[ROPE_PERM]
    sinS = (sin * sign[:, None])[ROPE_PERM]
    c["c_rope_cos"] = np.tile(cosP, (2, 1)).astype(np.float32)
    c["c_rope_sin"] = np.tile(sinS, (2, 1)).astype(np.float32)
    return c


WEIGHT_SHAPES = {
    "L0_attn_Wz": (D, D), "L0_attn_Wh": (D, D),
    "L0_attn_bz": (1, D), "L0_attn_bh": (1, D),
    "L0_ffn_w1_w": (HID, D), "L0_ffn_w1_s": (1, HID),
    "L0_ffn_w2_w": (HID, D), "L0_ffn_w2_s": (1, HID),
    "L0_ffn_h1_w": (HID, HID), "L0_ffn_h1_s": (1, HID),
    "L0_ffn_h2_w": (HID, HID), "L0_ffn_h2_s": (1, HID),
    "L0_ffn_w3_w": (D, HID), "L0_ffn_w3_s": (1, D),
    "L1_attn_Wqkv": (3 * D, D), "L1_attn_bqkv": (1, 3 * D),
    "L1_attn_Wproj": (D, D), "L1_attn_bproj": (1, D),
    "L1_attn_qn_w": (1, HD), "L1_attn_kn_w": (1, HD),
    "L1_ffn_attn_norm_w": (1, D),
    "L1_ffn_wq": (D, D), "L1_ffn_wk": (D, D),
    "L1_ffn_wv": (D, D), "L1_ffn_wo": (D, D),
    "L1_ffn_moe_norm_w": (1, D),
    "L1_ffn_gate_w": (NE, D),
    "L1_ffn_w1": (NE, HID, D), "L1_ffn_b1": (NE, HID),
    "L1_ffn_w2": (NE, D, HID), "L1_ffn_b2": (1, NE * D),
}

SLOT_WEIGHTS = {
    0: ["L0_attn_Wz", "L0_attn_Wh", "L0_attn_bz", "L0_attn_bh"],
    1: ["L0_ffn_w1_w", "L0_ffn_w1_s", "L0_ffn_w2_w", "L0_ffn_w2_s",
        "L0_ffn_h1_w", "L0_ffn_h1_s", "L0_ffn_h2_w", "L0_ffn_h2_s",
        "L0_ffn_w3_w", "L0_ffn_w3_s"],
    2: ["L1_attn_Wqkv", "L1_attn_bqkv", "L1_attn_Wproj", "L1_attn_bproj",
        "L1_attn_qn_w", "L1_attn_kn_w"],
    3: ["L1_ffn_attn_norm_w", "L1_ffn_wq", "L1_ffn_wk", "L1_ffn_wv",
        "L1_ffn_wo", "L1_ffn_moe_norm_w", "L1_ffn_gate_w",
        "L1_ffn_w1", "L1_ffn_b1", "L1_ffn_w2", "L1_ffn_b2"],
}

CONST_NAMES = {
    0: ["c_ident_f", "c_ident_b", "c_ones_col", "c_ones_row", "c_eps"],
    1: ["c_ident_f", "c_ident_b", "c_ones_col", "c_ones_row", "c_eps"],
    2: ["c_ident_f", "c_ident_b", "c_ones_col", "c_ones_row", "c_neg8ident",
        "c_blockones", "c_blockonesT", "c_eps", "c_mask_st", "c_mask_ts"],
    3: ["c_ident_f", "c_ident_b", "c_ones_col", "c_ones_row", "c_neg8ident",
        "c_blockones", "c_blockonesT", "c_iota8", "c_mask_st", "c_mask_ts",
        "c_rope_cos", "c_rope_sin", "c_eps", "c_ones512b"],
}
CONST_DTYPES = {"c_ident_b": BF16, "c_blockones": BF16, "c_ones512b": BF16}
CONST_NOSBUF = {"c_rope_cos", "c_rope_sin"}
CONST_SHAPES = {
    "c_eps": (128, 2),
    "c_ident_f": (128, 128), "c_ident_b": (128, 128), "c_neg8ident": (128, 128),
    "c_ones_col": (128, 1), "c_ones_row": (1, 128), "c_iota8": (128, 8),
    "c_ones_row512": (1, 512), "c_ones512b": (1, 512),
    "c_mask_st": (128, 128), "c_mask_ts": (128, 128),
    "c_blockones": (128, 2), "c_blockonesT": (2, 128),
    "c_rope_cos": (128, T), "c_rope_sin": (128, T),
}


class MK:
    def __init__(self, nc, ctx, slots):
        self.nc = nc
        self.ctx = ctx
        self.slots = slots
        self.P = Prog(nc, ctx)
        self.io = {}
        self.sb = {}

    def dram_in(self, name, shape):
        ap = self.nc.dram_tensor(name, list(shape), F32, kind="ExternalInput").ap()
        self.io[name] = ap
        return ap

    def dram_out(self, name, shape):
        ap = self.nc.dram_tensor(name, list(shape), F32, kind="ExternalOutput").ap()
        self.io[name] = ap
        return ap

    def dram_scratch(self, name, shape, dtype=BF16):
        ap = self.nc.dram_tensor(name, list(shape), dtype, kind="Internal").ap()
        self.io[name] = ap
        return ap

    def sbuf(self, name, shape, dtype=F32):
        h = self.ctx.enter_context(
            self.nc.sbuf_tensor("sb_" + name, list(shape), dtype))
        self.sb[name] = h
        return h

    def psum(self, name, shape, dtype=F32):
        h = self.ctx.enter_context(self.nc.psum_tensor(name, list(shape), dtype))
        return h

    def arena_alloc(self):
        return _Alloc(self.arena, AB_COLS)

    def setup(self):
        P = self.P
        self.LD = P.sem("s_ld")
        self.LS = P.sem("s_ls")
        self.MM = P.sem("s_mm")
        self.EV = P.sem("s_ev")
        self.AC = P.sem("s_ac")
        self.PO = P.sem("s_po")
        self.ST = P.sem("s_st")
        self.STG = P.sem("s_stg")
        self.TICK = P.sem("s_tick")

        self.resid = self.sbuf("resid", [128, DX * NT], BF16)
        self.arena = self.sbuf("arena", [128, AB_COLS], BF16)
        self.hbuf = self.sbuf("hbuf", [128, DX * NT], BF16)

        self.dram_in("x_in", (B, T, D))
        self.dram_out("x_out", (B, T, D))
        cns = sorted({n for s in self.slots for n in CONST_NAMES[s]})
        for nm in cns:
            self.dram_in(nm, CONST_SHAPES[nm])
            if nm not in CONST_NOSBUF:
                self.sbuf(nm, CONST_SHAPES[nm], CONST_DTYPES.get(nm, F32))
        for s in self.slots:
            for nm in SLOT_WEIGHTS[s]:
                self.dram_in(nm, WEIGHT_SHAPES[nm])
        self.sbuf("c_ones_col_b", [128, 1], BF16)
        self.sbuf("c_ones_row_b", [1, 128], BF16)

        for nm in cns:
            if nm in CONST_NOSBUF:
                continue
            if CONST_DTYPES.get(nm, F32) == F32:
                P.dma("sync", self.sb[nm][:], self.io[nm], self.LS)
            else:
                P.dma("gpsimd", self.sb[nm][:], self.io[nm], self.LD)
        P.dma("gpsimd", self.sb["c_ones_col_b"][:], self.io["c_ones_col"], self.LD)
        P.dma("gpsimd", self.sb["c_ones_row_b"][:], self.io["c_ones_row"], self.LD)

        self.pA = self.psum("pA", [128, 2048], F32)
        self.pB = self.psum("pB", [128, 1024], F32)
        self.pT = self.psum("pT", [128, 1024], BF16)
        self.pR = self.psum("pR", [128, 512], F32)
        self.sbuf("wtmp", [128, 4096], BF16)
        self.sbuf("lnsum", [128, DX * 512], BF16)
        self.sbuf("rows", [1, 1024], F32)
        self.sbuf("rows2", [2, 2048], F32)
        self.sbuf("rowb", [1, 128], BF16)
        self.sbuf("rowd", [1, 512], BF16)
        self.sbuf("fcols", [128, 512], F32)
        self.sbuf("rowbuf", [1, 1024], F32)
        self.sbuf("t1f", [128, 512], F32)
        self.sbuf("otile", [128, D], BF16)
        self.barrier("sync")

    def barrier(self, owner="vector"):
        P = self.P
        sems = [s for s in P.sems.values()
                if s is not self.TICK and s.n > 0]
        for s in sems:
            P.wait(owner, s)
        self.TICK.n += 1
        P.emit(owner, lambda e, h=self.TICK.h: e.sem_inc(h, 1))
        for e in ENGINES:
            P.wait(e, self.TICK)

    def prep_transposed(self, dst, dst_col0, src_ap, rows, cols, row0=0,
                        dst_stride=None, scale_col=None):
        P = self.P
        if dst_stride is None:
            dst_stride = rows
        wtmp = self.sb["wtmp"]
        identb = self.sb["c_ident_b"]
        nk = cols // 128
        nr = rows // 128
        P.wait("vector", self.ST)
        for r in range(nr):
            P.wait("gpsimd", self.MM)
            P.wait("gpsimd", self.EV)
            P.dma("gpsimd", wtmp[:, 0:cols],
                  src_ap[row0 + r * 128: row0 + (r + 1) * 128, :], self.LD)
            for k0 in range(0, nk, 8):
                kn = min(8, nk - k0)
                P.wait("tensor", self.LD)
                P.wait("tensor", self.EV)
                for kk in range(kn):
                    kX = k0 + kk
                    P.op("tensor", "transpose",
                         self.pT[:, kk * 128:(kk + 1) * 128],
                         wtmp[:, kX * 128:(kX + 1) * 128], identb[:],
                         inc=self.MM if kk == kn - 1 else None)
                P.wait("vector", self.MM)
                for kk in range(kn):
                    kX = k0 + kk
                    c0 = dst_col0 + kX * dst_stride + r * 128
                    if scale_col is None:
                        P.op("vector", "tensor_copy", dst[:, c0:c0 + 128],
                             self.pT[:, kk * 128:(kk + 1) * 128],
                             inc=self.EV if kk == kn - 1 else None)
                    else:
                        P.op("vector", "tensor_scalar_mul", dst[:, c0:c0 + 128],
                             self.pT[:, kk * 128:(kk + 1) * 128],
                             scale_col(kX),
                             inc=self.EV if kk == kn - 1 else None)
        P.wait("tensor", self.EV)

    def load_bias_cols(self, dst_ap, src_row_ap, n):
        P = self.P
        onesr = self.sb["c_ones_row"]
        rowbuf = self.sb["rowbuf"]
        for j0 in range(0, n, 8):
            jn = min(8, n - j0)
            P.wait("sync", self.MM)
            P.dma("sync", rowbuf[0:1, 0:jn * 128],
                  src_row_ap[0:1, j0 * 128:(j0 + jn) * 128], self.LS)
            P.wait("tensor", self.LS)
            P.wait("tensor", self.EV)
            for j in range(jn):
                P.op("tensor", "matmul", self.pR[:, j:j + 1],
                     rowbuf[0:1, j * 128:(j + 1) * 128], onesr[0:1, 0:1],
                     start=True, stop=True,
                     inc=self.MM if j == jn - 1 else None)
            P.wait("vector", self.MM)
            P.op("vector", "tensor_copy", dst_ap[:, j0:j0 + jn],
                 self.pR[:, 0:jn], inc=self.EV)

    def load_resid(self):
        P = self.P
        src = self.io["x_in"]
        xtm = self.sb["wtmp"]
        identb = self.sb["c_ident_b"]
        for b in range(B):
            for tt in range(T // 128):
                P.wait("gpsimd", self.MM)
                P.dma("gpsimd", xtm[:, 0:D],
                      src[b, tt * 128:(tt + 1) * 128, :], self.LD)
                P.wait("tensor", self.LD)
                P.wait("tensor", self.EV)
                for dXi in range(DX):
                    P.op("tensor", "transpose",
                         self.pT[:, dXi * 128:(dXi + 1) * 128],
                         xtm[:, dXi * 128:(dXi + 1) * 128], identb[:],
                         inc=self.MM if dXi == DX - 1 else None)
                P.wait("vector", self.MM)
                for dXi in range(DX):
                    c0 = dXi * NT + b * T + tt * 128
                    P.op("vector", "tensor_copy",
                         self.resid[:, c0:c0 + 128],
                         self.pT[:, dXi * 128:(dXi + 1) * 128],
                         inc=self.EV if dXi == DX - 1 else None)
        self.barrier()

    def store_out(self, src_name=None):
        P = self.P
        dst = self.io["x_out"]
        identb = self.sb["c_ident_b"]
        otile = self.sb["otile"]
        src = self.sb[src_name] if src_name else self.resid
        for b in range(B):
            for tt in range(T // 128):
                P.wait("tensor", self.STG)
                P.wait("tensor", self.EV)
                for dXi in range(DX):
                    c0 = dXi * NT + b * T + tt * 128
                    P.op("tensor", "transpose",
                         self.pT[:, dXi * 128:(dXi + 1) * 128],
                         src[:, c0:c0 + 128], identb[:],
                         inc=self.MM if dXi == DX - 1 else None)

                P.wait("vector", self.MM)
                P.wait("vector", self.STG)
                P.op("vector", "tensor_copy", otile[:, 0:D],
                     self.pT[:, 0:1024], inc=self.EV)
                P.wait("gpsimd", self.EV)
                P.dma("gpsimd", dst[b, tt * 128:(tt + 1) * 128, :],
                      otile[:, 0:D], self.STG)
        P.wait("gpsimd", self.STG)
        self.barrier("gpsimd")

    def postln_chunk(self, rcol0, slot_fn, bias_col=None, add_resid=True,
                     extra_fn=None, presummed=False):
        P = self.P
        lnsum, lnsq = self.sb["lnsum"], self.sb["wtmp"]
        onesb = self.sb["c_ones_col_b"]
        ones1f = self.sb["c_ones_row"]
        rows = self.sb["rows"]
        t1f = self.sb["t1f"]
        for dXi in range(DX if not presummed else 0):
            rc = rcol0 + dXi * NT
            d_ap = lnsum[:, dXi * 512:(dXi + 1) * 512]
            s_ap = slot_fn(dXi)
            last = (dXi == DX - 1) and extra_fn is None
            if add_resid:
                if bias_col is not None:
                    P.op("vector", "scalar_tensor_tensor", d_ap, s_ap,
                         bias_col, self.resid[:, rc:rc + 512], ALU.add,
                         ALU.add, inc=self.EV if last else None)
                else:
                    P.op("vector", "tensor_add", d_ap, s_ap,
                         self.resid[:, rc:rc + 512],
                         inc=self.EV if last else None)
            else:
                P.op("vector", "tensor_copy", d_ap, s_ap,
                     inc=self.EV if last else None)
        if extra_fn is not None:
            P.wait("vector", self.EV)
            for dXi in range(DX):
                d_ap = lnsum[:, dXi * 512:(dXi + 1) * 512]
                P.op("vector", "tensor_add", d_ap, d_ap, extra_fn(dXi),
                     inc=self.EV if dXi == DX - 1 else None)
        P.wait("scalar", self.EV)
        for dXi in range(DX):
            P.op("scalar", "activation", lnsq[:, dXi * 512:(dXi + 1) * 512],
                 lnsum[:, dXi * 512:(dXi + 1) * 512], AF.Square,
                 inc=self.AC if dXi == DX - 1 else None)
        P.wait("tensor", self.EV)
        P.wait("tensor", self.AC)
        for dXi in range(DX):
            P.op("tensor", "matmul", self.pB[0:1, 0:512], onesb[:],
                 lnsum[:, dXi * 512:(dXi + 1) * 512],
                 start=(dXi == 0), stop=(dXi == DX - 1))
        for dXi in range(DX):
            P.op("tensor", "matmul", self.pB[0:1, 512:1024], onesb[:],
                 lnsq[:, dXi * 512:(dXi + 1) * 512],
                 start=(dXi == 0), stop=(dXi == DX - 1),
                 inc=self.MM if dXi == DX - 1 else None)
        P.wait("vector", self.MM)
        P.op("vector", "tensor_scalar_mul", rows[0:1, 0:512],
             self.pB[0:1, 0:512], 1.0 / D, inc=self.EV)
        P.wait("scalar", self.EV)
        P.op("scalar", "activation", rows[0:1, 512:1024], rows[0:1, 0:512],
             AF.Square, inc=self.AC)
        P.wait("vector", self.AC)
        P.op("vector", "scalar_tensor_tensor", rows[0:1, 512:1024],
             self.pB[0:1, 512:1024], 1.0 / D, rows[0:1, 512:1024],
             ALU.mult, ALU.subtract, inc=self.EV)
        P.wait("scalar", self.EV)
        P.op("scalar", "activation", rows[0:1, 512:1024],
             rows[0:1, 512:1024], AF.Ln,
             bias=self.sb["c_eps"][0:1, 0:1], inc=self.AC)
        P.wait("scalar", self.AC, self.AC.n)
        P.op("scalar", "activation", rows[0:1, 512:1024],
             rows[0:1, 512:1024], AF.Exp, scale=-0.5, inc=self.AC)
        P.wait("tensor", self.AC)
        P.op("tensor", "matmul", self.pA[:, 0:512], ones1f[0:1, :],
             rows[0:1, 0:512], start=True, stop=True)
        P.op("tensor", "matmul", self.pA[:, 512:1024], ones1f[0:1, :],
             rows[0:1, 512:1024], start=True, stop=True, inc=self.MM)
        P.wait("vector", self.MM)
        for dXi in range(DX):
            rc = rcol0 + dXi * NT
            P.op("vector", "tensor_sub", t1f[:],
                 lnsum[:, dXi * 512:(dXi + 1) * 512], self.pA[:, 0:512],
                 inc=self.EV)
            P.wait("vector", self.EV)
            P.op("vector", "tensor_mul", self.resid[:, rc:rc + 512],
                 t1f[:], self.pA[:, 512:1024], inc=self.EV)
            P.wait("vector", self.EV)

    def slot0(self):
        P = self.P
        if True:
            al = self.arena_alloc()
            wzT = al.take(DX * D)
            whT = al.take(DX * D)
            bcol = self.sb["fcols"][:, 0:2 * DX]
            nbz = self.sb["fcols"][:, 2 * DX:3 * DX]
            SC = 512
            hb = self.hbuf
            Ab = hb[:, 0:DX * SC]
            Bb = hb[:, DX * SC:2 * DX * SC]
            XM = [hb[:, 2 * DX * SC:3 * DX * SC],
                  hb[:, 3 * DX * SC:4 * DX * SC]]
            zt = hb[:, 4 * DX * SC:4 * DX * SC + SC]
            sg = hb[:, 4 * DX * SC + SC:4 * DX * SC + 2 * SC]
            hp5 = hb[:, 4 * DX * SC + 2 * SC:4 * DX * SC + 3 * SC]
            gt = hb[:, 4 * DX * SC + 3 * SC:4 * DX * SC + 4 * SC]

            self.prep_transposed(wzT, 0, self.io["L0_attn_Wz"], D, D)
            self.prep_transposed(whT, 0, self.io["L0_attn_Wh"], D, D)
            self.load_bias_cols(bcol[:, 0:DX], self.io["L0_attn_bz"], DX)
            self.load_bias_cols(bcol[:, DX:2 * DX], self.io["L0_attn_bh"], DX)
            P.wait("vector", self.EV)
            P.op("vector", "tensor_scalar_mul", nbz[:], bcol[:, 0:DX], -1.0,
                 inc=self.EV)
            self.barrier()

            for b in range(B):
                for ci in range(T // SC):
                    c0 = ci * SC
                    par = ci % 2
                    for hXi in range(DX):
                        pk = self.pA[:, (hXi % 2) * 512:(hXi % 2) * 512 + 512]
                        ph = self.pB[:, (hXi % 2) * 512:(hXi % 2) * 512 + 512]
                        P.wait("tensor", self.EV)
                        P.wait("tensor", self.AC)
                        for (w, p) in ((wzT, pk), (whT, ph)):
                            for dXi in range(DX):
                                rhs = self.resid[:, dXi * NT + b * T + c0:
                                                 dXi * NT + b * T + c0 + SC]
                                lhsT = w[:, dXi * D + hXi * 128:
                                         dXi * D + (hXi + 1) * 128]
                                for n in range(SC // 512):
                                    P.op("tensor", "matmul",
                                         p[:, n * 512:(n + 1) * 512], lhsT,
                                         rhs[:, n * 512:(n + 1) * 512],
                                         start=(dXi == 0), stop=(dXi == DX - 1),
                                         inc=self.MM if (w is whT
                                                         and dXi == DX - 1
                                                         and n == SC // 512 - 1)
                                         else None)
                        P.wait("scalar", self.MM)
                        hcol = slice(hXi * SC, (hXi + 1) * SC)
                        P.op("scalar", "activation", Ab[:, hcol], pk[:],
                             AF.Sigmoid, bias=nbz[:, hXi:hXi + 1], scale=-1.0,
                             inc=self.AC)
                        P.op("scalar", "activation", zt[:], pk[:], AF.Sigmoid,
                             bias=bcol[:, hXi:hXi + 1], inc=self.AC)
                        P.op("scalar", "activation", sg[:], ph[:], AF.Sigmoid,
                             bias=bcol[:, DX + hXi:DX + hXi + 1], inc=self.AC)
                        P.wait("vector", self.MM)
                        P.op("vector", "tensor_scalar", hp5[:], ph[:],
                             bcol[:, DX + hXi:DX + hXi + 1], 0.5, ALU.add,
                             ALU.add, inc=self.EV)
                        P.wait("vector", self.EV)
                        P.wait("vector", self.AC)
                        P.op("vector", "tensor_max", gt[:], hp5[:], sg[:],
                             inc=self.EV)
                        P.wait("vector", self.EV)
                        P.op("vector", "tensor_mul", Bb[:, hcol], zt[:], gt[:],
                             inc=self.EV)
                        P.wait("vector", self.EV)
                        init = 0.0 if ci == 0 else \
                            XM[1 - par][:, hXi * SC + SC - 1: hXi * SC + SC]
                        P.op("vector", "tensor_tensor_scan", XM[par][:, hcol],
                             Ab[:, hcol], Bb[:, hcol], init, ALU.mult, ALU.add,
                             inc=self.EV)
                        P.wait("vector", self.EV)
                    self.barrier()
                    self.postln_chunk(
                        b * T + c0,
                        lambda dXi, par=par:
                            XM[par][:, dXi * SC: dXi * SC + 512])
                    self.barrier()


    def slot1(self):
        P = self.P
        al = self.arena_alloc()
        h1 = al.take(HX * 512)
        h2 = self.hbuf[:, 0:HX * 512]
        wb = [self.hbuf[:, HX * 512:HX * 512 + 8192],
              self.hbuf[:, HX * 512 + 8192:HX * 512 + 16384]]
        silu_s = al.take(512)
        fc = self.sb["fcols"]
        s1c, s2c = fc[:, 0:32], fc[:, 32:64]
        sh1c, sh2c = fc[:, 64:96], fc[:, 96:128]
        s3c = fc[:, 128:136]

        scr = {
            "w1": self.dram_scratch("scr_w1", (HX, 128, 1024)),
            "w2": self.dram_scratch("scr_w2", (HX, 128, 1024)),
            "h1": self.dram_scratch("scr_h1", (HX, 128, 4096)),
            "h2": self.dram_scratch("scr_h2", (HX, 128, 4096)),
            "w3": self.dram_scratch("scr_w3", (DX, 128, 4096)),
        }
        self.load_bias_cols(s1c, self.io["L0_ffn_w1_s"], 32)
        self.load_bias_cols(s2c, self.io["L0_ffn_w2_s"], 32)
        self.load_bias_cols(sh1c, self.io["L0_ffn_h1_s"], 32)
        self.load_bias_cols(sh2c, self.io["L0_ffn_h2_s"], 32)
        self.load_bias_cols(s3c, self.io["L0_ffn_w3_s"], DX)
        self.barrier()

        lastmm = {}

        def get_block(g, bufs, par, tag, src_name, m, nk, blkcols):
            wap = bufs[par][:, 0:blkcols]
            if g == 0:
                P.wait("sync", self.MM, lastmm.get((tag, par), 0))
                self.prep_transposed(wap, 0, self.io[src_name], 128,
                                     nk * 128, row0=m * 128, dst_stride=128)
                P.wait("sync", self.EV)
                P.dma("sync", scr[tag][m], wap, self.ST)
            else:
                P.wait("sync", self.MM, lastmm.get((tag, par), 0))
                P.dma("sync", wap, scr[tag][m], self.LS)
            return wap

        def stageA(g, gc):
            for m in range(HX):
                par = m % 2
                w1ap = get_block(g, [wb[0][:, 0:1024], wb[1][:, 0:1024]],
                                 par, "w1", "L0_ffn_w1_w", m, DX, 1024)
                w2ap = get_block(g, [wb[0][:, 1024:2048], wb[1][:, 1024:2048]],
                                 par, "w2", "L0_ffn_w2_w", m, DX, 1024)
                pa = self.pA[:, par * 1024: par * 1024 + 512]
                pb = self.pA[:, par * 1024 + 512: par * 1024 + 1024]
                P.wait("tensor", self.LS)
                P.wait("tensor", self.EV, lastmm.get(("psA", par), 0))
                P.wait("tensor", self.AC, lastmm.get(("psAa", par), 0))
                for k in range(DX):
                    rhs = self.resid[:, k * NT + gc: k * NT + gc + 512]
                    P.op("tensor", "matmul", pa, w1ap[:, k * 128:(k + 1) * 128],
                         rhs, start=(k == 0), stop=(k == DX - 1))
                for k in range(DX):
                    rhs = self.resid[:, k * NT + gc: k * NT + gc + 512]
                    P.op("tensor", "matmul", pb, w2ap[:, k * 128:(k + 1) * 128],
                         rhs, start=(k == 0), stop=(k == DX - 1),
                         inc=self.MM if k == DX - 1 else None)
                lastmm[("w1", par)] = self.MM.n
                lastmm[("w2", par)] = self.MM.n
                P.wait("scalar", self.MM)
                P.wait("scalar", self.EV)
                P.op("scalar", "activation", silu_s[:, 0:512], pa, AF.Silu,
                     scale=s1c[:, m:m + 1], inc=self.AC)
                lastmm[("psAa", par)] = self.AC.n
                P.wait("vector", self.AC)
                P.op("vector", "scalar_tensor_tensor",
                     h1[:, m * 512:(m + 1) * 512], pb, s2c[:, m:m + 1],
                     silu_s[:, 0:512], ALU.mult, ALU.mult, inc=self.EV)
                lastmm[("psA", par)] = self.EV.n

        def stageMM(g, gc, tag, src_name, src_buf, dst_buf, nm, nk, scol,
                    act):
            for mg in range(nm // 2):
                par = mg % 2
                if g == 0:
                    P.wait("vector", self.MM, lastmm.get((tag, par), 0))
                    for mm_ in range(2):
                        self.prep_transposed(
                            wap_sub(wb[par], mm_, nk), 0, self.io[src_name],
                            128, nk * 128, row0=(mg * 2 + mm_) * 128,
                            dst_stride=128)
                    P.wait("sync", self.EV)
                    P.dma("sync", scr[tag][mg * 2:(mg + 1) * 2].transpose(
                        [1, 0, 2]),
                        wb[par][:, 0:2 * nk * 128].rearrange(
                            "p (m c) -> p m c", m=2), self.ST)
                else:
                    P.wait("sync", self.MM, lastmm.get((tag, par), 0))
                    P.dma("sync", wb[par][:, 0:2 * nk * 128].rearrange(
                        "p (m c) -> p m c", m=2),
                        scr[tag][mg * 2:(mg + 1) * 2].transpose([1, 0, 2]),
                        self.LS)
                P.wait("tensor", self.LS)
                for mm_ in range(2):
                    m = mg * 2 + mm_
                    p2 = m % 2
                    pc = self.pB[:, p2 * 512: p2 * 512 + 512]
                    P.wait("tensor", self.AC,
                           lastmm.get(("psB" + act.name, p2), 0))
                    for k in range(nk):
                        P.op("tensor", "matmul", pc,
                             wb[par][:, mm_ * nk * 128 + k * 128:
                                     mm_ * nk * 128 + (k + 1) * 128],
                             src_buf[:, k * 512:(k + 1) * 512],
                             start=(k == 0), stop=(k == nk - 1),
                             inc=self.MM if k == nk - 1 else None)
                    P.wait("scalar", self.MM)
                    P.op("scalar", "activation",
                         dst_buf[:, m * 512:(m + 1) * 512],
                         pc, act, scale=scol[:, m:m + 1], inc=self.AC)
                    lastmm[("psB" + act.name, p2)] = self.AC.n
                lastmm[(tag, par)] = self.MM.n

        def wap_sub(wbuf, mm_, nk):
            return wbuf[:, mm_ * nk * 128:(mm_ + 1) * nk * 128]

        for g in range(NT // 512):
            gc = g * 512
            stageA(g, gc)
            self.barrier()
            stageMM(g, gc, "h1", "L0_ffn_h1_w", h1, h2, HX, HX, sh1c, AF.Gelu)
            self.barrier()
            stageMM(g, gc, "h2", "L0_ffn_h2_w", h2, h1, HX, HX, sh2c, AF.Gelu)
            self.barrier()
            for m in range(DX):
                par = m % 2
                wap = get_block(g, wb, par, "w3", "L0_ffn_w3_w", m, HX, 4096)
                pc = self.pB[:, par * 512: par * 512 + 512]
                P.wait("tensor", self.LS)
                P.wait("tensor", self.EV, lastmm.get(("psD", par), 0))
                for k in range(HX):
                    P.op("tensor", "matmul", pc, wap[:, k * 128:(k + 1) * 128],
                         h1[:, k * 512:(k + 1) * 512],
                         start=(k == 0), stop=(k == HX - 1),
                         inc=self.MM if k == HX - 1 else None)
                lastmm[("w3", par)] = self.MM.n
                P.wait("vector", self.MM)
                P.op("vector", "scalar_tensor_tensor",
                     self.sb["lnsum"][:, m * 512:(m + 1) * 512], pc,
                     s3c[:, m:m + 1], self.resid[:, m * NT + gc:
                                                 m * NT + gc + 512],
                     ALU.mult, ALU.add, inc=self.EV)
                lastmm[("psD", par)] = self.EV.n
            self.barrier()
            self.postln_chunk(gc, None, presummed=True)
            self.barrier()


    def col64_to_128(self, dst_col, src_row_ap, n64=64):
        P = self.P
        rowbuf = self.sb["rowbuf"]
        onesr = self.sb["c_ones_row"]
        P.wait("sync", self.MM)
        P.dma("sync", rowbuf[0:1, 0:n64], src_row_ap[0:1, 0:n64], self.LS)
        P.wait("tensor", self.LS)
        P.wait("tensor", self.EV)
        P.op("tensor", "matmul", self.pR[0:n64, 0:1], rowbuf[0:1, 0:n64],
             onesr[0:1, 0:1], start=True, stop=True, inc=self.MM)
        P.wait("vector", self.MM)
        P.op("vector", "tensor_copy", dst_col[0:64, :], self.pR[0:64, 0:1],
             inc=self.EV)
        P.wait("vector", self.EV)
        P.op("vector", "tensor_copy", dst_col[64:128, :], self.pR[0:64, 0:1],
             inc=self.EV)

    def attn_heads(self, Qb, Kb, Vb2, causal, mrow, out_fn, post_fn=None,
                   bound_cols=None):
        P = self.P
        fc = self.sb["fcols"]
        m_all = fc[:, 0:16]
        mh = fc[:, 16:20]
        t1f = self.sb["t1f"]
        ones1f = self.sb["c_ones_row"]
        n8i = self.sb["c_neg8ident"]
        mask_st = self.sb["c_mask_st"]
        mask_ts = self.sb["c_mask_ts"]
        lu = self._attn_lu = {}

        def wl(eng, key):
            v = lu.get(key)
            if v is None:
                return
            if isinstance(v, list):
                for sem, val in v:
                    P.wait(eng, sem, val)
            else:
                P.wait(eng, v[0], v[1])

        lu[("pa", 0)] = (self.EV, self.EV.n)
        lu[("pa", 1)] = (self.EV, self.EV.n)
        for tg_ in ("s", 0, 1):
            lu[("pst", tg_, 0)] = [(self.AC, self.AC.n), (self.EV, self.EV.n)]
            lu[("pst", tg_, 1)] = [(self.AC, self.AC.n), (self.EV, self.EV.n)]
        lu["mrow_free"] = (self.MM, self.MM.n)

        def passA_items(h):
            hb = 64 * h
            items = []
            for i in range(16):
                def item(i=i, hb=hb, h=h):
                    ncols = (i + 1) * 128 if causal else 2048
                    sl = i % 2
                    base = sl * 1024
                    msl = mh[:, sl * 2:sl * 2 + 2]
                    nhalf = min(2, (ncols + 1023) // 1024)
                    wl("vector", ("mh", sl))
                    for hf in range(nhalf):
                        w = min(1024, ncols - hf * 1024)
                        wl("tensor", ("pa", sl))
                        nch = (w + 511) // 512
                        for cc in range(nch):
                            cw = min(512, w - cc * 512)
                            P.op("tensor", "matmul",
                                 self.pA[:, base + cc * 512:
                                         base + cc * 512 + cw],
                                 Qb[hb:hb + 64, i * 128:(i + 1) * 128],
                                 Kb[hb:hb + 64, hf * 1024 + cc * 512:
                                    hf * 1024 + cc * 512 + cw],
                                 start=True, stop=True,
                                 inc=self.MM if cc == nch - 1 else None)
                        mmc = self.MM.n
                        P.wait("vector", self.MM, mmc)
                        if causal and hf * 1024 <= i * 128 < hf * 1024 + w:
                            off = i * 128 - hf * 1024
                            P.op("vector", "tensor_add",
                                 self.pA[:, base + off:base + off + 128],
                                 self.pA[:, base + off:base + off + 128],
                                 mask_st[:], inc=self.EV)
                            P.wait("vector", self.EV, self.EV.n)
                        P.op("vector", "reduce_max", msl[:, hf:hf + 1],
                             self.pA[:, base:base + w], axis=AX.X,
                             negate=True, inc=self.EV)
                        lu[("pa", sl)] = (self.EV, self.EV.n)
                    P.wait("vector", self.EV, self.EV.n)
                    if nhalf == 2:
                        P.op("vector", "tensor_tensor", msl[:, 0:1],
                             msl[:, 0:1], msl[:, 1:2], ALU.min, inc=self.EV)
                        P.wait("vector", self.EV, self.EV.n)
                    P.op("vector", "tensor_scalar_mul", m_all[:, i:i + 1],
                         msl[:, 0:1], 0.125, inc=self.EV)
                    lu[("mh", sl)] = (self.EV, self.EV.n)
                    lu[("mcol", h, i)] = (self.EV, self.EV.n)
                items.append(item)
            return items

        def emit_mrows(h):
            wl("tensor", ("pa", 0))
            for i in range(16):
                wl("tensor", ("mcol", h, i))
            wl("tensor", "mrow_free")
            for i in range(16):
                P.op("tensor", "matmul",
                     self.pA[0:1, (i % 4) * 128:(i % 4 + 1) * 128],
                     m_all[:, i:i + 1], n8i[:], start=True, stop=True,
                     inc=self.MM if i % 4 == 3 else None)
                if i % 4 == 3:
                    mmc = self.MM.n
                    P.wait("vector", self.MM, mmc)
                    P.op("vector", "tensor_copy",
                         mrow[0:1, (i - 3) * 128:(i + 1) * 128],
                         self.pA[0:1, 0:512], inc=self.EV)
                    if i < 15:
                        P.wait("tensor", self.EV)
            lu[("pa", 0)] = (self.EV, self.EV.n)
            lu["mrow_ready"] = (self.EV, self.EV.n)

        def passB_items(h):
            hb = 64 * h
            res = (self._attn_res or {}).get(h) if getattr(
                self, "_attn_res", None) else None
            if res is None:
                res = dict(pst=(self.pB[:, 0:512], self.pB[:, 512:1024]),
                           av=self.pR[0:65, 0:512],
                           ptb=(self._attn_ptb[:, 0:512],
                                self._attn_ptb[:, 512:1024]),
                           rep=self.pB[:, 512:1024], repk=("pst", "s", 1),
                           avs=t1f[0:65, :], tag="s")
            else:
                res = dict(res)
                res["tag"] = h
            tg = res["tag"]
            items = []
            for c in range(4):
                jstop = min(15, (c * 512 + 511) // 128) if causal else 15
                jlist = list(range(jstop + 1))
                for j in jlist:
                    def item(j=j, c=c, hb=hb, h=h, res=res, tg=tg,
                             first=(j == 0), last=(j == jlist[-1])):
                        pj = j % 2
                        bank = res["pst"][pj]
                        sc0 = c * 512
                        wl("tensor", ("pst", tg, pj))
                        if bound_cols is None:
                            wl("tensor", "mrow_ready")
                            P.op("tensor", "matmul", bank,
                                 Kb[hb:hb + 64, j * 128:(j + 1) * 128],
                                 Qb[hb:hb + 64, sc0:sc0 + 512],
                                 start=True, stop=False)
                            P.op("tensor", "matmul", bank,
                                 self.sb["c_ones_row_b"][0:1, :],
                                 mrow[0:1, sc0:sc0 + 512],
                                 start=False, stop=True, inc=self.MM)
                            lu["mrow_free"] = (self.MM, self.MM.n)
                        else:
                            P.op("tensor", "matmul", bank,
                                 Kb[hb:hb + 64, j * 128:(j + 1) * 128],
                                 Qb[hb:hb + 64, sc0:sc0 + 512],
                                 start=True, stop=True, inc=self.MM)
                        mmc = self.MM.n
                        nev = None
                        if causal and sc0 <= j * 128 < sc0 + 512:
                            off = j * 128 - sc0
                            P.wait("vector", self.MM, mmc)
                            if off > 0:
                                P.op("vector", "tensor_scalar_add",
                                     bank[:, 0:off], bank[:, 0:off],
                                     -1e30, inc=self.EV)
                            P.op("vector", "tensor_add",
                                 bank[:, off:off + 128],
                                 bank[:, off:off + 128], mask_ts[:],
                                 inc=self.EV)
                            nev = self.EV.n
                        P.wait("scalar", self.MM, mmc)
                        if nev:
                            P.wait("scalar", self.EV, nev)
                        wl("scalar", ("ptb", tg, pj))
                        ptb_ap = res["ptb"][pj]
                        if bound_cols is None:
                            P.op("scalar", "activation", ptb_ap, bank,
                                 AF.Exp, scale=0.125, inc=self.AC)
                        else:
                            P.op("scalar", "activation", ptb_ap, bank,
                                 AF.Exp, scale=0.125, bias=bound_cols[h],
                                 inc=self.AC)
                        lu[("pst", tg, pj)] = (self.AC, self.AC.n)
                        acc = self.AC.n

                        def av_part(acc=acc, ptb_ap=ptb_ap, j=j, c=c, h=h,
                                    res=res, tg=tg, first=first, last=last):
                            P.wait("tensor", self.AC, acc)
                            if first:
                                wl("tensor", ("prav", tg))
                            P.op("tensor", "matmul", res["av"],
                                 Vb2[:, j * 130 + h * 65:
                                     j * 130 + h * 65 + 65],
                                 ptb_ap, start=first, stop=last,
                                 inc=self.MM if last else None)
                            lu[("ptb", tg, pj)] = (self.MM, self.MM.n)
                            if last:
                                self._finish_chunk(h, c, res, tg)
                        return av_part
                    items.append(item)
            return items

        def finish_chunk(h, c, res, tg):
            avs = res["avs"]
            mmc = self.MM.n
            P.wait("vector", self.MM, mmc)
            wl("vector", ("avsfree", tg))
            P.op("vector", "tensor_copy", avs, res["av"], inc=self.EV)
            lu[("prav", tg)] = (self.EV, self.EV.n)
            P.wait("scalar", self.EV, self.EV.n)
            wl("scalar", "rowd")
            P.op("scalar", "activation", avs[64:65, :], avs[64:65, :],
                 AF.Ln, inc=self.AC)
            P.wait("scalar", self.AC, self.AC.n)
            P.op("scalar", "activation", self.sb["rowd"][0:1, 0:512],
                 avs[64:65, :], AF.Exp, scale=-1.0, inc=self.AC)
            rc = self.AC.n
            P.wait("tensor", self.AC, rc)
            if res["repk"] is not None:
                wl("tensor", res["repk"])
            else:
                wl("tensor", ("repfree", tg))
            P.op("tensor", "matmul", res["rep"],
                 self.sb["c_ones_row_b"][0:1, :],
                 self.sb["rowd"][0:1, 0:512], start=True, stop=True,
                 inc=self.MM)
            lu["rowd"] = (self.MM, self.MM.n)
            repc = self.MM.n
            P.wait("vector", self.MM, repc)
            dst = out_fn(h, c)
            P.op("vector", "tensor_mul", dst, avs[0:64, :],
                 res["rep"][0:64, :], inc=self.EV)
            lu[("pst", tg, 1)] = (self.EV, self.EV.n)
            lu[("repfree", tg)] = (self.EV, self.EV.n)
            lu[("avsfree", tg)] = (self.EV, self.EV.n)
            if post_fn is not None:
                post_fn(h, c)
        self._finish_chunk = finish_chunk

        if bound_cols is not None:
            res_h = {
                0: dict(pst=(self.pB[:, 0:512], self.pB[:, 512:1024]),
                        av=self.pR[0:65, 0:512],
                        ptb=(self._attn_ptb[:, 0:512],
                             self._attn_ptb[:, 512:1024]),
                        rep=self.pB[:, 512:1024], repk=("pst", 0, 1),
                        avs=t1f[0:65, :]),
                1: dict(pst=(self.pA[:, 0:512], self.pA[:, 512:1024]),
                        av=self.pA[0:65, 1024:1536],
                        ptb=(self._attn_ptb[:, 1024:1536],
                             self._attn_ptb[:, 1536:2048]),
                        rep=self.pA[:, 1536:2048], repk=None,
                        avs=self.sb["wtmp"][0:65, 0:512]),
            }
            self._attn_res = res_h
            b0 = passB_items(0)
            b1 = passB_items(1)
            n = max(len(b0), len(b1))
            for i in range(n):
                avs_ = []
                if i < len(b0):
                    avs_.append(b0[i]())
                if i < len(b1):
                    avs_.append(b1[i]())
                for a_ in avs_:
                    a_()
            self._attn_res = None
        else:
            for it in passA_items(0):
                it()
            emit_mrows(0)
            b0 = passB_items(0)
            a1 = passA_items(1)
            k = 0
            prev_av = None
            for i, it in enumerate(b0):
                av_ = it()
                if i % 2 == 1 and k < len(a1):
                    a1[k]()
                    k += 1
                if prev_av is not None:
                    prev_av()
                prev_av = av_
            if prev_av is not None:
                prev_av()
            while k < len(a1):
                a1[k]()
                k += 1
            emit_mrows(1)
            prev_av = None
            for it in passB_items(1):
                av_ = it()
                if prev_av is not None:
                    prev_av()
                prev_av = av_
            if prev_av is not None:
                prev_av()

    def slot2(self):
        P = self.P
        al = self.arena_alloc()
        wqb = [al.take(3072), al.take(3072)]
        Qb = al.take(2048)
        Kb = al.take(2048)
        Vb2 = al.take(16 * 130)
        sqs = al.take(2048)
        ptb = al.take(1024)
        avT = self.hbuf[:, 0:DX * 2048]
        wpT = self.hbuf[:, DX * 2048:DX * 2048 + 8192]
        fc = self.sb["fcols"]
        m_all = fc[:, 0:16]
        den_all = fc[:, 16:32]
        g_all = fc[:, 32:48]
        nscal = fc[:, 48:49]
        bq_cols = fc[:, 50:58]
        bk_cols = fc[:, 58:66]
        bp_cols = fc[:, 66:74]
        qn_col = fc[:, 74:75]
        kn_col = fc[:, 75:76]
        rows2 = self.sb["rows2"]
        rowb = self.sb["rowb"]
        blockones = self.sb["c_blockones"]
        blockonesT = self.sb["c_blockonesT"]
        ones1f = self.sb["c_ones_row"]
        n8i = self.sb["c_neg8ident"]
        eps6 = self.sb["c_eps"][0:1, 1:2]
        eps6r = self.sb["c_eps"][0:2, 1:2]

        self.load_bias_cols(bq_cols, self.io["L1_attn_bqkv"][:, 0:1024], 8)
        self.load_bias_cols(bk_cols, self.io["L1_attn_bqkv"][:, 1024:2048], 8)
        self.load_bias_cols(bp_cols, self.io["L1_attn_bproj"], 8)
        self.col64_to_128(qn_col, self.io["L1_attn_qn_w"])
        self.col64_to_128(kn_col, self.io["L1_attn_kn_w"])
        self.prep_transposed(wpT, 0, self.io["L1_attn_Wproj"], D, D)
        P.wait("vector", self.LD)
        for tt in range(16):
            for hh in range(2):
                P.op("vector", "tensor_copy",
                     Vb2[:, tt * 130 + hh * 65 + 64:tt * 130 + hh * 65 + 65],
                     self.sb["c_ones_col_b"][:],
                     inc=self.EV if (tt == 15 and hh == 1) else None)
        self.barrier()

        lastu = {}
        for b in range(B):
            bc = b * T
            for p in range(NH // 2):
                par = p % 2
                wq = wqb[par][:, 0:1024]
                wk = wqb[par][:, 1024:2048]
                wv = wqb[par][:, 2048:3072]
                if b == 0:
                    P.wait("vector", self.MM, lastu.get(("wq", par), 0))
                    self.prep_transposed(wq, 0, self.io["L1_attn_Wqkv"],
                                         128, D, row0=p * 128, dst_stride=128)
                    self.prep_transposed(wk, 0, self.io["L1_attn_Wqkv"],
                                         128, D, row0=1024 + p * 128,
                                         dst_stride=128)
                    self.prep_transposed(wv, 0, self.io["L1_attn_Wqkv"],
                                         128, D, row0=2048 + p * 128,
                                         dst_stride=128)
                    if p == 0:
                        scr_qkv = self.dram_scratch("scr_qkv",
                                                    (NH // 2, 128, 3072))
                        self.scr_qkv = scr_qkv
                    P.wait("sync", self.EV)
                    P.dma("sync", self.scr_qkv[p], wqb[par][:, 0:3072],
                          self.ST)
                else:
                    P.wait("sync", self.MM, lastu.get(("wq", par), 0))
                    P.dma("sync", wqb[par][:, 0:3072], self.scr_qkv[p],
                          self.LS)
                for (wtile, dstb, bcols) in ((wq, Qb, bq_cols),
                                             (wk, Kb, bk_cols)):
                    P.wait("tensor", self.LS)
                    P.wait("tensor", self.ST)
                    P.wait("tensor", self.EV)
                    for k in range(DX):
                        for n in range(4):
                            P.op("tensor", "matmul",
                                 self.pA[:, n * 512:(n + 1) * 512],
                                 wtile[:, k * 128:(k + 1) * 128],
                                 self.resid[:, k * NT + bc + n * 512:
                                            k * NT + bc + n * 512 + 512],
                                 start=(k == 0), stop=(k == DX - 1),
                                 inc=self.MM if (k == DX - 1 and n == 3)
                                 else None)
                    lastu[("wq", par)] = self.MM.n
                    P.wait("vector", self.MM)
                    P.op("vector", "tensor_scalar_add", dstb[:, 0:2048],
                         self.pA[:, 0:2048], bcols[:, p:p + 1], inc=self.EV)
                P.wait("gpsimd", self.MM, lastu.get("rowb", 0))
                P.dma("gpsimd", rowb[0:1, 0:128],
                      self.io["L1_attn_bqkv"][:, 2048 + p * 128:
                                              2048 + (p + 1) * 128], self.LD)
                P.wait("tensor", self.EV)
                P.wait("tensor", self.LD)
                for tt in range(16):
                    pv = self.pB[:, (tt % 8) * 128:(tt % 8 + 1) * 128]
                    if tt % 8 == 0:
                        P.wait("tensor", self.EV, lastu.get("pv", 0))
                    for k in range(DX):
                        P.op("tensor", "matmul", pv,
                             self.resid[:, k * NT + bc + tt * 128:
                                        k * NT + bc + (tt + 1) * 128],
                             wv[:, k * 128:(k + 1) * 128],
                             start=(k == 0), stop=False)
                    P.op("tensor", "matmul", pv,
                         self.sb["c_ones_row_b"][0:1, :],
                         rowb[0:1, 0:128],
                         start=False, stop=True,
                         inc=self.MM if tt % 8 == 7 else None)
                    if tt == 15:
                        lastu["rowb"] = self.MM.n
                    if tt % 8 == 7:
                        P.wait("vector", self.MM)
                        t0c = tt - 7
                        P.op("vector", "tensor_copy",
                             Vb2[:, t0c * 130:(tt + 1) * 130].rearrange(
                                 "p (t c) -> p t c", t=8)[:, :, 0:64],
                             self.pB[:, 0:1024].rearrange(
                                 "p (t c) -> p t c", t=8)[:, :, 0:64],
                             inc=self.EV)
                        P.op("vector", "tensor_copy",
                             Vb2[:, t0c * 130:(tt + 1) * 130].rearrange(
                                 "p (t c) -> p t c", t=8)[:, :, 65:129],
                             self.pB[:, 0:1024].rearrange(
                                 "p (t c) -> p t c", t=8)[:, :, 64:128],
                             inc=self.EV)
                        lastu["pv"] = self.EV.n
                lastu[("wq", par)] = self.MM.n
                for (buf, ncol) in ((Qb, qn_col), (Kb, kn_col)):
                    P.wait("scalar", self.EV)
                    P.op("scalar", "activation", sqs[:, 0:2048], buf[:, 0:2048],
                         AF.Square, inc=self.AC)
                    P.wait("tensor", self.AC)
                    P.wait("tensor", self.EV)
                    for n in range(4):
                        P.op("tensor", "matmul",
                             self.pA[0:2, n * 512:(n + 1) * 512], blockones[:],
                             sqs[:, n * 512:(n + 1) * 512],
                             start=True, stop=True,
                             inc=self.MM if n == 3 else None)
                    P.wait("vector", self.MM)
                    P.op("vector", "tensor_scalar_mul", rows2[0:2, 0:2048],
                         self.pA[0:2, 0:2048], 1.0 / HD, inc=self.EV)
                    P.wait("scalar", self.EV)
                    P.op("scalar", "activation", rows2[0:2, 0:2048],
                         rows2[0:2, 0:2048], AF.Ln, bias=eps6r, inc=self.AC)
                    P.wait("scalar", self.AC, self.AC.n)
                    P.op("scalar", "activation", rows2[0:2, 0:2048],
                         rows2[0:2, 0:2048], AF.Exp, scale=-0.5, inc=self.AC)
                    P.wait("tensor", self.AC)
                    for n in range(4):
                        P.op("tensor", "matmul",
                             self.pA[:, n * 512:(n + 1) * 512],
                             blockonesT[0:2, :], rows2[0:2, n * 512:
                                                       (n + 1) * 512],
                             start=True, stop=True,
                             inc=self.MM if n == 3 else None)
                    P.wait("vector", self.MM)
                    P.op("vector", "scalar_tensor_tensor", buf[:, 0:2048],
                         buf[:, 0:2048], ncol, self.pA[:, 0:2048],
                         ALU.mult, ALU.mult, inc=self.EV)
                self.barrier()
                self._attn_ptb = ptb
                self.attn_heads(
                    Qb, Kb, Vb2, causal=False, mrow=sqs[0:1, 0:2048],
                    out_fn=lambda h, c, p=p: avT[64 * h:64 * h + 64,
                                                 p * 2048 + c * 512:
                                                 p * 2048 + c * 512 + 512])
                self.barrier()
            for n in range(4):
                for m in range(DX):
                    par = m % 2
                    pc = self.pB[:, par * 512:par * 512 + 512]
                    P.wait("tensor", self.EV, lastu.get(("pprj", par), 0))
                    for k in range(DX):
                        P.op("tensor", "matmul", pc,
                             wpT[:, k * 1024 + m * 128:k * 1024 + m * 128
                                 + 128],
                             avT[:, k * 2048 + n * 512:k * 2048 + n * 512
                                 + 512],
                             start=(k == 0), stop=(k == DX - 1),
                             inc=self.MM if k == DX - 1 else None)
                    P.wait("vector", self.MM)
                    P.op("vector", "scalar_tensor_tensor",
                         self.sb["lnsum"][:, m * 512:(m + 1) * 512], pc,
                         bp_cols[:, m:m + 1],
                         self.resid[:, m * NT + bc + n * 512:
                                    m * NT + bc + n * 512 + 512],
                         ALU.add, ALU.add, inc=self.EV)
                    lastu[("pprj", par)] = self.EV.n
                self.barrier()
                self.postln_chunk(bc + n * 512, None, presummed=True)
                self.barrier()


    def slot3(self):
        P = self.P
        al = self.arena_alloc()
        wqb = [al.take(3072)]
        Qb = al.take(2048)
        Kb = al.take(2048)
        Vb2 = al.take(16 * 130)
        shf = al.take(2048)
        sqs = shf
        ptb = al.take(2048)
        ropc = al.take(2048)
        rops = al.take(2048)
        avstg = ptb
        woT = self.arena[:, 0:8192]
        projstg = self.arena[:, 8192:12288]
        av_scr = self.dram_scratch("scr_av", (DX, 128, 2048))
        h_scr = self.dram_scratch("scr_h", (DX, 128, NT))
        fc = self.sb["fcols"]
        m_all = fc[:, 0:16]
        den_all = fc[:, 16:32]
        g_all = fc[:, 32:48]
        nscal = fc[:, 48:49]
        wn1c = fc[:, 49:57]
        wn2c = fc[:, 57:65]
        rstd_tt = fc[:, 65:97]
        rows2 = self.sb["rows2"]
        rows = self.sb["rows"]
        t1f = self.sb["t1f"]
        ones1f = self.sb["c_ones_row"]
        onescb = self.sb["c_ones_col_b"]
        n8i = self.sb["c_neg8ident"]
        identf = self.sb["c_ident_f"]
        mask_st = self.sb["c_mask_st"]
        mask_ts = self.sb["c_mask_ts"]
        iota8 = self.sb["c_iota8"]
        eps6 = self.sb["c_eps"][0:1, 1:2]
        rowbuf = self.sb["rowbuf"]

        P.dma("gpsimd", ropc, self.io["c_rope_cos"], self.LD)
        P.dma("gpsimd", rops, self.io["c_rope_sin"], self.LD)
        self.load_bias_cols(wn1c, self.io["L1_ffn_attn_norm_w"], DX)
        self.load_bias_cols(wn2c, self.io["L1_ffn_moe_norm_w"], DX)
        scr_a = self.dram_scratch("scr_l1a", (NH // 2, 128, 3072))
        self.barrier()
        lastu = {}

        for b in range(B):
            bc = b * T
            for n in range(4):
                P.wait("scalar", self.EV)
                P.wait("scalar", self.MM)
                for dXi in range(DX):
                    P.op("scalar", "activation", sqs[:, 0:512],
                         self.resid[:, dXi * NT + bc + n * 512:
                                    dXi * NT + bc + n * 512 + 512],
                         AF.Square, inc=self.AC)
                    P.wait("tensor", self.AC)
                    P.op("tensor", "matmul", self.pR[0:1, 0:512], onescb[:],
                         sqs[:, 0:512], start=(dXi == 0), stop=(dXi == DX - 1),
                         inc=self.MM)
                    if dXi < DX - 1:
                        P.wait("scalar", self.MM)
                P.wait("vector", self.MM)
                P.op("vector", "tensor_scalar_mul", rows[0:1, 0:512],
                     self.pR[0:1, 0:512], 1.0 / D, inc=self.EV)
                P.wait("scalar", self.EV)
                P.op("scalar", "activation", rows[0:1, 0:512],
                     rows[0:1, 0:512], AF.Sqrt, bias=eps6, inc=self.AC)
                P.wait("vector", self.AC)
                P.op("vector", "reciprocal", rows[0:1, 0:512],
                     rows[0:1, 0:512], inc=self.EV)
                P.wait("tensor", self.EV)
                for i in range(4):
                    P.op("tensor", "matmul", self.pR[:, 4 + i:5 + i],
                         rows[0:1, i * 128:(i + 1) * 128], ones1f[0:1, 0:1],
                         start=True, stop=True,
                         inc=self.MM if i == 3 else None)
                P.wait("vector", self.MM)
                P.op("vector", "tensor_copy",
                     rstd_tt[:, b * 16 + n * 4:b * 16 + n * 4 + 4],
                     self.pR[:, 4:8], inc=self.EV)
                P.wait("tensor", self.EV)
            P.wait("vector", self.LD)
            for tt in range(16):
                for hh in range(2):
                    P.op("vector", "tensor_copy",
                         Vb2[:, tt * 130 + hh * 65 + 64:
                             tt * 130 + hh * 65 + 65],
                         self.sb["c_ones_col_b"][:],
                         inc=self.EV if (tt == 15 and hh == 1) else None)
            self.barrier()

            for p in range(NH // 2):
                par = 0
                wq = wqb[par][:, 0:1024]
                wk = wqb[par][:, 1024:2048]
                wv = wqb[par][:, 2048:3072]
                if b == 0:
                    P.wait("vector", self.MM, lastu.get(("w3a", par), 0))
                    for (wname, dst) in (("L1_ffn_wq", wq), ("L1_ffn_wk", wk),
                                         ("L1_ffn_wv", wv)):
                        if wname == "L1_ffn_wv":
                            self.prep_transposed(dst, 0, self.io[wname],
                                                 128, D, row0=p * 128,
                                                 dst_stride=128,
                                                 scale_col=lambda kX:
                                                 wn1c[:, kX:kX + 1])
                        else:
                            for blk in range(8):
                                hh = blk // 4
                                sub = ROPE_PERM[(blk % 4) * 16:
                                                (blk % 4) * 16 + 16]
                                r0 = int(sub[0])
                                self.prep_16rows(dst, blk * 16,
                                                 self.io[wname],
                                                 p * 128 + hh * 64 + r0,
                                                 wn1c)
                    P.wait("sync", self.EV)
                    P.dma("sync", scr_a[p], wqb[par][:, 0:3072], self.ST)
                else:
                    P.wait("sync", self.MM, lastu.get(("w3a", par), 0))
                    P.dma("sync", wqb[par][:, 0:3072], scr_a[p], self.LS)
                for (wtile, dstb) in ((wq, Qb), (wk, Kb)):
                    P.wait("tensor", self.LS)
                    P.wait("tensor", self.ST)
                    P.wait("tensor", self.EV)
                    for k in range(DX):
                        for n in range(4):
                            P.op("tensor", "matmul",
                                 self.pA[:, n * 512:(n + 1) * 512],
                                 wtile[:, k * 128:(k + 1) * 128],
                                 self.resid[:, k * NT + bc + n * 512:
                                            k * NT + bc + n * 512 + 512],
                                 start=(k == 0), stop=(k == DX - 1),
                                 inc=self.MM if (k == DX - 1 and n == 3)
                                 else None)
                    P.wait("vector", self.MM)
                    P.op("vector", "tensor_copy", dstb[:, 0:2048],
                         self.pA[:, 0:2048], inc=self.EV)
                for n in range(4):
                    P.wait("tensor", self.EV)
                    for i in range(4):
                        P.op("tensor", "matmul",
                             self.pR[0:1, i * 128:(i + 1) * 128],
                             rstd_tt[:, b * 16 + n * 4 + i:
                                     b * 16 + n * 4 + i + 1], identf[:],
                             start=True, stop=True,
                             inc=self.MM if i == 3 else None)
                    P.wait("vector", self.MM)
                    P.op("vector", "tensor_copy", rows[0:1, 512:1024],
                         self.pR[0:1, 0:512], inc=self.EV)
                    P.wait("tensor", self.EV)
                    P.op("tensor", "matmul",
                         self.pA[:, n * 512:(n + 1) * 512], ones1f[0:1, :],
                         rows[0:1, 512:1024],
                         start=True, stop=True,
                         inc=self.MM if n == 3 else None)
                P.wait("vector", self.MM)
                for dstb in (Qb, Kb):
                    P.op("vector", "tensor_mul", dstb[:, 0:2048],
                         dstb[:, 0:2048], self.pA[:, 0:2048], inc=self.EV)
                for dstb in (Qb, Kb):
                    P.wait("vector", self.EV)
                    P.op("vector", "stream_shuffle", shf[:, 0:2048],
                         dstb[:, 0:2048], ROPE_SHUF_MASK, inc=self.EV)
                    P.wait("vector", self.EV)
                    P.op("vector", "tensor_mul", shf[:, 0:2048], shf[:, 0:2048],
                         rops, inc=self.EV)
                    P.wait("vector", self.EV)
                    P.op("vector", "tensor_mul", dstb[:, 0:2048],
                         dstb[:, 0:2048], ropc, inc=self.EV)
                    P.wait("vector", self.EV)
                    P.op("vector", "tensor_add", dstb[:, 0:2048],
                         dstb[:, 0:2048], shf[:, 0:2048], inc=self.EV)
                P.wait("tensor", self.EV)
                for tt in range(16):
                    pv = self.pB[:, (tt % 8) * 128:(tt % 8 + 1) * 128]
                    if tt % 8 == 0:
                        P.wait("tensor", self.EV, lastu.get("pv3", 0))
                    for k in range(DX):
                        P.op("tensor", "matmul", pv,
                             self.resid[:, k * NT + bc + tt * 128:
                                        k * NT + bc + (tt + 1) * 128],
                             wv[:, k * 128:(k + 1) * 128],
                             start=(k == 0), stop=(k == DX - 1),
                             inc=self.MM if (k == DX - 1 and tt % 8 == 7)
                             else None)
                    if tt % 8 == 7:
                        P.wait("vector", self.MM)
                        for t2 in range(tt - 7, tt + 1):
                            P.op("vector", "tensor_scalar_mul",
                                 Vb2[:, t2 * 130:t2 * 130 + 64],
                                 self.pB[:, (t2 % 8) * 128:
                                         (t2 % 8) * 128 + 64],
                                 rstd_tt[:, b * 16 + t2:b * 16 + t2 + 1])
                            P.op("vector", "tensor_scalar_mul",
                                 Vb2[:, t2 * 130 + 65:t2 * 130 + 129],
                                 self.pB[:, (t2 % 8) * 128 + 64:
                                         (t2 % 8) * 128 + 128],
                                 rstd_tt[:, b * 16 + t2:b * 16 + t2 + 1],
                                 inc=self.EV if t2 == tt else None)
                        lastu["pv3"] = self.EV.n
                lastu[("w3a", par)] = self.MM.n
                self.barrier()
                fcb = self.sb["fcols"]
                rows_ = self.sb["rows"]
                blockones = self.sb["c_blockones"]
                for (buf, dst) in ((Qb, fcb[0:2, 22:23]), (Kb, fcb[0:2, 23:24])):
                    P.wait("scalar", self.EV)
                    P.wait("scalar", self.MM)
                    P.op("scalar", "activation", shf[:, 0:2048], buf[:, 0:2048],
                         AF.Square, inc=self.AC)
                    P.wait("tensor", self.AC)
                    P.wait("tensor", self.EV)
                    for n in range(4):
                        P.op("tensor", "matmul",
                             self.pA[0:2, n * 512:(n + 1) * 512], blockones[:],
                             shf[:, n * 512:(n + 1) * 512],
                             start=True, stop=True,
                             inc=self.MM if n == 3 else None)
                    P.wait("vector", self.MM)
                    P.op("vector", "reduce_max", dst, self.pA[0:2, 0:2048],
                         axis=AX.X, inc=self.EV)
                P.wait("vector", self.EV, self.EV.n)
                P.op("vector", "tensor_mul", fcb[0:2, 24:25], fcb[0:2, 22:23],
                     fcb[0:2, 23:24], inc=self.EV)
                P.wait("scalar", self.EV)
                P.op("scalar", "activation", fcb[0:2, 24:25], fcb[0:2, 24:25],
                     AF.Ln, inc=self.AC)
                P.wait("scalar", self.AC, self.AC.n)
                P.op("scalar", "activation", fcb[0:2, 24:25], fcb[0:2, 24:25],
                     AF.Exp, scale=0.5, inc=self.AC)
                P.wait("vector", self.AC)
                P.op("vector", "tensor_scalar_mul", fcb[0:2, 24:25],
                     fcb[0:2, 24:25], -0.125, inc=self.EV)
                P.wait("tensor", self.EV)
                P.op("tensor", "matmul", self.pR[0:1, 0:2], fcb[0:2, 24:25],
                     self.sb["c_ident_f"][0:2, 0:2], start=True, stop=True,
                     inc=self.MM)
                P.wait("vector", self.MM)
                P.op("vector", "tensor_copy", rows_[0:1, 0:2],
                     self.pR[0:1, 0:2], inc=self.EV)
                P.wait("tensor", self.EV)
                for hh in range(2):
                    P.op("tensor", "matmul", self.pR[:, 4 + hh:5 + hh],
                         self.sb["c_ones_row"][0:1, :],
                         rows_[0:1, hh:hh + 1], start=True, stop=True,
                         inc=self.MM if hh == 1 else None)
                P.wait("vector", self.MM)
                P.op("vector", "tensor_copy", fcb[:, 25:27], self.pR[:, 4:6],
                     inc=self.EV)
                P.wait("tensor", self.EV)
                P.wait("scalar", self.EV)
                self._attn_ptb = ptb

                def s3_out(h, c, p=p):
                    return shf[64 * h:64 * h + 64, c * 512:(c + 1) * 512]

                def s3_post(h, c, p=p, b=b):
                    lu = self._attn_lu
                    P.wait("sync", self.EV, self.EV.n)
                    P.dma("sync", av_scr[p, 64 * h:64 * h + 64,
                                         c * 512:(c + 1) * 512],
                          shf[64 * h:64 * h + 64, c * 512:(c + 1) * 512],
                          self.ST)
                    lu[("avsfree", h)] = (self.ST, self.ST.n)

                self.attn_heads(Qb, Kb, Vb2, causal=True, mrow=None,
                                out_fn=s3_out, post_fn=s3_post,
                                bound_cols=(fcb[:, 25:26], fcb[:, 26:27]))
                self.barrier()
            self.prep_transposed(woT, 0, self.io["L1_ffn_wo"], D, D)
            self.barrier()
            for n in range(4):
                P.wait("sync", self.MM, lastu.get("projstg", 0))
                for k in range(DX):
                    P.dma("sync", projstg[:, k * 512:(k + 1) * 512],
                          av_scr[k][:, n * 512:(n + 1) * 512], self.LS)
                for m in range(DX):
                    par = m % 2
                    pc = self.pB[:, par * 512:par * 512 + 512]
                    P.wait("tensor", self.LS)
                    P.wait("tensor", self.EV, lastu.get(("pprj3", par), 0))
                    for k in range(DX):
                        P.op("tensor", "matmul", pc,
                             woT[:, k * 1024 + m * 128:k * 1024 + m * 128
                                 + 128],
                             projstg[:, k * 512:(k + 1) * 512],
                             start=(k == 0), stop=(k == DX - 1),
                             inc=self.MM if k == DX - 1 else None)
                    if m == DX - 1:
                        lastu["projstg"] = self.MM.n
                    P.wait("vector", self.MM)
                    P.wait("vector", self.ST)
                    P.op("vector", "tensor_add",
                         self.sb["lnsum"][:, m * 512:(m + 1) * 512], pc,
                         self.resid[:, m * NT + bc + n * 512:
                                    m * NT + bc + n * 512 + 512],
                         inc=self.EV)
                    lastu[("pprj3", par)] = self.EV.n
                    P.wait("sync", self.EV)
                    P.dma("sync", h_scr[m][:, bc + n * 512:
                                           bc + n * 512 + 512],
                          self.sb["lnsum"][:, m * 512:(m + 1) * 512],
                          self.ST)
                    lastu[("hstg", m)] = self.ST.n
            self.barrier()

        al2 = self.arena_alloc()
        hid = self.hbuf[:, 0:HX * 512]
        tn2 = self.hbuf[:, HX * 512:HX * 512 + 4096]
        wb3 = [self.hbuf[:, HX * 512 + 4096:HX * 512 + 8192],
               self.hbuf[:, HX * 512 + 8192:HX * 512 + 12288]]
        hgb = self.hbuf[:, HX * 512 + 12288:HX * 512 + 16384]
        gateT = al2.take(64)
        b2row = al2.take(1024)[0:1, :]
        b1a = fc[:, 256:512]
        logt = fc[:, 129:161]
        top2 = fc[:, 161:169]
        fidx = fc[:, 169:171]
        rw12 = fc[:, 171:173]
        msk = fc[:, 173:181]
        w8s = fc[:, 181:213]
        idxu = self.sbuf("idxu", [128, 8], U32)
        scr_m1 = self.dram_scratch("scr_m1", (NE, HX, 128, 1024))
        scr_m2 = self.dram_scratch("scr_m2", (NE, DX, 128, 4096))
        wtmp = self.sb["wtmp"]
        identb = self.sb["c_ident_b"]
        P.wait("vector", self.ST)
        P.wait("gpsimd", self.MM)
        P.dma("gpsimd", wtmp[0:NE, 0:D], self.io["L1_ffn_gate_w"], self.LD)
        P.wait("tensor", self.LD)
        P.wait("tensor", self.EV)
        for kX in range(DX):
            P.op("tensor", "transpose", self.pT[:, kX * 8:(kX + 1) * 8],
                 wtmp[0:NE, kX * 128:(kX + 1) * 128], identb[0:NE, 0:NE],
                 inc=self.MM if kX == DX - 1 else None)
        P.wait("vector", self.MM)
        for kX in range(DX):
            P.op("vector", "tensor_scalar_mul", gateT[:, kX * 8:(kX + 1) * 8],
                 self.pT[:, kX * 8:(kX + 1) * 8], wn2c[:, kX:kX + 1],
                 inc=self.EV if kX == DX - 1 else None)
        for e in range(NE):
            self.load_bias_cols(b1a[:, e * 32:(e + 1) * 32],
                                self.io["L1_ffn_b1"][e:e + 1, :], 32)
        self.barrier()

        for g in range(NT // 512):
            gc = g * 512
            P.wait("sync", self.EV, lastu.get("hgb", 0))
            P.wait("sync", self.AC, lastu.get("hgb_a", 0))
            for dXi in range(DX):
                P.dma("sync", hgb[:, dXi * 512:(dXi + 1) * 512],
                      h_scr[dXi][:, gc:gc + 512], self.LS)
            P.wait("scalar", self.LS)
            P.wait("vector", self.LS)
            for dXi in range(DX):
                P.wait("scalar", self.EV)
                P.wait("scalar", self.MM)
                P.op("scalar", "activation", sqs[:, 0:512],
                     hgb[:, dXi * 512:(dXi + 1) * 512],
                     AF.Square, inc=self.AC)
                P.wait("tensor", self.AC)
                P.op("tensor", "matmul", self.pR[0:1, 0:512], onescb[:],
                     sqs[:, 0:512], start=(dXi == 0), stop=(dXi == DX - 1),
                     inc=self.MM)
            P.wait("vector", self.MM)
            P.op("vector", "tensor_scalar_mul", rows[0:1, 0:512],
                 self.pR[0:1, 0:512], 1.0 / D, inc=self.EV)
            P.wait("scalar", self.EV)
            P.op("scalar", "activation", rows[0:1, 0:512], rows[0:1, 0:512],
                 AF.Sqrt, bias=eps6, inc=self.AC)
            P.wait("vector", self.AC)
            P.op("vector", "reciprocal", rows[0:1, 0:512], rows[0:1, 0:512],
                 inc=self.EV)
            P.wait("tensor", self.EV)
            P.op("tensor", "matmul", self.pA[:, 0:512], ones1f[0:1, :],
                 rows[0:1, 0:512], start=True, stop=True, inc=self.MM)
            P.wait("vector", self.MM)
            for dXi in range(DX):
                P.op("vector", "tensor_mul",
                     tn2[:, dXi * 512:(dXi + 1) * 512],
                     hgb[:, dXi * 512:(dXi + 1) * 512],
                     self.pA[:, 0:512], inc=self.EV)
            P.wait("tensor", self.EV)
            for k in range(DX):
                P.op("tensor", "matmul", self.pB[0:NE, 0:512],
                     gateT[:, k * NE:(k + 1) * NE],
                     tn2[:, k * 512:(k + 1) * 512],
                     start=(k == 0), stop=(k == DX - 1), inc=self.MM
                     if k == DX - 1 else None)
            P.wait("vector", self.MM)
            P.op("vector", "tensor_copy", t1f[0:NE, 0:512],
                 self.pB[0:NE, 0:512], inc=self.EV)
            P.wait("tensor", self.EV)
            for tt in range(4):
                P.op("tensor", "transpose", self.pA[:, tt * 8:(tt + 1) * 8],
                     t1f[0:NE, tt * 128:(tt + 1) * 128], identf[0:NE, 0:NE],
                     inc=self.MM if tt == 3 else None)
            P.wait("vector", self.MM)
            P.op("vector", "tensor_copy", logt, self.pA[:, 0:32], inc=self.EV)
            for tt in range(4):
                lg = logt[:, tt * 8:(tt + 1) * 8]
                P.wait("vector", self.EV)
                P.op("vector", "max", top2, lg, inc=self.EV)
                P.wait("vector", self.EV)
                P.op("vector", "max_index", idxu[:], top2, lg, inc=self.EV)
                P.wait("vector", self.EV)
                P.op("vector", "tensor_copy", fidx, idxu[:, 0:2], inc=self.EV)
                P.wait("vector", self.EV)
                P.op("vector", "tensor_sub", rw12[:, 0:1], top2[:, 1:2],
                     top2[:, 0:1], inc=self.EV)
                P.wait("scalar", self.EV)
                P.op("scalar", "activation", rw12[:, 0:1], rw12[:, 0:1],
                     AF.Sigmoid, scale=-1.0, inc=self.AC)
                P.wait("vector", self.AC)
                P.op("vector", "tensor_scalar", rw12[:, 1:2], rw12[:, 0:1],
                     -1.0, 1.0, ALU.mult, ALU.add, inc=self.EV)
                P.wait("vector", self.EV)
                P.op("vector", "tensor_scalar", msk, iota8[:],
                     fidx[:, 0:1], None, ALU.is_equal, inc=self.EV)
                P.wait("vector", self.EV)
                P.op("vector", "tensor_scalar_mul",
                     w8s[:, tt * 8:(tt + 1) * 8], msk, rw12[:, 0:1],
                     inc=self.EV)
                P.wait("vector", self.EV)
                P.op("vector", "tensor_scalar", msk, iota8[:],
                     fidx[:, 1:2], None, ALU.is_equal, inc=self.EV)
                P.wait("vector", self.EV)
                P.op("vector", "scalar_tensor_tensor",
                     w8s[:, tt * 8:(tt + 1) * 8], msk, rw12[:, 1:2],
                     w8s[:, tt * 8:(tt + 1) * 8], ALU.mult, ALU.add,
                     inc=self.EV)
            self.barrier()
            for e in range(NE):
                par = e % 2
                P.wait("tensor", self.EV)
                for tt in range(4):
                    P.op("tensor", "matmul",
                         self.pR[0:1, tt * 128:(tt + 1) * 128],
                         w8s[:, tt * 8 + e:tt * 8 + e + 1], identf[:],
                         start=True, stop=True,
                         inc=self.MM if tt == 3 else None)
                P.wait("vector", self.MM)
                P.op("vector", "tensor_copy", rows[0:1, 512:1024],
                     self.pR[0:1, 0:512], inc=self.EV)
                P.wait("tensor", self.EV)
                P.op("tensor", "matmul", self.pA[:, 1536:2048],
                     ones1f[0:1, :], rows[0:1, 512:1024],
                     start=True, stop=True, inc=self.MM)
                P.wait("vector", self.MM)
                P.op("vector", "tensor_copy", t1f[:], self.pA[:, 1536:2048],
                     inc=self.EV)
                P.wait("gpsimd", self.MM)
                P.dma("gpsimd", b2row,
                      self.io["L1_ffn_b2"][0:1, e * 1024:(e + 1) * 1024],
                      self.LD)
                P.wait("scalar", self.EV)
                P.wait("vector", self.EV)
                for mg in range(HX // 4):
                    wap = wb3[par][:, 0:4096]
                    if g == 0:
                        P.wait("vector", self.MM,
                               lastu.get(("wm1", par), 0))
                        for mm_ in range(4):
                            self.prep_transposed(
                                wap[:, mm_ * 1024:(mm_ + 1) * 1024], 0,
                                self.io["L1_ffn_w1"][e],
                                128, D, row0=(mg * 4 + mm_) * 128,
                                dst_stride=128,
                                scale_col=lambda kX: wn2c[:, kX:kX + 1])
                        P.wait("sync", self.EV)
                        P.dma("sync",
                              scr_m1[e, mg * 4:(mg + 1) * 4].transpose([1, 0, 2]),
                              wap.rearrange("p (m c) -> p m c", m=4), self.ST)
                    else:
                        P.wait("sync", self.MM, lastu.get(("wm1", par), 0))
                        P.dma("sync",
                              wap.rearrange("p (m c) -> p m c", m=4),
                              scr_m1[e, mg * 4:(mg + 1) * 4].transpose([1, 0, 2]),
                              self.LS)
                    P.wait("tensor", self.LS)
                    for mm_ in range(4):
                        m = mg * 4 + mm_
                        ph = self.pB[:, (m % 2) * 512:(m % 2) * 512 + 512]
                        P.wait("tensor", self.AC,
                               lastu.get(("ph3", m % 2), 0))
                        for k in range(DX):
                            P.op("tensor", "matmul", ph,
                                 wap[:, mm_ * 1024 + k * 128:
                                     mm_ * 1024 + (k + 1) * 128],
                                 tn2[:, k * 512:(k + 1) * 512],
                                 start=(k == 0), stop=(k == DX - 1),
                                 inc=self.MM if k == DX - 1 else None)
                        P.wait("scalar", self.MM)
                        P.op("scalar", "activation",
                             hid[:, m * 512:(m + 1) * 512], ph, AF.Gelu,
                             bias=b1a[:, e * 32 + m:e * 32 + m + 1],
                             inc=self.AC)
                        lastu[("ph3", m % 2)] = self.AC.n
                    lastu[("wm1", par)] = self.MM.n
                    par = (par + 1) % 2
                P.wait("tensor", self.LD)
                P.wait("tensor", self.AC)
                for dXi in range(DX):
                    wap = wb3[par][:, 0:4096]
                    if g == 0:
                        P.wait("vector", self.MM,
                               lastu.get(("wm1", par), 0))
                        self.prep_transposed(wap, 0,
                                             self.io["L1_ffn_w2"][e],
                                             128, HID, row0=dXi * 128,
                                             dst_stride=128)
                        P.wait("sync", self.EV)
                        P.dma("sync", scr_m2[e, dXi], wap, self.ST)
                    else:
                        P.wait("sync", self.MM, lastu.get(("wm1", par), 0))
                        P.dma("sync", wap, scr_m2[e, dXi], self.LS)
                    pz = self.pA[:, (dXi % 2) * 512:(dXi % 2) * 512 + 512]
                    P.wait("tensor", self.LS)
                    P.wait("tensor", self.EV, lastu.get(("pz3", dXi % 2), 0))
                    for k in range(HX):
                        P.op("tensor", "matmul", pz,
                             wap[:, k * 128:(k + 1) * 128],
                             hid[:, k * 512:(k + 1) * 512],
                             start=(k == 0), stop=False)
                    P.op("tensor", "matmul", pz,
                         b2row[0:1, dXi * 128:(dXi + 1) * 128],
                         self.sb["c_ones512b"][0:1, :],
                         start=False, stop=True, inc=self.MM)
                    lastu[("wm1", par)] = self.MM.n
                    P.wait("vector", self.MM)
                    P.op("vector", "tensor_mul",
                         self.sb["lnsum"][:, 0:512], pz, t1f[:], inc=self.EV)
                    lastu[("pz3", dXi % 2)] = self.EV.n
                    P.wait("vector", self.EV)
                    P.op("vector", "tensor_add",
                         self.resid[:, dXi * NT + gc:dXi * NT + gc + 512],
                         self.resid[:, dXi * NT + gc:dXi * NT + gc + 512],
                         self.sb["lnsum"][:, 0:512], inc=self.EV)
                    P.wait("vector", self.EV)
                    par = (par + 1) % 2
                P.wait("scalar", self.MM)
                P.wait("vector", self.MM)
                P.wait("tensor", self.EV)
            self.postln_chunk(gc, lambda dXi: hgb[:, dXi * 512:
                                                  (dXi + 1) * 512])
            lastu["hgb"] = self.EV.n
            lastu["hgb_a"] = self.AC.n
            self.barrier()

    def prep_16rows(self, dst, dst_sub0, src_ap, row0, wcol):
        P = self.P
        wtmp = self.sb["wtmp"]
        identb = self.sb["c_ident_b"]
        P.wait("vector", self.ST)
        P.wait("gpsimd", self.MM)
        P.wait("gpsimd", self.EV)
        P.dma("gpsimd", wtmp[0:16, 0:D], src_ap[row0:row0 + 16, :], self.LD)
        P.wait("tensor", self.LD)
        P.wait("tensor", self.EV)
        for kX in range(DX):
            P.op("tensor", "transpose", self.pT[:, kX * 16:(kX + 1) * 16],
                 wtmp[0:16, kX * 128:(kX + 1) * 128], identb[0:16, 0:16],
                 inc=self.MM if kX == DX - 1 else None)
        P.wait("vector", self.MM)
        for kX in range(DX):
            P.op("vector", "tensor_scalar_mul",
                 dst[:, kX * 128 + dst_sub0:kX * 128 + dst_sub0 + 16],
                 self.pT[:, kX * 16:(kX + 1) * 16], wcol[:, kX:kX + 1],
                 inc=self.EV if kX == DX - 1 else None)
        P.wait("tensor", self.EV)


def build_program(slots):
    nc = bass.Bass("TRN2", target_bir_lowering=False)
    with contextlib.ExitStack() as ctx:
        mk = MK(nc, ctx, slots)
        mk.setup()
        mk.load_resid()
        for s in slots:
            getattr(mk, f"slot{s}")()
        mk.store_out()
        block = ctx.enter_context(nc.Block())
        mk.P.run(block)
    return nc


def shard_inputs(inputs, core):
    im = {}
    im["x_in"] = np.ascontiguousarray(inputs["L0_attn_x"][2 * core:2 * core + 2])
    return im


def kernel(**inputs):
    from concourse.bass_utils import run_bass_kernel_spmd
    consts = host_constants()
    slots_chain = [[0, 1, 2, 3]]
    x = np.asarray(inputs["L0_attn_x"], dtype=np.float32)
    cur = x
    for slots in slots_chain:
        nc = build_program(slots)
        in_maps = []
        for core in range(8):
            im = {"x_in": np.ascontiguousarray(cur[2 * core:2 * core + 2])}
            for s in slots:
                for nm in SLOT_WEIGHTS[s]:
                    im[nm] = np.ascontiguousarray(
                        np.asarray(inputs[nm], dtype=np.float32)
                        .reshape(WEIGHT_SHAPES[nm]))
            cns = sorted({n for s in slots for n in CONST_NAMES[s]})
            for nm in cns:
                im[nm] = consts[nm]
            in_maps.append(im)
        res = run_bass_kernel_spmd(nc, in_maps, core_ids=list(range(8)))
        cur = np.concatenate([np.asarray(r["x_out"]) for r in res.results],
                             axis=0)
    return cur


# version 07 (final)
# speedup vs baseline: 1.1921x; 1.0196x over previous
import contextlib
import numpy as np
import concourse.bass as bass
import concourse.mybir as mybir

F32 = mybir.dt.float32
BF16 = mybir.dt.bfloat16
U32 = mybir.dt.uint32
AF = mybir.ActivationFunctionType
ALU = mybir.AluOpType
AX = mybir.AxisListType

B = 2
T = 2048
D = 1024
DX = D // 128
HID = 4096
HX = HID // 128
NH = 16
HD = 64
NE = 8
NT = B * T

ENGINES = ("tensor", "vector", "scalar", "sync", "gpsimd")

AB_COLS = 17856


class _Alloc:
    def __init__(self, arena, cols):
        self.arena = arena
        self.cols = cols
        self.off = 0

    def take(self, ncols):
        assert self.off + ncols <= self.cols, \
            f"arena overflow: {self.off}+{ncols} > {self.cols}"
        ap = self.arena[:, self.off:self.off + ncols]
        self.off += ncols
        return ap


class Sem:
    def __init__(self, h, name):
        self.h = h
        self.name = name
        self.n = 0


class Prog:

    def __init__(self, nc, ctx):
        self.nc = nc
        self.ctx = ctx
        self.q = {e: [] for e in ENGINES}
        self.sems = {}

    def sem(self, name):
        s = Sem(self.ctx.enter_context(self.nc.semaphore(name)), name)
        self.sems[name] = s
        return s

    def emit(self, eng, fn):
        self.q[eng].append(fn)

    def op(self, eng, name, *args, inc=None, incv=1, **kw):
        if inc is not None:
            inc.n += incv
        def f(e, name=name, args=args, kw=kw, inc=inc, incv=incv):
            ins = getattr(e, name)(*args, **kw)
            if inc is not None:
                ins.then_inc(inc.h, incv)
        self.emit(eng, f)

    def wait(self, eng, sem, val=None):
        v = sem.n if val is None else val
        if v <= 0:
            return
        self.emit(eng, lambda e, s=sem.h, v=v: e.wait_ge(s, v))

    def dma(self, eng, out, in_, sem, serialize=True):
        if serialize and sem.n > 0:
            self.wait(eng, sem, sem.n)
        self.op(eng, "dma_start", out=out, in_=in_, inc=sem, incv=16)

    def run(self, block):
        for eng in ENGINES:
            ops = self.q[eng]
            getattr(block, eng)(lambda e, ops=ops: [f(e) for f in ops])


ROPE_PERM = np.array(list(range(0, 16)) + list(range(32, 48))
                     + list(range(16, 32)) + list(range(48, 64)))
ROPE_SHUF_MASK = list(range(16, 32)) + list(range(16))


def host_constants():
    c = {}
    c["c_ident_f"] = np.eye(128, dtype=np.float32)
    c["c_eps"] = np.tile(np.array([[1e-5, 1e-6]], dtype=np.float32), (128, 1))
    c["c_ident_b"] = np.eye(128, dtype=np.float32)
    c["c_neg8ident"] = (-8.0 * np.eye(128)).astype(np.float32)
    c["c_ones_col"] = np.ones((128, 1), dtype=np.float32)
    c["c_ones_row"] = np.ones((1, 128), dtype=np.float32)
    c["c_ones_row512"] = np.ones((1, 512), dtype=np.float32)
    c["c_ones512b"] = np.ones((1, 512), dtype=np.float32)
    c["c_iota8"] = np.tile(np.arange(8, dtype=np.float32), (128, 1))
    tri = np.where(np.arange(128)[None, :] <= np.arange(128)[:, None],
                   np.float32(0), np.float32(-1e30))
    c["c_mask_st"] = tri.astype(np.float32)
    c["c_mask_ts"] = np.ascontiguousarray(tri.T).astype(np.float32)
    bo = np.zeros((128, 2), dtype=np.float32)
    bo[:64, 0] = 1.0
    bo[64:, 1] = 1.0
    c["c_blockones"] = bo
    c["c_blockonesT"] = np.ascontiguousarray(bo.T)
    hd = HD
    inv_freq = (1.0 / (10000.0 ** (np.arange(0, hd, 2, dtype=np.float64) / hd)))
    freqs = np.arange(T, dtype=np.float64)[:, None] * inv_freq[None, :]
    emb = np.concatenate([freqs, freqs], axis=1)
    cos = emb.T * 0 + np.cos(emb).T
    sin = np.sin(emb).T
    sign = np.where(np.arange(hd) < 32, -1.0, 1.0)
    cosP = cos[ROPE_PERM]
    sinS = (sin * sign[:, None])[ROPE_PERM]
    c["c_rope_cos"] = np.tile(cosP, (2, 1)).astype(np.float32)
    c["c_rope_sin"] = np.tile(sinS, (2, 1)).astype(np.float32)
    return c


WEIGHT_SHAPES = {
    "L0_attn_Wz": (D, D), "L0_attn_Wh": (D, D),
    "L0_attn_bz": (1, D), "L0_attn_bh": (1, D),
    "L0_ffn_w1_w": (HID, D), "L0_ffn_w1_s": (1, HID),
    "L0_ffn_w2_w": (HID, D), "L0_ffn_w2_s": (1, HID),
    "L0_ffn_h1_w": (HID, HID), "L0_ffn_h1_s": (1, HID),
    "L0_ffn_h2_w": (HID, HID), "L0_ffn_h2_s": (1, HID),
    "L0_ffn_w3_w": (D, HID), "L0_ffn_w3_s": (1, D),
    "L1_attn_Wqkv": (3 * D, D), "L1_attn_bqkv": (1, 3 * D),
    "L1_attn_Wproj": (D, D), "L1_attn_bproj": (1, D),
    "L1_attn_qn_w": (1, HD), "L1_attn_kn_w": (1, HD),
    "L1_ffn_attn_norm_w": (1, D),
    "L1_ffn_wq": (D, D), "L1_ffn_wk": (D, D),
    "L1_ffn_wv": (D, D), "L1_ffn_wo": (D, D),
    "L1_ffn_moe_norm_w": (1, D),
    "L1_ffn_gate_w": (NE, D),
    "L1_ffn_w1": (NE, HID, D), "L1_ffn_b1": (NE, HID),
    "L1_ffn_w2": (NE, D, HID), "L1_ffn_b2": (1, NE * D),
}

SLOT_WEIGHTS = {
    0: ["L0_attn_Wz", "L0_attn_Wh", "L0_attn_bz", "L0_attn_bh"],
    1: ["L0_ffn_w1_w", "L0_ffn_w1_s", "L0_ffn_w2_w", "L0_ffn_w2_s",
        "L0_ffn_h1_w", "L0_ffn_h1_s", "L0_ffn_h2_w", "L0_ffn_h2_s",
        "L0_ffn_w3_w", "L0_ffn_w3_s"],
    2: ["L1_attn_Wqkv", "L1_attn_bqkv", "L1_attn_Wproj", "L1_attn_bproj",
        "L1_attn_qn_w", "L1_attn_kn_w"],
    3: ["L1_ffn_attn_norm_w", "L1_ffn_wq", "L1_ffn_wk", "L1_ffn_wv",
        "L1_ffn_wo", "L1_ffn_moe_norm_w", "L1_ffn_gate_w",
        "L1_ffn_w1", "L1_ffn_b1", "L1_ffn_w2", "L1_ffn_b2"],
}

CONST_NAMES = {
    0: ["c_ident_f", "c_ident_b", "c_ones_col", "c_ones_row", "c_eps"],
    1: ["c_ident_f", "c_ident_b", "c_ones_col", "c_ones_row", "c_eps"],
    2: ["c_ident_f", "c_ident_b", "c_ones_col", "c_ones_row", "c_neg8ident",
        "c_blockones", "c_blockonesT", "c_eps", "c_mask_st", "c_mask_ts"],
    3: ["c_ident_f", "c_ident_b", "c_ones_col", "c_ones_row", "c_neg8ident",
        "c_blockones", "c_blockonesT", "c_iota8", "c_mask_st", "c_mask_ts",
        "c_rope_cos", "c_rope_sin", "c_eps", "c_ones512b"],
}
CONST_DTYPES = {"c_ident_b": BF16, "c_blockones": BF16, "c_ones512b": BF16}
CONST_NOSBUF = {"c_rope_cos", "c_rope_sin"}
CONST_SHAPES = {
    "c_eps": (128, 2),
    "c_ident_f": (128, 128), "c_ident_b": (128, 128), "c_neg8ident": (128, 128),
    "c_ones_col": (128, 1), "c_ones_row": (1, 128), "c_iota8": (128, 8),
    "c_ones_row512": (1, 512), "c_ones512b": (1, 512),
    "c_mask_st": (128, 128), "c_mask_ts": (128, 128),
    "c_blockones": (128, 2), "c_blockonesT": (2, 128),
    "c_rope_cos": (128, T), "c_rope_sin": (128, T),
}


class MK:
    def __init__(self, nc, ctx, slots):
        self.nc = nc
        self.ctx = ctx
        self.slots = slots
        self.P = Prog(nc, ctx)
        self.io = {}
        self.sb = {}

    def dram_in(self, name, shape):
        ap = self.nc.dram_tensor(name, list(shape), F32, kind="ExternalInput").ap()
        self.io[name] = ap
        return ap

    def dram_out(self, name, shape):
        ap = self.nc.dram_tensor(name, list(shape), F32, kind="ExternalOutput").ap()
        self.io[name] = ap
        return ap

    def dram_scratch(self, name, shape, dtype=BF16):
        ap = self.nc.dram_tensor(name, list(shape), dtype, kind="Internal").ap()
        self.io[name] = ap
        return ap

    def sbuf(self, name, shape, dtype=F32):
        h = self.ctx.enter_context(
            self.nc.sbuf_tensor("sb_" + name, list(shape), dtype))
        self.sb[name] = h
        return h

    def psum(self, name, shape, dtype=F32):
        h = self.ctx.enter_context(self.nc.psum_tensor(name, list(shape), dtype))
        return h

    def arena_alloc(self):
        return _Alloc(self.arena, AB_COLS)

    def setup(self):
        P = self.P
        self.LD = P.sem("s_ld")
        self.LS = P.sem("s_ls")
        self.MM = P.sem("s_mm")
        self.EV = P.sem("s_ev")
        self.AC = P.sem("s_ac")
        self.PO = P.sem("s_po")
        self.ST = P.sem("s_st")
        self.STG = P.sem("s_stg")
        self.TICK = P.sem("s_tick")

        self.resid = self.sbuf("resid", [128, DX * NT], BF16)
        self.arena = self.sbuf("arena", [128, AB_COLS], BF16)
        self.hbuf = self.sbuf("hbuf", [128, DX * NT], BF16)

        self.dram_in("x_in", (B, T, D))
        self.dram_out("x_out", (B, T, D))
        cns = sorted({n for s in self.slots for n in CONST_NAMES[s]})
        for nm in cns:
            self.dram_in(nm, CONST_SHAPES[nm])
            if nm not in CONST_NOSBUF:
                self.sbuf(nm, CONST_SHAPES[nm], CONST_DTYPES.get(nm, F32))
        for s in self.slots:
            for nm in SLOT_WEIGHTS[s]:
                self.dram_in(nm, WEIGHT_SHAPES[nm])
        self.sbuf("c_ones_col_b", [128, 1], BF16)
        self.sbuf("c_ones_row_b", [1, 128], BF16)

        for nm in cns:
            if nm in CONST_NOSBUF:
                continue
            if CONST_DTYPES.get(nm, F32) == F32:
                P.dma("sync", self.sb[nm][:], self.io[nm], self.LS)
            else:
                P.dma("gpsimd", self.sb[nm][:], self.io[nm], self.LD)
        P.dma("gpsimd", self.sb["c_ones_col_b"][:], self.io["c_ones_col"], self.LD)
        P.dma("gpsimd", self.sb["c_ones_row_b"][:], self.io["c_ones_row"], self.LD)

        self.pA = self.psum("pA", [128, 2048], F32)
        self.pB = self.psum("pB", [128, 1024], F32)
        self.pT = self.psum("pT", [128, 1024], BF16)
        self.pR = self.psum("pR", [128, 512], F32)
        self.sbuf("wtmp", [128, 4096], BF16)
        self.sbuf("lnsum", [128, DX * 512], BF16)
        self.sbuf("rows", [1, 1024], F32)
        self.sbuf("rows2", [2, 2048], F32)
        self.sbuf("rowb", [1, 128], BF16)
        self.sbuf("rowd", [1, 512], BF16)
        self.sbuf("fcols", [128, 512], F32)
        self.sbuf("rowbuf", [1, 1024], F32)
        self.sbuf("t1f", [128, 512], F32)
        self.sbuf("otile", [128, D], BF16)
        self.barrier("sync")

    def barrier(self, owner="vector"):
        P = self.P
        sems = [s for s in P.sems.values()
                if s is not self.TICK and s.n > 0]
        for s in sems:
            P.wait(owner, s)
        self.TICK.n += 1
        P.emit(owner, lambda e, h=self.TICK.h: e.sem_inc(h, 1))
        for e in ENGINES:
            P.wait(e, self.TICK)

    def prep_transposed(self, dst, dst_col0, src_ap, rows, cols, row0=0,
                        dst_stride=None, scale_col=None):
        P = self.P
        if dst_stride is None:
            dst_stride = rows
        wtmp = self.sb["wtmp"]
        identb = self.sb["c_ident_b"]
        nk = cols // 128
        nr = rows // 128
        P.wait("vector", self.ST)
        for r in range(nr):
            P.wait("gpsimd", self.MM)
            P.wait("gpsimd", self.EV)
            P.dma("gpsimd", wtmp[:, 0:cols],
                  src_ap[row0 + r * 128: row0 + (r + 1) * 128, :], self.LD)
            for k0 in range(0, nk, 8):
                kn = min(8, nk - k0)
                P.wait("tensor", self.LD)
                P.wait("tensor", self.EV)
                for kk in range(kn):
                    kX = k0 + kk
                    P.op("tensor", "transpose",
                         self.pT[:, kk * 128:(kk + 1) * 128],
                         wtmp[:, kX * 128:(kX + 1) * 128], identb[:],
                         inc=self.MM if kk == kn - 1 else None)
                P.wait("vector", self.MM)
                for kk in range(kn):
                    kX = k0 + kk
                    c0 = dst_col0 + kX * dst_stride + r * 128
                    if scale_col is None:
                        P.op("vector", "tensor_copy", dst[:, c0:c0 + 128],
                             self.pT[:, kk * 128:(kk + 1) * 128],
                             inc=self.EV if kk == kn - 1 else None)
                    else:
                        P.op("vector", "tensor_scalar_mul", dst[:, c0:c0 + 128],
                             self.pT[:, kk * 128:(kk + 1) * 128],
                             scale_col(kX),
                             inc=self.EV if kk == kn - 1 else None)
        P.wait("tensor", self.EV)

    def load_bias_cols(self, dst_ap, src_row_ap, n):
        P = self.P
        onesr = self.sb["c_ones_row"]
        rowbuf = self.sb["rowbuf"]
        for j0 in range(0, n, 8):
            jn = min(8, n - j0)
            P.wait("sync", self.MM)
            P.dma("sync", rowbuf[0:1, 0:jn * 128],
                  src_row_ap[0:1, j0 * 128:(j0 + jn) * 128], self.LS)
            P.wait("tensor", self.LS)
            P.wait("tensor", self.EV)
            for j in range(jn):
                P.op("tensor", "matmul", self.pR[:, j:j + 1],
                     rowbuf[0:1, j * 128:(j + 1) * 128], onesr[0:1, 0:1],
                     start=True, stop=True,
                     inc=self.MM if j == jn - 1 else None)
            P.wait("vector", self.MM)
            P.op("vector", "tensor_copy", dst_ap[:, j0:j0 + jn],
                 self.pR[:, 0:jn], inc=self.EV)

    def load_resid(self):
        P = self.P
        src = self.io["x_in"]
        xtm = self.sb["wtmp"]
        identb = self.sb["c_ident_b"]
        for b in range(B):
            for tt in range(T // 128):
                P.wait("gpsimd", self.MM)
                P.dma("gpsimd", xtm[:, 0:D],
                      src[b, tt * 128:(tt + 1) * 128, :], self.LD)
                P.wait("tensor", self.LD)
                P.wait("tensor", self.EV)
                for dXi in range(DX):
                    P.op("tensor", "transpose",
                         self.pT[:, dXi * 128:(dXi + 1) * 128],
                         xtm[:, dXi * 128:(dXi + 1) * 128], identb[:],
                         inc=self.MM if dXi == DX - 1 else None)
                P.wait("vector", self.MM)
                for dXi in range(DX):
                    c0 = dXi * NT + b * T + tt * 128
                    P.op("vector", "tensor_copy",
                         self.resid[:, c0:c0 + 128],
                         self.pT[:, dXi * 128:(dXi + 1) * 128],
                         inc=self.EV if dXi == DX - 1 else None)
        self.barrier()

    def store_out(self, src_name=None):
        P = self.P
        dst = self.io["x_out"]
        identb = self.sb["c_ident_b"]
        otile = self.sb["otile"]
        src = self.sb[src_name] if src_name else self.resid
        for b in range(B):
            for tt in range(T // 128):
                P.wait("tensor", self.STG)
                P.wait("tensor", self.EV)
                for dXi in range(DX):
                    c0 = dXi * NT + b * T + tt * 128
                    P.op("tensor", "transpose",
                         self.pT[:, dXi * 128:(dXi + 1) * 128],
                         src[:, c0:c0 + 128], identb[:],
                         inc=self.MM if dXi == DX - 1 else None)

                P.wait("vector", self.MM)
                P.wait("vector", self.STG)
                P.op("vector", "tensor_copy", otile[:, 0:D],
                     self.pT[:, 0:1024], inc=self.EV)
                P.wait("gpsimd", self.EV)
                P.dma("gpsimd", dst[b, tt * 128:(tt + 1) * 128, :],
                      otile[:, 0:D], self.STG)
        P.wait("gpsimd", self.STG)
        self.barrier("gpsimd")

    def postln_chunk(self, rcol0, slot_fn, bias_col=None, add_resid=True,
                     extra_fn=None, presummed=False):
        P = self.P
        lnsum, lnsq = self.sb["lnsum"], self.sb["wtmp"]
        onesb = self.sb["c_ones_col_b"]
        ones1f = self.sb["c_ones_row"]
        rows = self.sb["rows"]
        t1f = self.sb["t1f"]
        for dXi in range(DX if not presummed else 0):
            rc = rcol0 + dXi * NT
            d_ap = lnsum[:, dXi * 512:(dXi + 1) * 512]
            s_ap = slot_fn(dXi)
            last = (dXi == DX - 1) and extra_fn is None
            if add_resid:
                if bias_col is not None:
                    P.op("vector", "scalar_tensor_tensor", d_ap, s_ap,
                         bias_col, self.resid[:, rc:rc + 512], ALU.add,
                         ALU.add, inc=self.EV if last else None)
                else:
                    P.op("vector", "tensor_add", d_ap, s_ap,
                         self.resid[:, rc:rc + 512],
                         inc=self.EV if last else None)
            else:
                P.op("vector", "tensor_copy", d_ap, s_ap,
                     inc=self.EV if last else None)
        if extra_fn is not None:
            P.wait("vector", self.EV)
            for dXi in range(DX):
                d_ap = lnsum[:, dXi * 512:(dXi + 1) * 512]
                P.op("vector", "tensor_add", d_ap, d_ap, extra_fn(dXi),
                     inc=self.EV if dXi == DX - 1 else None)
        P.wait("scalar", self.EV)
        for dXi in range(DX):
            P.op("scalar", "activation", lnsq[:, dXi * 512:(dXi + 1) * 512],
                 lnsum[:, dXi * 512:(dXi + 1) * 512], AF.Square,
                 inc=self.AC if dXi == DX - 1 else None)
        P.wait("tensor", self.EV)
        P.wait("tensor", self.AC)
        for dXi in range(DX):
            P.op("tensor", "matmul", self.pB[0:1, 0:512], onesb[:],
                 lnsum[:, dXi * 512:(dXi + 1) * 512],
                 start=(dXi == 0), stop=(dXi == DX - 1))
        for dXi in range(DX):
            P.op("tensor", "matmul", self.pB[0:1, 512:1024], onesb[:],
                 lnsq[:, dXi * 512:(dXi + 1) * 512],
                 start=(dXi == 0), stop=(dXi == DX - 1),
                 inc=self.MM if dXi == DX - 1 else None)
        P.wait("vector", self.MM)
        P.op("vector", "tensor_scalar_mul", rows[0:1, 0:512],
             self.pB[0:1, 0:512], 1.0 / D, inc=self.EV)
        P.wait("scalar", self.EV)
        P.op("scalar", "activation", rows[0:1, 512:1024], rows[0:1, 0:512],
             AF.Square, inc=self.AC)
        P.wait("vector", self.AC)
        P.op("vector", "scalar_tensor_tensor", rows[0:1, 512:1024],
             self.pB[0:1, 512:1024], 1.0 / D, rows[0:1, 512:1024],
             ALU.mult, ALU.subtract, inc=self.EV)
        P.wait("scalar", self.EV)
        P.op("scalar", "activation", rows[0:1, 512:1024],
             rows[0:1, 512:1024], AF.Ln,
             bias=self.sb["c_eps"][0:1, 0:1], inc=self.AC)
        P.wait("scalar", self.AC, self.AC.n)
        P.op("scalar", "activation", rows[0:1, 512:1024],
             rows[0:1, 512:1024], AF.Exp, scale=-0.5, inc=self.AC)
        P.wait("tensor", self.AC)
        P.op("tensor", "matmul", self.pA[:, 0:512], ones1f[0:1, :],
             rows[0:1, 0:512], start=True, stop=True)
        P.op("tensor", "matmul", self.pA[:, 512:1024], ones1f[0:1, :],
             rows[0:1, 512:1024], start=True, stop=True, inc=self.MM)
        P.wait("vector", self.MM)
        for dXi in range(DX):
            rc = rcol0 + dXi * NT
            P.op("vector", "tensor_sub", t1f[:],
                 lnsum[:, dXi * 512:(dXi + 1) * 512], self.pA[:, 0:512],
                 inc=self.EV)
            P.wait("vector", self.EV)
            P.op("vector", "tensor_mul", self.resid[:, rc:rc + 512],
                 t1f[:], self.pA[:, 512:1024], inc=self.EV)
            P.wait("vector", self.EV)

    def slot0(self):
        P = self.P
        if True:
            al = self.arena_alloc()
            wzT = al.take(DX * D)
            whT = al.take(DX * D)
            bcol = self.sb["fcols"][:, 0:2 * DX]
            nbz = self.sb["fcols"][:, 2 * DX:3 * DX]
            SC = 512
            hb = self.hbuf
            Ab = hb[:, 0:DX * SC]
            Bb = hb[:, DX * SC:2 * DX * SC]
            XM = [hb[:, 2 * DX * SC:3 * DX * SC],
                  hb[:, 3 * DX * SC:4 * DX * SC]]
            zt = hb[:, 4 * DX * SC:4 * DX * SC + SC]
            sg = hb[:, 4 * DX * SC + SC:4 * DX * SC + 2 * SC]
            hp5 = hb[:, 4 * DX * SC + 2 * SC:4 * DX * SC + 3 * SC]
            gt = hb[:, 4 * DX * SC + 3 * SC:4 * DX * SC + 4 * SC]

            self.prep_transposed(wzT, 0, self.io["L0_attn_Wz"], D, D)
            self.prep_transposed(whT, 0, self.io["L0_attn_Wh"], D, D)
            self.load_bias_cols(bcol[:, 0:DX], self.io["L0_attn_bz"], DX)
            self.load_bias_cols(bcol[:, DX:2 * DX], self.io["L0_attn_bh"], DX)
            P.wait("vector", self.EV)
            P.op("vector", "tensor_scalar_mul", nbz[:], bcol[:, 0:DX], -1.0,
                 inc=self.EV)
            self.barrier()

            for b in range(B):
                for ci in range(T // SC):
                    c0 = ci * SC
                    par = ci % 2
                    for hXi in range(DX):
                        pk = self.pA[:, (hXi % 2) * 512:(hXi % 2) * 512 + 512]
                        ph = self.pB[:, (hXi % 2) * 512:(hXi % 2) * 512 + 512]
                        P.wait("tensor", self.EV)
                        P.wait("tensor", self.AC)
                        for (w, p) in ((wzT, pk), (whT, ph)):
                            for dXi in range(DX):
                                rhs = self.resid[:, dXi * NT + b * T + c0:
                                                 dXi * NT + b * T + c0 + SC]
                                lhsT = w[:, dXi * D + hXi * 128:
                                         dXi * D + (hXi + 1) * 128]
                                for n in range(SC // 512):
                                    P.op("tensor", "matmul",
                                         p[:, n * 512:(n + 1) * 512], lhsT,
                                         rhs[:, n * 512:(n + 1) * 512],
                                         start=(dXi == 0), stop=(dXi == DX - 1),
                                         inc=self.MM if (w is whT
                                                         and dXi == DX - 1
                                                         and n == SC // 512 - 1)
                                         else None)
                        P.wait("scalar", self.MM)
                        hcol = slice(hXi * SC, (hXi + 1) * SC)
                        P.op("scalar", "activation", Ab[:, hcol], pk[:],
                             AF.Sigmoid, bias=nbz[:, hXi:hXi + 1], scale=-1.0,
                             inc=self.AC)
                        P.op("scalar", "activation", zt[:], pk[:], AF.Sigmoid,
                             bias=bcol[:, hXi:hXi + 1], inc=self.AC)
                        P.op("scalar", "activation", sg[:], ph[:], AF.Sigmoid,
                             bias=bcol[:, DX + hXi:DX + hXi + 1], inc=self.AC)
                        P.wait("vector", self.MM)
                        P.op("vector", "tensor_scalar", hp5[:], ph[:],
                             bcol[:, DX + hXi:DX + hXi + 1], 0.5, ALU.add,
                             ALU.add, inc=self.EV)
                        P.wait("vector", self.EV)
                        P.wait("vector", self.AC)
                        P.op("vector", "tensor_max", gt[:], hp5[:], sg[:],
                             inc=self.EV)
                        P.wait("vector", self.EV)
                        P.op("vector", "tensor_mul", Bb[:, hcol], zt[:], gt[:],
                             inc=self.EV)
                        P.wait("vector", self.EV)
                        init = 0.0 if ci == 0 else \
                            XM[1 - par][:, hXi * SC + SC - 1: hXi * SC + SC]
                        P.op("vector", "tensor_tensor_scan", XM[par][:, hcol],
                             Ab[:, hcol], Bb[:, hcol], init, ALU.mult, ALU.add,
                             inc=self.EV)
                        P.wait("vector", self.EV)
                    self.barrier()
                    self.postln_chunk(
                        b * T + c0,
                        lambda dXi, par=par:
                            XM[par][:, dXi * SC: dXi * SC + 512])
                    self.barrier()


    def slot1(self):
        P = self.P
        al = self.arena_alloc()
        h1 = al.take(HX * 512)
        h2 = self.hbuf[:, 0:HX * 512]
        wb = [self.hbuf[:, HX * 512:HX * 512 + 8192],
              self.hbuf[:, HX * 512 + 8192:HX * 512 + 16384]]
        silu_s = al.take(512)
        fc = self.sb["fcols"]
        s1c, s2c = fc[:, 0:32], fc[:, 32:64]
        sh1c, sh2c = fc[:, 64:96], fc[:, 96:128]
        s3c = fc[:, 128:136]

        scr = {
            "w1": self.dram_scratch("scr_w1", (HX, 128, 1024)),
            "w2": self.dram_scratch("scr_w2", (HX, 128, 1024)),
            "h1": self.dram_scratch("scr_h1", (HX, 128, 4096)),
            "h2": self.dram_scratch("scr_h2", (HX, 128, 4096)),
            "w3": self.dram_scratch("scr_w3", (DX, 128, 4096)),
        }
        self.load_bias_cols(s1c, self.io["L0_ffn_w1_s"], 32)
        self.load_bias_cols(s2c, self.io["L0_ffn_w2_s"], 32)
        self.load_bias_cols(sh1c, self.io["L0_ffn_h1_s"], 32)
        self.load_bias_cols(sh2c, self.io["L0_ffn_h2_s"], 32)
        self.load_bias_cols(s3c, self.io["L0_ffn_w3_s"], DX)
        self.barrier()

        lastmm = {}

        def get_block(g, bufs, par, tag, src_name, m, nk, blkcols):
            wap = bufs[par][:, 0:blkcols]
            if g == 0:
                P.wait("sync", self.MM, lastmm.get((tag, par), 0))
                self.prep_transposed(wap, 0, self.io[src_name], 128,
                                     nk * 128, row0=m * 128, dst_stride=128)
                P.wait("sync", self.EV)
                P.dma("sync", scr[tag][m], wap, self.ST)
            else:
                P.wait("sync", self.MM, lastmm.get((tag, par), 0))
                P.dma("sync", wap, scr[tag][m], self.LS)
            return wap

        def stageA(g, gc):
            for m in range(HX):
                par = m % 2
                w1ap = get_block(g, [wb[0][:, 0:1024], wb[1][:, 0:1024]],
                                 par, "w1", "L0_ffn_w1_w", m, DX, 1024)
                w2ap = get_block(g, [wb[0][:, 1024:2048], wb[1][:, 1024:2048]],
                                 par, "w2", "L0_ffn_w2_w", m, DX, 1024)
                pa = self.pA[:, par * 1024: par * 1024 + 512]
                pb = self.pA[:, par * 1024 + 512: par * 1024 + 1024]
                P.wait("tensor", self.LS)
                P.wait("tensor", self.EV, lastmm.get(("psA", par), 0))
                P.wait("tensor", self.AC, lastmm.get(("psAa", par), 0))
                for k in range(DX):
                    rhs = self.resid[:, k * NT + gc: k * NT + gc + 512]
                    P.op("tensor", "matmul", pa, w1ap[:, k * 128:(k + 1) * 128],
                         rhs, start=(k == 0), stop=(k == DX - 1))
                for k in range(DX):
                    rhs = self.resid[:, k * NT + gc: k * NT + gc + 512]
                    P.op("tensor", "matmul", pb, w2ap[:, k * 128:(k + 1) * 128],
                         rhs, start=(k == 0), stop=(k == DX - 1),
                         inc=self.MM if k == DX - 1 else None)
                lastmm[("w1", par)] = self.MM.n
                lastmm[("w2", par)] = self.MM.n
                P.wait("scalar", self.MM)
                P.wait("scalar", self.EV)
                P.op("scalar", "activation", silu_s[:, 0:512], pa, AF.Silu,
                     scale=s1c[:, m:m + 1], inc=self.AC)
                lastmm[("psAa", par)] = self.AC.n
                P.wait("vector", self.AC)
                P.op("vector", "scalar_tensor_tensor",
                     h1[:, m * 512:(m + 1) * 512], pb, s2c[:, m:m + 1],
                     silu_s[:, 0:512], ALU.mult, ALU.mult, inc=self.EV)
                lastmm[("psA", par)] = self.EV.n

        def stageMM(g, gc, tag, src_name, src_buf, dst_buf, nm, nk, scol,
                    act):
            for mg in range(nm // 2):
                par = mg % 2
                if g == 0:
                    P.wait("vector", self.MM, lastmm.get((tag, par), 0))
                    for mm_ in range(2):
                        self.prep_transposed(
                            wap_sub(wb[par], mm_, nk), 0, self.io[src_name],
                            128, nk * 128, row0=(mg * 2 + mm_) * 128,
                            dst_stride=128)
                    P.wait("sync", self.EV)
                    P.dma("sync", scr[tag][mg * 2:(mg + 1) * 2].transpose(
                        [1, 0, 2]),
                        wb[par][:, 0:2 * nk * 128].rearrange(
                            "p (m c) -> p m c", m=2), self.ST)
                else:
                    P.wait("sync", self.MM, lastmm.get((tag, par), 0))
                    P.dma("sync", wb[par][:, 0:2 * nk * 128].rearrange(
                        "p (m c) -> p m c", m=2),
                        scr[tag][mg * 2:(mg + 1) * 2].transpose([1, 0, 2]),
                        self.LS)
                P.wait("tensor", self.LS)
                for mm_ in range(2):
                    m = mg * 2 + mm_
                    p2 = m % 2
                    pc = self.pB[:, p2 * 512: p2 * 512 + 512]
                    P.wait("tensor", self.AC,
                           lastmm.get(("psB" + act.name, p2), 0))
                    for k in range(nk):
                        P.op("tensor", "matmul", pc,
                             wb[par][:, mm_ * nk * 128 + k * 128:
                                     mm_ * nk * 128 + (k + 1) * 128],
                             src_buf[:, k * 512:(k + 1) * 512],
                             start=(k == 0), stop=(k == nk - 1),
                             inc=self.MM if k == nk - 1 else None)
                    P.wait("scalar", self.MM)
                    P.op("scalar", "activation",
                         dst_buf[:, m * 512:(m + 1) * 512],
                         pc, act, scale=scol[:, m:m + 1], inc=self.AC)
                    lastmm[("psB" + act.name, p2)] = self.AC.n
                lastmm[(tag, par)] = self.MM.n

        def wap_sub(wbuf, mm_, nk):
            return wbuf[:, mm_ * nk * 128:(mm_ + 1) * nk * 128]

        for g in range(NT // 512):
            gc = g * 512
            stageA(g, gc)
            self.barrier()
            stageMM(g, gc, "h1", "L0_ffn_h1_w", h1, h2, HX, HX, sh1c, AF.Gelu)
            self.barrier()
            stageMM(g, gc, "h2", "L0_ffn_h2_w", h2, h1, HX, HX, sh2c, AF.Gelu)
            self.barrier()
            for m in range(DX):
                par = m % 2
                wap = get_block(g, wb, par, "w3", "L0_ffn_w3_w", m, HX, 4096)
                pc = self.pB[:, par * 512: par * 512 + 512]
                P.wait("tensor", self.LS)
                P.wait("tensor", self.EV, lastmm.get(("psD", par), 0))
                for k in range(HX):
                    P.op("tensor", "matmul", pc, wap[:, k * 128:(k + 1) * 128],
                         h1[:, k * 512:(k + 1) * 512],
                         start=(k == 0), stop=(k == HX - 1),
                         inc=self.MM if k == HX - 1 else None)
                lastmm[("w3", par)] = self.MM.n
                P.wait("vector", self.MM)
                P.op("vector", "scalar_tensor_tensor",
                     self.sb["lnsum"][:, m * 512:(m + 1) * 512], pc,
                     s3c[:, m:m + 1], self.resid[:, m * NT + gc:
                                                 m * NT + gc + 512],
                     ALU.mult, ALU.add, inc=self.EV)
                lastmm[("psD", par)] = self.EV.n
            self.barrier()
            self.postln_chunk(gc, None, presummed=True)
            self.barrier()


    def col64_to_128(self, dst_col, src_row_ap, n64=64):
        P = self.P
        rowbuf = self.sb["rowbuf"]
        onesr = self.sb["c_ones_row"]
        P.wait("sync", self.MM)
        P.dma("sync", rowbuf[0:1, 0:n64], src_row_ap[0:1, 0:n64], self.LS)
        P.wait("tensor", self.LS)
        P.wait("tensor", self.EV)
        P.op("tensor", "matmul", self.pR[0:n64, 0:1], rowbuf[0:1, 0:n64],
             onesr[0:1, 0:1], start=True, stop=True, inc=self.MM)
        P.wait("vector", self.MM)
        P.op("vector", "tensor_copy", dst_col[0:64, :], self.pR[0:64, 0:1],
             inc=self.EV)
        P.wait("vector", self.EV)
        P.op("vector", "tensor_copy", dst_col[64:128, :], self.pR[0:64, 0:1],
             inc=self.EV)

    def attn_heads(self, Qb, Kb, Vb2, causal, mrow, out_fn, post_fn=None,
                   bound_cols=None):
        P = self.P
        fc = self.sb["fcols"]
        m_all = fc[:, 0:16]
        mh = fc[:, 16:20]
        t1f = self.sb["t1f"]
        ones1f = self.sb["c_ones_row"]
        n8i = self.sb["c_neg8ident"]
        mask_st = self.sb["c_mask_st"]
        mask_ts = self.sb["c_mask_ts"]
        lu = self._attn_lu = {}

        def wl(eng, key):
            v = lu.get(key)
            if v is None:
                return
            if isinstance(v, list):
                for sem, val in v:
                    P.wait(eng, sem, val)
            else:
                P.wait(eng, v[0], v[1])

        lu[("pa", 0)] = (self.EV, self.EV.n)
        lu[("pa", 1)] = (self.EV, self.EV.n)
        for tg_ in ("s", 0, 1):
            lu[("pst", tg_, 0)] = [(self.AC, self.AC.n), (self.EV, self.EV.n)]
            lu[("pst", tg_, 1)] = [(self.AC, self.AC.n), (self.EV, self.EV.n)]
        lu["mrow_free"] = (self.MM, self.MM.n)

        def passA_items(h):
            hb = 64 * h
            items = []
            for i in range(16):
                def item(i=i, hb=hb, h=h):
                    ncols = (i + 1) * 128 if causal else 2048
                    sl = i % 2
                    base = sl * 1024
                    msl = mh[:, sl * 2:sl * 2 + 2]
                    nhalf = min(2, (ncols + 1023) // 1024)
                    wl("vector", ("mh", sl))
                    for hf in range(nhalf):
                        w = min(1024, ncols - hf * 1024)
                        wl("tensor", ("pa", sl))
                        nch = (w + 511) // 512
                        for cc in range(nch):
                            cw = min(512, w - cc * 512)
                            P.op("tensor", "matmul",
                                 self.pA[:, base + cc * 512:
                                         base + cc * 512 + cw],
                                 Qb[hb:hb + 64, i * 128:(i + 1) * 128],
                                 Kb[hb:hb + 64, hf * 1024 + cc * 512:
                                    hf * 1024 + cc * 512 + cw],
                                 start=True, stop=True,
                                 inc=self.MM if cc == nch - 1 else None)
                        mmc = self.MM.n
                        P.wait("vector", self.MM, mmc)
                        if causal and hf * 1024 <= i * 128 < hf * 1024 + w:
                            off = i * 128 - hf * 1024
                            P.op("vector", "tensor_add",
                                 self.pA[:, base + off:base + off + 128],
                                 self.pA[:, base + off:base + off + 128],
                                 mask_st[:], inc=self.EV)
                            P.wait("vector", self.EV, self.EV.n)
                        P.op("vector", "reduce_max", msl[:, hf:hf + 1],
                             self.pA[:, base:base + w], axis=AX.X,
                             negate=True, inc=self.EV)
                        lu[("pa", sl)] = (self.EV, self.EV.n)
                    P.wait("vector", self.EV, self.EV.n)
                    if nhalf == 2:
                        P.op("vector", "tensor_tensor", msl[:, 0:1],
                             msl[:, 0:1], msl[:, 1:2], ALU.min, inc=self.EV)
                        P.wait("vector", self.EV, self.EV.n)
                    P.op("vector", "tensor_scalar_mul", m_all[:, i:i + 1],
                         msl[:, 0:1], 0.125, inc=self.EV)
                    lu[("mh", sl)] = (self.EV, self.EV.n)
                    lu[("mcol", h, i)] = (self.EV, self.EV.n)
                items.append(item)
            return items

        def emit_mrows(h):
            wl("tensor", ("pa", 0))
            for i in range(16):
                wl("tensor", ("mcol", h, i))
            wl("tensor", "mrow_free")
            for i in range(16):
                P.op("tensor", "matmul",
                     self.pA[0:1, (i % 4) * 128:(i % 4 + 1) * 128],
                     m_all[:, i:i + 1], n8i[:], start=True, stop=True,
                     inc=self.MM if i % 4 == 3 else None)
                if i % 4 == 3:
                    mmc = self.MM.n
                    P.wait("vector", self.MM, mmc)
                    P.op("vector", "tensor_copy",
                         mrow[0:1, (i - 3) * 128:(i + 1) * 128],
                         self.pA[0:1, 0:512], inc=self.EV)
                    if i < 15:
                        P.wait("tensor", self.EV)
            lu[("pa", 0)] = (self.EV, self.EV.n)
            lu["mrow_ready"] = (self.EV, self.EV.n)

        def passB_items(h):
            hb = 64 * h
            res = (self._attn_res or {}).get(h) if getattr(
                self, "_attn_res", None) else None
            if res is None:
                res = dict(pst=(self.pB[:, 0:512], self.pB[:, 512:1024]),
                           av=self.pR[0:65, 0:512],
                           ptb=(self._attn_ptb[:, 0:512],
                                self._attn_ptb[:, 512:1024]),
                           rep=self.pB[:, 512:1024], repk=("pst", "s", 1),
                           avs=t1f[0:65, :], tag="s")
            else:
                res = dict(res)
                res["tag"] = h
            tg = res["tag"]
            items = []
            for c in range(4):
                jstop = min(15, (c * 512 + 511) // 128) if causal else 15
                jlist = list(range(jstop + 1))
                for j in jlist:
                    def item(j=j, c=c, hb=hb, h=h, res=res, tg=tg,
                             first=(j == 0), last=(j == jlist[-1])):
                        pj = j % 2
                        bank = res["pst"][pj]
                        sc0 = c * 512
                        wl("tensor", ("pst", tg, pj))
                        if bound_cols is None:
                            wl("tensor", "mrow_ready")
                            P.op("tensor", "matmul", bank,
                                 Kb[hb:hb + 64, j * 128:(j + 1) * 128],
                                 Qb[hb:hb + 64, sc0:sc0 + 512],
                                 start=True, stop=False)
                            P.op("tensor", "matmul", bank,
                                 self.sb["c_ones_row_b"][0:1, :],
                                 mrow[0:1, sc0:sc0 + 512],
                                 start=False, stop=True, inc=self.MM)
                            lu["mrow_free"] = (self.MM, self.MM.n)
                        else:
                            P.op("tensor", "matmul", bank,
                                 Kb[hb:hb + 64, j * 128:(j + 1) * 128],
                                 Qb[hb:hb + 64, sc0:sc0 + 512],
                                 start=True, stop=True, inc=self.MM)
                        mmc = self.MM.n
                        nev = None
                        if causal and sc0 <= j * 128 < sc0 + 512:
                            off = j * 128 - sc0
                            P.wait("vector", self.MM, mmc)
                            if off > 0:
                                P.op("vector", "tensor_scalar_add",
                                     bank[:, 0:off], bank[:, 0:off],
                                     -1e30, inc=self.EV)
                            P.op("vector", "tensor_add",
                                 bank[:, off:off + 128],
                                 bank[:, off:off + 128], mask_ts[:],
                                 inc=self.EV)
                            nev = self.EV.n
                        P.wait("scalar", self.MM, mmc)
                        if nev:
                            P.wait("scalar", self.EV, nev)
                        wl("scalar", ("ptb", tg, pj))
                        ptb_ap = res["ptb"][pj]
                        if bound_cols is None:
                            P.op("scalar", "activation", ptb_ap, bank,
                                 AF.Exp, scale=0.125, inc=self.AC)
                        else:
                            P.op("scalar", "activation", ptb_ap, bank,
                                 AF.Exp, scale=0.125, bias=bound_cols[h],
                                 inc=self.AC)
                        lu[("pst", tg, pj)] = (self.AC, self.AC.n)
                        acc = self.AC.n

                        def av_part(acc=acc, ptb_ap=ptb_ap, j=j, c=c, h=h,
                                    res=res, tg=tg, first=first, last=last):
                            P.wait("tensor", self.AC, acc)
                            if first:
                                wl("tensor", ("prav", tg))
                            P.op("tensor", "matmul", res["av"],
                                 Vb2[:, j * 130 + h * 65:
                                     j * 130 + h * 65 + 65],
                                 ptb_ap, start=first, stop=last,
                                 inc=self.MM if last else None)
                            lu[("ptb", tg, pj)] = (self.MM, self.MM.n)
                            if last:
                                self._finish_chunk(h, c, res, tg)
                        return av_part
                    items.append(item)
            return items

        def finish_chunk(h, c, res, tg):
            avs = res["avs"]
            mmc = self.MM.n
            P.wait("vector", self.MM, mmc)
            wl("vector", ("avsfree", tg))
            P.op("vector", "tensor_copy", avs, res["av"], inc=self.EV)
            lu[("prav", tg)] = (self.EV, self.EV.n)
            P.wait("scalar", self.EV, self.EV.n)
            wl("scalar", "rowd")
            P.op("scalar", "activation", avs[64:65, :], avs[64:65, :],
                 AF.Ln, inc=self.AC)
            P.wait("scalar", self.AC, self.AC.n)
            P.op("scalar", "activation", self.sb["rowd"][0:1, 0:512],
                 avs[64:65, :], AF.Exp, scale=-1.0, inc=self.AC)
            rc = self.AC.n
            P.wait("tensor", self.AC, rc)
            if res["repk"] is not None:
                wl("tensor", res["repk"])
            else:
                wl("tensor", ("repfree", tg))
            P.op("tensor", "matmul", res["rep"],
                 self.sb["c_ones_row_b"][0:1, :],
                 self.sb["rowd"][0:1, 0:512], start=True, stop=True,
                 inc=self.MM)
            lu["rowd"] = (self.MM, self.MM.n)
            repc = self.MM.n
            P.wait("vector", self.MM, repc)
            dst = out_fn(h, c)
            P.op("vector", "tensor_mul", dst, avs[0:64, :],
                 res["rep"][0:64, :], inc=self.EV)
            lu[("pst", tg, 1)] = (self.EV, self.EV.n)
            lu[("repfree", tg)] = (self.EV, self.EV.n)
            lu[("avsfree", tg)] = (self.EV, self.EV.n)
            if post_fn is not None:
                post_fn(h, c)
        self._finish_chunk = finish_chunk

        if bound_cols is not None:
            res_h = {
                0: dict(pst=(self.pB[:, 0:512], self.pB[:, 512:1024]),
                        av=self.pR[0:65, 0:512],
                        ptb=(self._attn_ptb[:, 0:512],
                             self._attn_ptb[:, 512:1024]),
                        rep=self.pB[:, 512:1024], repk=("pst", 0, 1),
                        avs=t1f[0:65, :]),
                1: dict(pst=(self.pA[:, 0:512], self.pA[:, 512:1024]),
                        av=self.pA[0:65, 1024:1536],
                        ptb=(self._attn_ptb[:, 1024:1536],
                             self._attn_ptb[:, 1536:2048]),
                        rep=self.pA[:, 1536:2048], repk=None,
                        avs=self.sb["wtmp"][0:65, 0:512]),
            }
            self._attn_res = res_h
            b0 = passB_items(0)
            b1 = passB_items(1)
            n = max(len(b0), len(b1))
            for i in range(n):
                avs_ = []
                if i < len(b0):
                    avs_.append(b0[i]())
                if i < len(b1):
                    avs_.append(b1[i]())
                for a_ in avs_:
                    a_()
            self._attn_res = None
        else:
            for it in passA_items(0):
                it()
            emit_mrows(0)
            b0 = passB_items(0)
            a1 = passA_items(1)
            k = 0
            prev_av = None
            for i, it in enumerate(b0):
                av_ = it()
                if i % 2 == 1 and k < len(a1):
                    a1[k]()
                    k += 1
                if prev_av is not None:
                    prev_av()
                prev_av = av_
            if prev_av is not None:
                prev_av()
            while k < len(a1):
                a1[k]()
                k += 1
            emit_mrows(1)
            prev_av = None
            for it in passB_items(1):
                av_ = it()
                if prev_av is not None:
                    prev_av()
                prev_av = av_
            if prev_av is not None:
                prev_av()

    def slot2(self):
        P = self.P
        al = self.arena_alloc()
        wqb = [al.take(3072), al.take(3072)]
        Qb = al.take(2048)
        Kb = al.take(2048)
        Vb2 = al.take(16 * 130)
        sqs = al.take(2048)
        ptb = al.take(1024)
        avT = self.hbuf[:, 0:DX * 2048]
        wpT = self.hbuf[:, DX * 2048:DX * 2048 + 8192]
        fc = self.sb["fcols"]
        m_all = fc[:, 0:16]
        den_all = fc[:, 16:32]
        g_all = fc[:, 32:48]
        nscal = fc[:, 48:49]
        bq_cols = fc[:, 50:58]
        bk_cols = fc[:, 58:66]
        bp_cols = fc[:, 66:74]
        qn_col = fc[:, 74:75]
        kn_col = fc[:, 75:76]
        rows2 = self.sb["rows2"]
        rowb = self.sb["rowb"]
        blockones = self.sb["c_blockones"]
        blockonesT = self.sb["c_blockonesT"]
        ones1f = self.sb["c_ones_row"]
        n8i = self.sb["c_neg8ident"]
        eps6 = self.sb["c_eps"][0:1, 1:2]
        eps6r = self.sb["c_eps"][0:2, 1:2]

        self.load_bias_cols(bq_cols, self.io["L1_attn_bqkv"][:, 0:1024], 8)
        self.load_bias_cols(bk_cols, self.io["L1_attn_bqkv"][:, 1024:2048], 8)
        self.load_bias_cols(bp_cols, self.io["L1_attn_bproj"], 8)
        self.col64_to_128(qn_col, self.io["L1_attn_qn_w"])
        self.col64_to_128(kn_col, self.io["L1_attn_kn_w"])
        self.prep_transposed(wpT, 0, self.io["L1_attn_Wproj"], D, D)
        P.wait("vector", self.LD)
        for tt in range(16):
            for hh in range(2):
                P.op("vector", "tensor_copy",
                     Vb2[:, tt * 130 + hh * 65 + 64:tt * 130 + hh * 65 + 65],
                     self.sb["c_ones_col_b"][:],
                     inc=self.EV if (tt == 15 and hh == 1) else None)
        self.barrier()

        lastu = {}
        for b in range(B):
            bc = b * T
            for p in range(NH // 2):
                par = p % 2
                wq = wqb[par][:, 0:1024]
                wk = wqb[par][:, 1024:2048]
                wv = wqb[par][:, 2048:3072]
                if b == 0:
                    P.wait("vector", self.MM, lastu.get(("wq", par), 0))
                    self.prep_transposed(wq, 0, self.io["L1_attn_Wqkv"],
                                         128, D, row0=p * 128, dst_stride=128)
                    self.prep_transposed(wk, 0, self.io["L1_attn_Wqkv"],
                                         128, D, row0=1024 + p * 128,
                                         dst_stride=128)
                    self.prep_transposed(wv, 0, self.io["L1_attn_Wqkv"],
                                         128, D, row0=2048 + p * 128,
                                         dst_stride=128)
                    if p == 0:
                        scr_qkv = self.dram_scratch("scr_qkv",
                                                    (NH // 2, 128, 3072))
                        self.scr_qkv = scr_qkv
                    P.wait("sync", self.EV)
                    P.dma("sync", self.scr_qkv[p], wqb[par][:, 0:3072],
                          self.ST)
                else:
                    P.wait("sync", self.MM, lastu.get(("wq", par), 0))
                    P.dma("sync", wqb[par][:, 0:3072], self.scr_qkv[p],
                          self.LS)
                for (wtile, dstb, bcols) in ((wq, Qb, bq_cols),
                                             (wk, Kb, bk_cols)):
                    P.wait("tensor", self.LS)
                    P.wait("tensor", self.ST)
                    P.wait("tensor", self.EV)
                    for k in range(DX):
                        for n in range(4):
                            P.op("tensor", "matmul",
                                 self.pA[:, n * 512:(n + 1) * 512],
                                 wtile[:, k * 128:(k + 1) * 128],
                                 self.resid[:, k * NT + bc + n * 512:
                                            k * NT + bc + n * 512 + 512],
                                 start=(k == 0), stop=(k == DX - 1),
                                 inc=self.MM if (k == DX - 1 and n == 3)
                                 else None)
                    lastu[("wq", par)] = self.MM.n
                    P.wait("vector", self.MM)
                    P.op("vector", "tensor_scalar_add", dstb[:, 0:2048],
                         self.pA[:, 0:2048], bcols[:, p:p + 1], inc=self.EV)
                P.wait("gpsimd", self.MM, lastu.get("rowb", 0))
                P.dma("gpsimd", rowb[0:1, 0:128],
                      self.io["L1_attn_bqkv"][:, 2048 + p * 128:
                                              2048 + (p + 1) * 128], self.LD)
                P.wait("tensor", self.EV)
                P.wait("tensor", self.LD)
                for tg in range(4):
                    pvb = self.pB[:, (tg % 2) * 512:(tg % 2) * 512 + 512]
                    P.wait("tensor", self.EV, lastu.get(("pv", tg % 2), 0))
                    for t2 in range(4):
                        tt = tg * 4 + t2
                        pv = pvb[:, t2 * 128:(t2 + 1) * 128]
                        for k in range(DX):
                            P.op("tensor", "matmul", pv,
                                 self.resid[:, k * NT + bc + tt * 128:
                                            k * NT + bc + (tt + 1) * 128],
                                 wv[:, k * 128:(k + 1) * 128],
                                 start=(k == 0), stop=False)
                        P.op("tensor", "matmul", pv,
                             self.sb["c_ones_row_b"][0:1, :],
                             rowb[0:1, 0:128],
                             start=False, stop=True,
                             inc=self.MM if t2 == 3 else None)
                    mmc = self.MM.n
                    P.wait("vector", self.MM, mmc)
                    t0c = tg * 4
                    P.op("vector", "tensor_copy",
                         Vb2[:, t0c * 130:(t0c + 4) * 130].rearrange(
                             "p (t c) -> p t c", t=4)[:, :, 0:64],
                         pvb.rearrange("p (t c) -> p t c", t=4)[:, :, 0:64],
                         inc=self.EV)
                    P.op("vector", "tensor_copy",
                         Vb2[:, t0c * 130:(t0c + 4) * 130].rearrange(
                             "p (t c) -> p t c", t=4)[:, :, 65:129],
                         pvb.rearrange("p (t c) -> p t c", t=4)[:, :, 64:128],
                         inc=self.EV)
                    lastu[("pv", tg % 2)] = self.EV.n
                lastu["rowb"] = self.MM.n
                lastu[("wq", par)] = self.MM.n
                for (buf, ncol) in ((Qb, qn_col), (Kb, kn_col)):
                    P.wait("scalar", self.EV)
                    P.op("scalar", "activation", sqs[:, 0:2048], buf[:, 0:2048],
                         AF.Square, inc=self.AC)
                    P.wait("tensor", self.AC)
                    P.wait("tensor", self.EV)
                    for n in range(4):
                        P.op("tensor", "matmul",
                             self.pA[0:2, n * 512:(n + 1) * 512], blockones[:],
                             sqs[:, n * 512:(n + 1) * 512],
                             start=True, stop=True,
                             inc=self.MM if n == 3 else None)
                    P.wait("vector", self.MM)
                    P.op("vector", "tensor_scalar_mul", rows2[0:2, 0:2048],
                         self.pA[0:2, 0:2048], 1.0 / HD, inc=self.EV)
                    P.wait("scalar", self.EV)
                    P.op("scalar", "activation", rows2[0:2, 0:2048],
                         rows2[0:2, 0:2048], AF.Ln, bias=eps6r, inc=self.AC)
                    P.wait("scalar", self.AC, self.AC.n)
                    P.op("scalar", "activation", rows2[0:2, 0:2048],
                         rows2[0:2, 0:2048], AF.Exp, scale=-0.5, inc=self.AC)
                    P.wait("tensor", self.AC)
                    for n in range(4):
                        P.op("tensor", "matmul",
                             self.pA[:, n * 512:(n + 1) * 512],
                             blockonesT[0:2, :], rows2[0:2, n * 512:
                                                       (n + 1) * 512],
                             start=True, stop=True,
                             inc=self.MM if n == 3 else None)
                    P.wait("vector", self.MM)
                    P.op("vector", "scalar_tensor_tensor", buf[:, 0:2048],
                         buf[:, 0:2048], ncol, self.pA[:, 0:2048],
                         ALU.mult, ALU.mult, inc=self.EV)
                self.barrier()
                self._attn_ptb = ptb
                self.attn_heads(
                    Qb, Kb, Vb2, causal=False, mrow=sqs[0:1, 0:2048],
                    out_fn=lambda h, c, p=p: avT[64 * h:64 * h + 64,
                                                 p * 2048 + c * 512:
                                                 p * 2048 + c * 512 + 512])
                self.barrier()
            for n in range(4):
                for m in range(DX):
                    par = m % 2
                    pc = self.pB[:, par * 512:par * 512 + 512]
                    P.wait("tensor", self.EV, lastu.get(("pprj", par), 0))
                    for k in range(DX):
                        P.op("tensor", "matmul", pc,
                             wpT[:, k * 1024 + m * 128:k * 1024 + m * 128
                                 + 128],
                             avT[:, k * 2048 + n * 512:k * 2048 + n * 512
                                 + 512],
                             start=(k == 0), stop=(k == DX - 1),
                             inc=self.MM if k == DX - 1 else None)
                    P.wait("vector", self.MM)
                    P.op("vector", "scalar_tensor_tensor",
                         self.sb["lnsum"][:, m * 512:(m + 1) * 512], pc,
                         bp_cols[:, m:m + 1],
                         self.resid[:, m * NT + bc + n * 512:
                                    m * NT + bc + n * 512 + 512],
                         ALU.add, ALU.add, inc=self.EV)
                    lastu[("pprj", par)] = self.EV.n
                self.barrier()
                self.postln_chunk(bc + n * 512, None, presummed=True)
                self.barrier()


    def slot3(self):
        P = self.P
        al = self.arena_alloc()
        wqb = [al.take(3072)]
        Qb = al.take(2048)
        Kb = al.take(2048)
        Vb2 = al.take(16 * 130)
        shf = al.take(2048)
        sqs = shf
        ptb = al.take(2048)
        ropc = al.take(2048)
        rops = al.take(2048)
        avstg = ptb
        woT = self.arena[:, 0:8192]
        projstg = self.arena[:, 8192:12288]
        av_scr = self.dram_scratch("scr_av", (DX, 128, 2048))
        h_scr = self.dram_scratch("scr_h", (DX, 128, NT))
        fc = self.sb["fcols"]
        m_all = fc[:, 0:16]
        den_all = fc[:, 16:32]
        g_all = fc[:, 32:48]
        nscal = fc[:, 48:49]
        wn1c = fc[:, 49:57]
        wn2c = fc[:, 57:65]
        rstd_tt = fc[:, 65:97]
        rows2 = self.sb["rows2"]
        rows = self.sb["rows"]
        t1f = self.sb["t1f"]
        ones1f = self.sb["c_ones_row"]
        onescb = self.sb["c_ones_col_b"]
        n8i = self.sb["c_neg8ident"]
        identf = self.sb["c_ident_f"]
        mask_st = self.sb["c_mask_st"]
        mask_ts = self.sb["c_mask_ts"]
        iota8 = self.sb["c_iota8"]
        eps6 = self.sb["c_eps"][0:1, 1:2]
        rowbuf = self.sb["rowbuf"]

        P.dma("gpsimd", ropc, self.io["c_rope_cos"], self.LD)
        P.dma("gpsimd", rops, self.io["c_rope_sin"], self.LD)
        self.load_bias_cols(wn1c, self.io["L1_ffn_attn_norm_w"], DX)
        self.load_bias_cols(wn2c, self.io["L1_ffn_moe_norm_w"], DX)
        scr_a = self.dram_scratch("scr_l1a", (NH // 2, 128, 3072))
        self.barrier()
        lastu = {}

        for b in range(B):
            bc = b * T
            for n in range(4):
                P.wait("scalar", self.EV)
                P.wait("scalar", self.MM)
                for dXi in range(DX):
                    P.op("scalar", "activation", sqs[:, 0:512],
                         self.resid[:, dXi * NT + bc + n * 512:
                                    dXi * NT + bc + n * 512 + 512],
                         AF.Square, inc=self.AC)
                    P.wait("tensor", self.AC)
                    P.op("tensor", "matmul", self.pR[0:1, 0:512], onescb[:],
                         sqs[:, 0:512], start=(dXi == 0), stop=(dXi == DX - 1),
                         inc=self.MM)
                    if dXi < DX - 1:
                        P.wait("scalar", self.MM)
                P.wait("vector", self.MM)
                P.op("vector", "tensor_scalar_mul", rows[0:1, 0:512],
                     self.pR[0:1, 0:512], 1.0 / D, inc=self.EV)
                P.wait("scalar", self.EV)
                P.op("scalar", "activation", rows[0:1, 0:512],
                     rows[0:1, 0:512], AF.Sqrt, bias=eps6, inc=self.AC)
                P.wait("vector", self.AC)
                P.op("vector", "reciprocal", rows[0:1, 0:512],
                     rows[0:1, 0:512], inc=self.EV)
                P.wait("tensor", self.EV)
                for i in range(4):
                    P.op("tensor", "matmul", self.pR[:, 4 + i:5 + i],
                         rows[0:1, i * 128:(i + 1) * 128], ones1f[0:1, 0:1],
                         start=True, stop=True,
                         inc=self.MM if i == 3 else None)
                P.wait("vector", self.MM)
                P.op("vector", "tensor_copy",
                     rstd_tt[:, b * 16 + n * 4:b * 16 + n * 4 + 4],
                     self.pR[:, 4:8], inc=self.EV)
                P.wait("tensor", self.EV)
            P.wait("vector", self.LD)
            for tt in range(16):
                for hh in range(2):
                    P.op("vector", "tensor_copy",
                         Vb2[:, tt * 130 + hh * 65 + 64:
                             tt * 130 + hh * 65 + 65],
                         self.sb["c_ones_col_b"][:],
                         inc=self.EV if (tt == 15 and hh == 1) else None)
            self.barrier()

            for p in range(NH // 2):
                par = 0
                wq = wqb[par][:, 0:1024]
                wk = wqb[par][:, 1024:2048]
                wv = wqb[par][:, 2048:3072]
                if b == 0:
                    P.wait("vector", self.MM, lastu.get(("w3a", par), 0))
                    for (wname, dst) in (("L1_ffn_wq", wq), ("L1_ffn_wk", wk),
                                         ("L1_ffn_wv", wv)):
                        if wname == "L1_ffn_wv":
                            self.prep_transposed(dst, 0, self.io[wname],
                                                 128, D, row0=p * 128,
                                                 dst_stride=128,
                                                 scale_col=lambda kX:
                                                 wn1c[:, kX:kX + 1])
                        else:
                            for blk in range(8):
                                hh = blk // 4
                                sub = ROPE_PERM[(blk % 4) * 16:
                                                (blk % 4) * 16 + 16]
                                r0 = int(sub[0])
                                self.prep_16rows(dst, blk * 16,
                                                 self.io[wname],
                                                 p * 128 + hh * 64 + r0,
                                                 wn1c)
                    P.wait("sync", self.EV)
                    P.dma("sync", scr_a[p], wqb[par][:, 0:3072], self.ST)
                else:
                    P.wait("sync", self.MM, lastu.get(("w3a", par), 0))
                    P.dma("sync", wqb[par][:, 0:3072], scr_a[p], self.LS)
                for (wtile, dstb) in ((wq, Qb), (wk, Kb)):
                    P.wait("tensor", self.LS)
                    P.wait("tensor", self.ST)
                    P.wait("tensor", self.EV)
                    for k in range(DX):
                        for n in range(4):
                            P.op("tensor", "matmul",
                                 self.pA[:, n * 512:(n + 1) * 512],
                                 wtile[:, k * 128:(k + 1) * 128],
                                 self.resid[:, k * NT + bc + n * 512:
                                            k * NT + bc + n * 512 + 512],
                                 start=(k == 0), stop=(k == DX - 1),
                                 inc=self.MM if (k == DX - 1 and n == 3)
                                 else None)
                    P.wait("vector", self.MM)
                    P.op("vector", "tensor_copy", dstb[:, 0:2048],
                         self.pA[:, 0:2048], inc=self.EV)
                for n in range(4):
                    P.wait("tensor", self.EV)
                    for i in range(4):
                        P.op("tensor", "matmul",
                             self.pR[0:1, i * 128:(i + 1) * 128],
                             rstd_tt[:, b * 16 + n * 4 + i:
                                     b * 16 + n * 4 + i + 1], identf[:],
                             start=True, stop=True,
                             inc=self.MM if i == 3 else None)
                    P.wait("vector", self.MM)
                    P.op("vector", "tensor_copy", rows[0:1, 512:1024],
                         self.pR[0:1, 0:512], inc=self.EV)
                    P.wait("tensor", self.EV)
                    P.op("tensor", "matmul",
                         self.pA[:, n * 512:(n + 1) * 512], ones1f[0:1, :],
                         rows[0:1, 512:1024],
                         start=True, stop=True,
                         inc=self.MM if n == 3 else None)
                P.wait("vector", self.MM)
                for dstb in (Qb, Kb):
                    P.op("vector", "tensor_mul", dstb[:, 0:2048],
                         dstb[:, 0:2048], self.pA[:, 0:2048], inc=self.EV)
                for dstb in (Qb, Kb):
                    P.wait("vector", self.EV)
                    P.op("vector", "stream_shuffle", shf[:, 0:2048],
                         dstb[:, 0:2048], ROPE_SHUF_MASK, inc=self.EV)
                    P.wait("vector", self.EV)
                    P.op("vector", "tensor_mul", shf[:, 0:2048], shf[:, 0:2048],
                         rops, inc=self.EV)
                    P.wait("vector", self.EV)
                    P.op("vector", "tensor_mul", dstb[:, 0:2048],
                         dstb[:, 0:2048], ropc, inc=self.EV)
                    P.wait("vector", self.EV)
                    P.op("vector", "tensor_add", dstb[:, 0:2048],
                         dstb[:, 0:2048], shf[:, 0:2048], inc=self.EV)
                P.wait("tensor", self.EV)
                for tg in range(4):
                    pvb = self.pB[:, (tg % 2) * 512:(tg % 2) * 512 + 512]
                    P.wait("tensor", self.EV, lastu.get(("pv3", tg % 2), 0))
                    for t2 in range(4):
                        tt = tg * 4 + t2
                        pv = pvb[:, t2 * 128:(t2 + 1) * 128]
                        for k in range(DX):
                            P.op("tensor", "matmul", pv,
                                 self.resid[:, k * NT + bc + tt * 128:
                                            k * NT + bc + (tt + 1) * 128],
                                 wv[:, k * 128:(k + 1) * 128],
                                 start=(k == 0), stop=(k == DX - 1),
                                 inc=self.MM if (k == DX - 1 and t2 == 3)
                                 else None)
                    mmc = self.MM.n
                    P.wait("vector", self.MM, mmc)
                    for t2 in range(tg * 4, tg * 4 + 4):
                        P.op("vector", "tensor_scalar_mul",
                             Vb2[:, t2 * 130:t2 * 130 + 64],
                             pvb[:, (t2 % 4) * 128:(t2 % 4) * 128 + 64],
                             rstd_tt[:, b * 16 + t2:b * 16 + t2 + 1])
                        P.op("vector", "tensor_scalar_mul",
                             Vb2[:, t2 * 130 + 65:t2 * 130 + 129],
                             pvb[:, (t2 % 4) * 128 + 64:
                                 (t2 % 4) * 128 + 128],
                             rstd_tt[:, b * 16 + t2:b * 16 + t2 + 1],
                             inc=self.EV if t2 % 4 == 3 else None)
                    lastu[("pv3", tg % 2)] = self.EV.n
                lastu[("w3a", par)] = self.MM.n
                self.barrier()
                fcb = self.sb["fcols"]
                rows_ = self.sb["rows"]
                blockones = self.sb["c_blockones"]
                for (buf, dst) in ((Qb, fcb[0:2, 22:23]), (Kb, fcb[0:2, 23:24])):
                    P.wait("scalar", self.EV)
                    P.wait("scalar", self.MM)
                    P.op("scalar", "activation", shf[:, 0:2048], buf[:, 0:2048],
                         AF.Square, inc=self.AC)
                    P.wait("tensor", self.AC)
                    P.wait("tensor", self.EV)
                    for n in range(4):
                        P.op("tensor", "matmul",
                             self.pA[0:2, n * 512:(n + 1) * 512], blockones[:],
                             shf[:, n * 512:(n + 1) * 512],
                             start=True, stop=True,
                             inc=self.MM if n == 3 else None)
                    P.wait("vector", self.MM)
                    P.op("vector", "reduce_max", dst, self.pA[0:2, 0:2048],
                         axis=AX.X, inc=self.EV)
                P.wait("vector", self.EV, self.EV.n)
                P.op("vector", "tensor_mul", fcb[0:2, 24:25], fcb[0:2, 22:23],
                     fcb[0:2, 23:24], inc=self.EV)
                P.wait("scalar", self.EV)
                P.op("scalar", "activation", fcb[0:2, 24:25], fcb[0:2, 24:25],
                     AF.Ln, inc=self.AC)
                P.wait("scalar", self.AC, self.AC.n)
                P.op("scalar", "activation", fcb[0:2, 24:25], fcb[0:2, 24:25],
                     AF.Exp, scale=0.5, inc=self.AC)
                P.wait("vector", self.AC)
                P.op("vector", "tensor_scalar_mul", fcb[0:2, 24:25],
                     fcb[0:2, 24:25], -0.125, inc=self.EV)
                P.wait("tensor", self.EV)
                P.op("tensor", "matmul", self.pR[0:1, 0:2], fcb[0:2, 24:25],
                     self.sb["c_ident_f"][0:2, 0:2], start=True, stop=True,
                     inc=self.MM)
                P.wait("vector", self.MM)
                P.op("vector", "tensor_copy", rows_[0:1, 0:2],
                     self.pR[0:1, 0:2], inc=self.EV)
                P.wait("tensor", self.EV)
                for hh in range(2):
                    P.op("tensor", "matmul", self.pR[:, 4 + hh:5 + hh],
                         self.sb["c_ones_row"][0:1, :],
                         rows_[0:1, hh:hh + 1], start=True, stop=True,
                         inc=self.MM if hh == 1 else None)
                P.wait("vector", self.MM)
                P.op("vector", "tensor_copy", fcb[:, 25:27], self.pR[:, 4:6],
                     inc=self.EV)
                P.wait("tensor", self.EV)
                P.wait("scalar", self.EV)
                self._attn_ptb = ptb

                def s3_out(h, c, p=p):
                    return shf[64 * h:64 * h + 64, c * 512:(c + 1) * 512]

                def s3_post(h, c, p=p, b=b):
                    lu = self._attn_lu
                    P.wait("sync", self.EV, self.EV.n)
                    P.dma("sync", av_scr[p, 64 * h:64 * h + 64,
                                         c * 512:(c + 1) * 512],
                          shf[64 * h:64 * h + 64, c * 512:(c + 1) * 512],
                          self.ST)
                    lu[("avsfree", h)] = (self.ST, self.ST.n)

                self.attn_heads(Qb, Kb, Vb2, causal=True, mrow=None,
                                out_fn=s3_out, post_fn=s3_post,
                                bound_cols=(fcb[:, 25:26], fcb[:, 26:27]))
                self.barrier()
            self.prep_transposed(woT, 0, self.io["L1_ffn_wo"], D, D)
            self.barrier()
            for n in range(4):
                P.wait("sync", self.MM, lastu.get("projstg", 0))
                for k in range(DX):
                    P.dma("sync", projstg[:, k * 512:(k + 1) * 512],
                          av_scr[k][:, n * 512:(n + 1) * 512], self.LS)
                for m in range(DX):
                    par = m % 2
                    pc = self.pB[:, par * 512:par * 512 + 512]
                    P.wait("tensor", self.LS)
                    P.wait("tensor", self.EV, lastu.get(("pprj3", par), 0))
                    for k in range(DX):
                        P.op("tensor", "matmul", pc,
                             woT[:, k * 1024 + m * 128:k * 1024 + m * 128
                                 + 128],
                             projstg[:, k * 512:(k + 1) * 512],
                             start=(k == 0), stop=(k == DX - 1),
                             inc=self.MM if k == DX - 1 else None)
                    if m == DX - 1:
                        lastu["projstg"] = self.MM.n
                    P.wait("vector", self.MM)
                    if m == 0:
                        P.wait("vector", self.ST)
                    P.op("vector", "tensor_add",
                         self.sb["lnsum"][:, m * 512:(m + 1) * 512], pc,
                         self.resid[:, m * NT + bc + n * 512:
                                    m * NT + bc + n * 512 + 512],
                         inc=self.EV)
                    lastu[("pprj3", par)] = self.EV.n
                    if m == DX - 1:
                        P.wait("sync", self.EV)
                        P.dma("sync",
                              h_scr[:, :, bc + n * 512:bc + n * 512 + 512]
                              .transpose([1, 0, 2]),
                              self.sb["lnsum"][:, 0:DX * 512].rearrange(
                                  "p (m c) -> p m c", m=DX), self.ST)
            self.barrier()

        al2 = self.arena_alloc()
        hid = self.hbuf[:, 0:HX * 512]
        tn2 = self.hbuf[:, HX * 512:HX * 512 + 4096]
        wb3 = [self.hbuf[:, HX * 512 + 4096:HX * 512 + 8192],
               self.hbuf[:, HX * 512 + 8192:HX * 512 + 12288]]
        hgb = self.hbuf[:, HX * 512 + 12288:HX * 512 + 16384]
        gateT = al2.take(64)
        b2row = al2.take(1024)[0:1, :]
        b1a = fc[:, 256:512]
        logt = fc[:, 129:161]
        top2 = fc[:, 161:169]
        fidx = fc[:, 169:171]
        rw12 = fc[:, 171:173]
        msk = fc[:, 173:181]
        w8s = fc[:, 181:213]
        idxu = self.sbuf("idxu", [128, 8], U32)
        scr_m1 = self.dram_scratch("scr_m1", (NE, HX, 128, 1024))
        scr_m2 = self.dram_scratch("scr_m2", (NE, DX, 128, 4096))
        wtmp = self.sb["wtmp"]
        identb = self.sb["c_ident_b"]
        P.wait("vector", self.ST)
        P.wait("gpsimd", self.MM)
        P.dma("gpsimd", wtmp[0:NE, 0:D], self.io["L1_ffn_gate_w"], self.LD)
        P.wait("tensor", self.LD)
        P.wait("tensor", self.EV)
        for kX in range(DX):
            P.op("tensor", "transpose", self.pT[:, kX * 8:(kX + 1) * 8],
                 wtmp[0:NE, kX * 128:(kX + 1) * 128], identb[0:NE, 0:NE],
                 inc=self.MM if kX == DX - 1 else None)
        P.wait("vector", self.MM)
        for kX in range(DX):
            P.op("vector", "tensor_scalar_mul", gateT[:, kX * 8:(kX + 1) * 8],
                 self.pT[:, kX * 8:(kX + 1) * 8], wn2c[:, kX:kX + 1],
                 inc=self.EV if kX == DX - 1 else None)
        for e in range(NE):
            self.load_bias_cols(b1a[:, e * 32:(e + 1) * 32],
                                self.io["L1_ffn_b1"][e:e + 1, :], 32)
        self.barrier()

        for g in range(NT // 512):
            gc = g * 512
            P.wait("sync", self.EV, lastu.get("hgb", 0))
            P.wait("sync", self.AC, lastu.get("hgb_a", 0))
            P.dma("sync", hgb[:, 0:DX * 512].rearrange("p (m c) -> p m c",
                                                       m=DX),
                  h_scr[:, :, gc:gc + 512].transpose([1, 0, 2]), self.LS)
            P.wait("scalar", self.LS)
            P.wait("vector", self.LS)
            for dXi in range(DX):
                P.wait("scalar", self.EV)
                P.wait("scalar", self.MM)
                P.op("scalar", "activation", sqs[:, 0:512],
                     hgb[:, dXi * 512:(dXi + 1) * 512],
                     AF.Square, inc=self.AC)
                P.wait("tensor", self.AC)
                P.op("tensor", "matmul", self.pR[0:1, 0:512], onescb[:],
                     sqs[:, 0:512], start=(dXi == 0), stop=(dXi == DX - 1),
                     inc=self.MM)
            P.wait("vector", self.MM)
            P.op("vector", "tensor_scalar_mul", rows[0:1, 0:512],
                 self.pR[0:1, 0:512], 1.0 / D, inc=self.EV)
            P.wait("scalar", self.EV)
            P.op("scalar", "activation", rows[0:1, 0:512], rows[0:1, 0:512],
                 AF.Sqrt, bias=eps6, inc=self.AC)
            P.wait("vector", self.AC)
            P.op("vector", "reciprocal", rows[0:1, 0:512], rows[0:1, 0:512],
                 inc=self.EV)
            P.wait("tensor", self.EV)
            P.op("tensor", "matmul", self.pA[:, 0:512], ones1f[0:1, :],
                 rows[0:1, 0:512], start=True, stop=True, inc=self.MM)
            P.wait("vector", self.MM)
            for dXi in range(DX):
                P.op("vector", "tensor_mul",
                     tn2[:, dXi * 512:(dXi + 1) * 512],
                     hgb[:, dXi * 512:(dXi + 1) * 512],
                     self.pA[:, 0:512], inc=self.EV)
            P.wait("tensor", self.EV)
            for k in range(DX):
                P.op("tensor", "matmul", self.pB[0:NE, 0:512],
                     gateT[:, k * NE:(k + 1) * NE],
                     tn2[:, k * 512:(k + 1) * 512],
                     start=(k == 0), stop=(k == DX - 1), inc=self.MM
                     if k == DX - 1 else None)
            P.wait("vector", self.MM)
            P.op("vector", "tensor_copy", t1f[0:NE, 0:512],
                 self.pB[0:NE, 0:512], inc=self.EV)
            P.wait("tensor", self.EV)
            for tt in range(4):
                P.op("tensor", "transpose", self.pA[:, tt * 8:(tt + 1) * 8],
                     t1f[0:NE, tt * 128:(tt + 1) * 128], identf[0:NE, 0:NE],
                     inc=self.MM if tt == 3 else None)
            P.wait("vector", self.MM)
            P.op("vector", "tensor_copy", logt, self.pA[:, 0:32], inc=self.EV)
            for tt in range(4):
                lg = logt[:, tt * 8:(tt + 1) * 8]
                P.wait("vector", self.EV)
                P.op("vector", "max", top2, lg, inc=self.EV)
                P.wait("vector", self.EV)
                P.op("vector", "max_index", idxu[:], top2, lg, inc=self.EV)
                P.wait("vector", self.EV)
                P.op("vector", "tensor_copy", fidx, idxu[:, 0:2], inc=self.EV)
                P.wait("vector", self.EV)
                P.op("vector", "tensor_sub", rw12[:, 0:1], top2[:, 1:2],
                     top2[:, 0:1], inc=self.EV)
                P.wait("scalar", self.EV)
                P.op("scalar", "activation", rw12[:, 0:1], rw12[:, 0:1],
                     AF.Sigmoid, scale=-1.0, inc=self.AC)
                P.wait("vector", self.AC)
                P.op("vector", "tensor_scalar", rw12[:, 1:2], rw12[:, 0:1],
                     -1.0, 1.0, ALU.mult, ALU.add, inc=self.EV)
                P.wait("vector", self.EV)
                P.op("vector", "tensor_scalar", msk, iota8[:],
                     fidx[:, 0:1], None, ALU.is_equal, inc=self.EV)
                P.wait("vector", self.EV)
                P.op("vector", "tensor_scalar_mul",
                     w8s[:, tt * 8:(tt + 1) * 8], msk, rw12[:, 0:1],
                     inc=self.EV)
                P.wait("vector", self.EV)
                P.op("vector", "tensor_scalar", msk, iota8[:],
                     fidx[:, 1:2], None, ALU.is_equal, inc=self.EV)
                P.wait("vector", self.EV)
                P.op("vector", "scalar_tensor_tensor",
                     w8s[:, tt * 8:(tt + 1) * 8], msk, rw12[:, 1:2],
                     w8s[:, tt * 8:(tt + 1) * 8], ALU.mult, ALU.add,
                     inc=self.EV)
            self.barrier()
            for e in range(NE):
                par = e % 2
                P.wait("tensor", self.EV)
                for tt in range(4):
                    P.op("tensor", "matmul",
                         self.pR[0:1, tt * 128:(tt + 1) * 128],
                         w8s[:, tt * 8 + e:tt * 8 + e + 1], identf[:],
                         start=True, stop=True,
                         inc=self.MM if tt == 3 else None)
                P.wait("vector", self.MM)
                P.op("vector", "tensor_copy", rows[0:1, 512:1024],
                     self.pR[0:1, 0:512], inc=self.EV)
                P.wait("tensor", self.EV)
                P.op("tensor", "matmul", self.pA[:, 1536:2048],
                     ones1f[0:1, :], rows[0:1, 512:1024],
                     start=True, stop=True, inc=self.MM)
                P.wait("vector", self.MM)
                P.op("vector", "tensor_copy", t1f[:], self.pA[:, 1536:2048],
                     inc=self.EV)
                P.wait("gpsimd", self.MM)
                P.dma("gpsimd", b2row,
                      self.io["L1_ffn_b2"][0:1, e * 1024:(e + 1) * 1024],
                      self.LD)
                P.wait("scalar", self.EV)
                P.wait("vector", self.EV)
                for mg in range(HX // 4):
                    wap = wb3[par][:, 0:4096]
                    if g == 0:
                        P.wait("vector", self.MM,
                               lastu.get(("wm1", par), 0))
                        for mm_ in range(4):
                            self.prep_transposed(
                                wap[:, mm_ * 1024:(mm_ + 1) * 1024], 0,
                                self.io["L1_ffn_w1"][e],
                                128, D, row0=(mg * 4 + mm_) * 128,
                                dst_stride=128,
                                scale_col=lambda kX: wn2c[:, kX:kX + 1])
                        P.wait("sync", self.EV)
                        P.dma("sync",
                              scr_m1[e, mg * 4:(mg + 1) * 4].transpose([1, 0, 2]),
                              wap.rearrange("p (m c) -> p m c", m=4), self.ST)
                    else:
                        P.wait("sync", self.MM, lastu.get(("wm1", par), 0))
                        P.dma("sync",
                              wap.rearrange("p (m c) -> p m c", m=4),
                              scr_m1[e, mg * 4:(mg + 1) * 4].transpose([1, 0, 2]),
                              self.LS)
                    P.wait("tensor", self.LS)
                    for mm_ in range(4):
                        m = mg * 4 + mm_
                        ph = self.pB[:, (m % 2) * 512:(m % 2) * 512 + 512]
                        P.wait("tensor", self.AC,
                               lastu.get(("ph3", m % 2), 0))
                        for k in range(DX):
                            P.op("tensor", "matmul", ph,
                                 wap[:, mm_ * 1024 + k * 128:
                                     mm_ * 1024 + (k + 1) * 128],
                                 tn2[:, k * 512:(k + 1) * 512],
                                 start=(k == 0), stop=(k == DX - 1),
                                 inc=self.MM if k == DX - 1 else None)
                        P.wait("scalar", self.MM)
                        P.op("scalar", "activation",
                             hid[:, m * 512:(m + 1) * 512], ph, AF.Gelu,
                             bias=b1a[:, e * 32 + m:e * 32 + m + 1],
                             inc=self.AC)
                        lastu[("ph3", m % 2)] = self.AC.n
                    lastu[("wm1", par)] = self.MM.n
                    par = (par + 1) % 2
                P.wait("tensor", self.LD)
                P.wait("tensor", self.AC)
                for dXi in range(DX):
                    wap = wb3[par][:, 0:4096]
                    if g == 0:
                        P.wait("vector", self.MM,
                               lastu.get(("wm1", par), 0))
                        self.prep_transposed(wap, 0,
                                             self.io["L1_ffn_w2"][e],
                                             128, HID, row0=dXi * 128,
                                             dst_stride=128)
                        P.wait("sync", self.EV)
                        P.dma("sync", scr_m2[e, dXi], wap, self.ST)
                    else:
                        P.wait("sync", self.MM, lastu.get(("wm1", par), 0))
                        P.dma("sync", wap, scr_m2[e, dXi], self.LS)
                    pz = self.pA[:, (dXi % 2) * 512:(dXi % 2) * 512 + 512]
                    P.wait("tensor", self.LS)
                    P.wait("tensor", self.EV, lastu.get(("pz3", dXi % 2), 0))
                    for k in range(HX):
                        P.op("tensor", "matmul", pz,
                             wap[:, k * 128:(k + 1) * 128],
                             hid[:, k * 512:(k + 1) * 512],
                             start=(k == 0), stop=False)
                    P.op("tensor", "matmul", pz,
                         b2row[0:1, dXi * 128:(dXi + 1) * 128],
                         self.sb["c_ones512b"][0:1, :],
                         start=False, stop=True, inc=self.MM)
                    lastu[("wm1", par)] = self.MM.n
                    P.wait("vector", self.MM)
                    P.op("vector", "tensor_mul",
                         self.sb["lnsum"][:, 0:512], pz, t1f[:], inc=self.EV)
                    lastu[("pz3", dXi % 2)] = self.EV.n
                    P.wait("vector", self.EV)
                    P.op("vector", "tensor_add",
                         self.resid[:, dXi * NT + gc:dXi * NT + gc + 512],
                         self.resid[:, dXi * NT + gc:dXi * NT + gc + 512],
                         self.sb["lnsum"][:, 0:512], inc=self.EV)
                    P.wait("vector", self.EV)
                    par = (par + 1) % 2
                P.wait("scalar", self.MM)
                P.wait("vector", self.MM)
                P.wait("tensor", self.EV)
            self.postln_chunk(gc, lambda dXi: hgb[:, dXi * 512:
                                                  (dXi + 1) * 512])
            lastu["hgb"] = self.EV.n
            lastu["hgb_a"] = self.AC.n
            self.barrier()

    def prep_16rows(self, dst, dst_sub0, src_ap, row0, wcol):
        P = self.P
        wtmp = self.sb["wtmp"]
        identb = self.sb["c_ident_b"]
        P.wait("vector", self.ST)
        P.wait("gpsimd", self.MM)
        P.wait("gpsimd", self.EV)
        P.dma("gpsimd", wtmp[0:16, 0:D], src_ap[row0:row0 + 16, :], self.LD)
        P.wait("tensor", self.LD)
        P.wait("tensor", self.EV)
        for kX in range(DX):
            P.op("tensor", "transpose", self.pT[:, kX * 16:(kX + 1) * 16],
                 wtmp[0:16, kX * 128:(kX + 1) * 128], identb[0:16, 0:16],
                 inc=self.MM if kX == DX - 1 else None)
        P.wait("vector", self.MM)
        for kX in range(DX):
            P.op("vector", "tensor_scalar_mul",
                 dst[:, kX * 128 + dst_sub0:kX * 128 + dst_sub0 + 16],
                 self.pT[:, kX * 16:(kX + 1) * 16], wcol[:, kX:kX + 1],
                 inc=self.EV if kX == DX - 1 else None)
        P.wait("tensor", self.EV)


def build_program(slots):
    nc = bass.Bass("TRN2", target_bir_lowering=False)
    with contextlib.ExitStack() as ctx:
        mk = MK(nc, ctx, slots)
        mk.setup()
        mk.load_resid()
        for s in slots:
            getattr(mk, f"slot{s}")()
        mk.store_out()
        block = ctx.enter_context(nc.Block())
        mk.P.run(block)
    return nc


def shard_inputs(inputs, core):
    im = {}
    im["x_in"] = np.ascontiguousarray(inputs["L0_attn_x"][2 * core:2 * core + 2])
    return im


def kernel(**inputs):
    from concourse.bass_utils import run_bass_kernel_spmd
    consts = host_constants()
    slots_chain = [[0, 1, 2, 3]]
    x = np.asarray(inputs["L0_attn_x"], dtype=np.float32)
    cur = x
    for slots in slots_chain:
        nc = build_program(slots)
        in_maps = []
        for core in range(8):
            im = {"x_in": np.ascontiguousarray(cur[2 * core:2 * core + 2])}
            for s in slots:
                for nm in SLOT_WEIGHTS[s]:
                    im[nm] = np.ascontiguousarray(
                        np.asarray(inputs[nm], dtype=np.float32)
                        .reshape(WEIGHT_SHAPES[nm]))
            cns = sorted({n for s in slots for n in CONST_NAMES[s]})
            for nm in cns:
                im[nm] = consts[nm]
            in_maps.append(im)
        res = run_bass_kernel_spmd(nc, in_maps, core_ids=list(range(8)))
        cur = np.concatenate([np.asarray(r["x_out"]) for r in res.results],
                             axis=0)
    return cur


# version 08
# speedup vs baseline: 1.2012x; 1.0076x over previous
import contextlib
import numpy as np
import concourse.bass as bass
import concourse.mybir as mybir

F32 = mybir.dt.float32
BF16 = mybir.dt.bfloat16
U32 = mybir.dt.uint32
AF = mybir.ActivationFunctionType
ALU = mybir.AluOpType
AX = mybir.AxisListType

B = 2
T = 2048
D = 1024
DX = D // 128
HID = 4096
HX = HID // 128
NH = 16
HD = 64
NE = 8
NT = B * T

ENGINES = ("tensor", "vector", "scalar", "sync", "gpsimd")

AB_COLS = 17856


class _Alloc:
    def __init__(self, arena, cols):
        self.arena = arena
        self.cols = cols
        self.off = 0

    def take(self, ncols):
        assert self.off + ncols <= self.cols, \
            f"arena overflow: {self.off}+{ncols} > {self.cols}"
        ap = self.arena[:, self.off:self.off + ncols]
        self.off += ncols
        return ap


class Sem:
    def __init__(self, h, name):
        self.h = h
        self.name = name
        self.n = 0


class Prog:

    def __init__(self, nc, ctx):
        self.nc = nc
        self.ctx = ctx
        self.q = {e: [] for e in ENGINES}
        self.sems = {}

    def sem(self, name):
        s = Sem(self.ctx.enter_context(self.nc.semaphore(name)), name)
        self.sems[name] = s
        return s

    def emit(self, eng, fn):
        self.q[eng].append(fn)

    def op(self, eng, name, *args, inc=None, incv=1, **kw):
        if inc is not None:
            inc.n += incv
        def f(e, name=name, args=args, kw=kw, inc=inc, incv=incv):
            ins = getattr(e, name)(*args, **kw)
            if inc is not None:
                ins.then_inc(inc.h, incv)
        self.emit(eng, f)

    def wait(self, eng, sem, val=None):
        v = sem.n if val is None else val
        if v <= 0:
            return
        self.emit(eng, lambda e, s=sem.h, v=v: e.wait_ge(s, v))

    def dma(self, eng, out, in_, sem, serialize=True):
        if serialize and sem.n > 0:
            self.wait(eng, sem, sem.n)
        self.op(eng, "dma_start", out=out, in_=in_, inc=sem, incv=16)

    def run(self, block):
        for eng in ENGINES:
            ops = self.q[eng]
            getattr(block, eng)(lambda e, ops=ops: [f(e) for f in ops])


ROPE_PERM = np.array(list(range(0, 16)) + list(range(32, 48))
                     + list(range(16, 32)) + list(range(48, 64)))
ROPE_SHUF_MASK = list(range(16, 32)) + list(range(16))


def host_constants():
    c = {}
    c["c_ident_f"] = np.eye(128, dtype=np.float32)
    c["c_eps"] = np.tile(np.array([[1e-5, 1e-6]], dtype=np.float32), (128, 1))
    c["c_ident_b"] = np.eye(128, dtype=np.float32)
    c["c_neg8ident"] = (-8.0 * np.eye(128)).astype(np.float32)
    c["c_ones_col"] = np.ones((128, 1), dtype=np.float32)
    c["c_ones_row"] = np.ones((1, 128), dtype=np.float32)
    c["c_ones_row512"] = np.ones((1, 512), dtype=np.float32)
    c["c_ones512b"] = np.ones((1, 512), dtype=np.float32)
    c["c_iota8"] = np.tile(np.arange(8, dtype=np.float32), (128, 1))
    tri = np.where(np.arange(128)[None, :] <= np.arange(128)[:, None],
                   np.float32(0), np.float32(-1e30))
    c["c_mask_st"] = tri.astype(np.float32)
    c["c_mask_ts"] = np.ascontiguousarray(tri.T).astype(np.float32)
    bo = np.zeros((128, 2), dtype=np.float32)
    bo[:64, 0] = 1.0
    bo[64:, 1] = 1.0
    c["c_blockones"] = bo
    c["c_blockonesT"] = np.ascontiguousarray(bo.T)
    hd = HD
    inv_freq = (1.0 / (10000.0 ** (np.arange(0, hd, 2, dtype=np.float64) / hd)))
    freqs = np.arange(T, dtype=np.float64)[:, None] * inv_freq[None, :]
    emb = np.concatenate([freqs, freqs], axis=1)
    cos = emb.T * 0 + np.cos(emb).T
    sin = np.sin(emb).T
    sign = np.where(np.arange(hd) < 32, -1.0, 1.0)
    cosP = cos[ROPE_PERM]
    sinS = (sin * sign[:, None])[ROPE_PERM]
    c["c_rope_cos"] = np.tile(cosP, (2, 1)).astype(np.float32)
    c["c_rope_sin"] = np.tile(sinS, (2, 1)).astype(np.float32)
    return c


WEIGHT_SHAPES = {
    "L0_attn_Wz": (D, D), "L0_attn_Wh": (D, D),
    "L0_attn_bz": (1, D), "L0_attn_bh": (1, D),
    "L0_ffn_w1_w": (HID, D), "L0_ffn_w1_s": (1, HID),
    "L0_ffn_w2_w": (HID, D), "L0_ffn_w2_s": (1, HID),
    "L0_ffn_h1_w": (HID, HID), "L0_ffn_h1_s": (1, HID),
    "L0_ffn_h2_w": (HID, HID), "L0_ffn_h2_s": (1, HID),
    "L0_ffn_w3_w": (D, HID), "L0_ffn_w3_s": (1, D),
    "L1_attn_Wqkv": (3 * D, D), "L1_attn_bqkv": (1, 3 * D),
    "L1_attn_Wproj": (D, D), "L1_attn_bproj": (1, D),
    "L1_attn_qn_w": (1, HD), "L1_attn_kn_w": (1, HD),
    "L1_ffn_attn_norm_w": (1, D),
    "L1_ffn_wq": (D, D), "L1_ffn_wk": (D, D),
    "L1_ffn_wv": (D, D), "L1_ffn_wo": (D, D),
    "L1_ffn_moe_norm_w": (1, D),
    "L1_ffn_gate_w": (NE, D),
    "L1_ffn_w1": (NE, HID, D), "L1_ffn_b1": (NE, HID),
    "L1_ffn_w2": (NE, D, HID), "L1_ffn_b2": (1, NE * D),
}

SLOT_WEIGHTS = {
    0: ["L0_attn_Wz", "L0_attn_Wh", "L0_attn_bz", "L0_attn_bh"],
    1: ["L0_ffn_w1_w", "L0_ffn_w1_s", "L0_ffn_w2_w", "L0_ffn_w2_s",
        "L0_ffn_h1_w", "L0_ffn_h1_s", "L0_ffn_h2_w", "L0_ffn_h2_s",
        "L0_ffn_w3_w", "L0_ffn_w3_s"],
    2: ["L1_attn_Wqkv", "L1_attn_bqkv", "L1_attn_Wproj", "L1_attn_bproj",
        "L1_attn_qn_w", "L1_attn_kn_w"],
    3: ["L1_ffn_attn_norm_w", "L1_ffn_wq", "L1_ffn_wk", "L1_ffn_wv",
        "L1_ffn_wo", "L1_ffn_moe_norm_w", "L1_ffn_gate_w",
        "L1_ffn_w1", "L1_ffn_b1", "L1_ffn_w2", "L1_ffn_b2"],
}

CONST_NAMES = {
    0: ["c_ident_f", "c_ident_b", "c_ones_col", "c_ones_row", "c_eps"],
    1: ["c_ident_f", "c_ident_b", "c_ones_col", "c_ones_row", "c_eps"],
    2: ["c_ident_f", "c_ident_b", "c_ones_col", "c_ones_row", "c_neg8ident",
        "c_blockones", "c_blockonesT", "c_eps", "c_mask_st", "c_mask_ts"],
    3: ["c_ident_f", "c_ident_b", "c_ones_col", "c_ones_row", "c_neg8ident",
        "c_blockones", "c_blockonesT", "c_iota8", "c_mask_st", "c_mask_ts",
        "c_rope_cos", "c_rope_sin", "c_eps", "c_ones512b"],
}
CONST_DTYPES = {"c_ident_b": BF16, "c_blockones": BF16, "c_ones512b": BF16}
CONST_NOSBUF = {"c_rope_cos", "c_rope_sin"}
CONST_SHAPES = {
    "c_eps": (128, 2),
    "c_ident_f": (128, 128), "c_ident_b": (128, 128), "c_neg8ident": (128, 128),
    "c_ones_col": (128, 1), "c_ones_row": (1, 128), "c_iota8": (128, 8),
    "c_ones_row512": (1, 512), "c_ones512b": (1, 512),
    "c_mask_st": (128, 128), "c_mask_ts": (128, 128),
    "c_blockones": (128, 2), "c_blockonesT": (2, 128),
    "c_rope_cos": (128, T), "c_rope_sin": (128, T),
}


class MK:
    def __init__(self, nc, ctx, slots):
        self.nc = nc
        self.ctx = ctx
        self.slots = slots
        self.P = Prog(nc, ctx)
        self.io = {}
        self.sb = {}

    def dram_in(self, name, shape):
        ap = self.nc.dram_tensor(name, list(shape), F32, kind="ExternalInput").ap()
        self.io[name] = ap
        return ap

    def dram_out(self, name, shape):
        ap = self.nc.dram_tensor(name, list(shape), F32, kind="ExternalOutput").ap()
        self.io[name] = ap
        return ap

    def dram_scratch(self, name, shape, dtype=BF16):
        ap = self.nc.dram_tensor(name, list(shape), dtype, kind="Internal").ap()
        self.io[name] = ap
        return ap

    def sbuf(self, name, shape, dtype=F32):
        h = self.ctx.enter_context(
            self.nc.sbuf_tensor("sb_" + name, list(shape), dtype))
        self.sb[name] = h
        return h

    def psum(self, name, shape, dtype=F32):
        h = self.ctx.enter_context(self.nc.psum_tensor(name, list(shape), dtype))
        return h

    def arena_alloc(self):
        return _Alloc(self.arena, AB_COLS)

    def setup(self):
        P = self.P
        self.LD = P.sem("s_ld")
        self.LS = P.sem("s_ls")
        self.MM = P.sem("s_mm")
        self.EV = P.sem("s_ev")
        self.AC = P.sem("s_ac")
        self.PO = P.sem("s_po")
        self.ST = P.sem("s_st")
        self.STG = P.sem("s_stg")
        self.TICK = P.sem("s_tick")

        self.resid = self.sbuf("resid", [128, DX * NT], BF16)
        self.arena = self.sbuf("arena", [128, AB_COLS], BF16)
        self.hbuf = self.sbuf("hbuf", [128, DX * NT], BF16)

        self.dram_in("x_in", (B, T, D))
        self.dram_out("x_out", (B, T, D))
        cns = sorted({n for s in self.slots for n in CONST_NAMES[s]})
        for nm in cns:
            self.dram_in(nm, CONST_SHAPES[nm])
            if nm not in CONST_NOSBUF:
                self.sbuf(nm, CONST_SHAPES[nm], CONST_DTYPES.get(nm, F32))
        for s in self.slots:
            for nm in SLOT_WEIGHTS[s]:
                self.dram_in(nm, WEIGHT_SHAPES[nm])
        self.sbuf("c_ones_col_b", [128, 1], BF16)
        self.sbuf("c_ones_row_b", [1, 128], BF16)

        for nm in cns:
            if nm in CONST_NOSBUF:
                continue
            if CONST_DTYPES.get(nm, F32) == F32:
                P.dma("sync", self.sb[nm][:], self.io[nm], self.LS)
            else:
                P.dma("gpsimd", self.sb[nm][:], self.io[nm], self.LD)
        P.dma("gpsimd", self.sb["c_ones_col_b"][:], self.io["c_ones_col"], self.LD)
        P.dma("gpsimd", self.sb["c_ones_row_b"][:], self.io["c_ones_row"], self.LD)

        self.pA = self.psum("pA", [128, 2048], F32)
        self.pB = self.psum("pB", [128, 1024], F32)
        self.pT = self.psum("pT", [128, 1024], BF16)
        self.pR = self.psum("pR", [128, 512], F32)
        self.sbuf("wtmp", [128, 4096], BF16)
        self.sbuf("lnsum", [128, DX * 512], BF16)
        self.sbuf("rows", [1, 1024], F32)
        self.sbuf("rows2", [2, 2048], F32)
        self.sbuf("rowb", [1, 128], BF16)
        self.sbuf("rowd", [1, 512], BF16)
        self.sbuf("fcols", [128, 512], F32)
        self.sbuf("rowbuf", [1, 1024], F32)
        self.sbuf("t1f", [128, 512], F32)
        self.sbuf("otile", [128, D], BF16)
        self.barrier("sync")

    def barrier(self, owner="vector"):
        P = self.P
        sems = [s for s in P.sems.values()
                if s is not self.TICK and s.n > 0]
        for s in sems:
            P.wait(owner, s)
        self.TICK.n += 1
        P.emit(owner, lambda e, h=self.TICK.h: e.sem_inc(h, 1))
        for e in ENGINES:
            P.wait(e, self.TICK)

    def prep_transposed(self, dst, dst_col0, src_ap, rows, cols, row0=0,
                        dst_stride=None, scale_col=None):
        P = self.P
        if dst_stride is None:
            dst_stride = rows
        wtmp = self.sb["wtmp"]
        identb = self.sb["c_ident_b"]
        nk = cols // 128
        nr = rows // 128
        P.wait("vector", self.ST)
        for r in range(nr):
            P.wait("gpsimd", self.MM)
            P.wait("gpsimd", self.EV)
            P.dma("gpsimd", wtmp[:, 0:cols],
                  src_ap[row0 + r * 128: row0 + (r + 1) * 128, :], self.LD)
            for k0 in range(0, nk, 8):
                kn = min(8, nk - k0)
                P.wait("tensor", self.LD)
                P.wait("tensor", self.EV)
                for kk in range(kn):
                    kX = k0 + kk
                    P.op("tensor", "transpose",
                         self.pT[:, kk * 128:(kk + 1) * 128],
                         wtmp[:, kX * 128:(kX + 1) * 128], identb[:],
                         inc=self.MM if kk == kn - 1 else None)
                P.wait("vector", self.MM)
                for kk in range(kn):
                    kX = k0 + kk
                    c0 = dst_col0 + kX * dst_stride + r * 128
                    if scale_col is None:
                        P.op("vector", "tensor_copy", dst[:, c0:c0 + 128],
                             self.pT[:, kk * 128:(kk + 1) * 128],
                             inc=self.EV if kk == kn - 1 else None)
                    else:
                        P.op("vector", "tensor_scalar_mul", dst[:, c0:c0 + 128],
                             self.pT[:, kk * 128:(kk + 1) * 128],
                             scale_col(kX),
                             inc=self.EV if kk == kn - 1 else None)
        P.wait("tensor", self.EV)

    def load_bias_cols(self, dst_ap, src_row_ap, n):
        P = self.P
        onesr = self.sb["c_ones_row"]
        rowbuf = self.sb["rowbuf"]
        for j0 in range(0, n, 8):
            jn = min(8, n - j0)
            P.wait("sync", self.MM)
            P.dma("sync", rowbuf[0:1, 0:jn * 128],
                  src_row_ap[0:1, j0 * 128:(j0 + jn) * 128], self.LS)
            P.wait("tensor", self.LS)
            P.wait("tensor", self.EV)
            for j in range(jn):
                P.op("tensor", "matmul", self.pR[:, j:j + 1],
                     rowbuf[0:1, j * 128:(j + 1) * 128], onesr[0:1, 0:1],
                     start=True, stop=True,
                     inc=self.MM if j == jn - 1 else None)
            P.wait("vector", self.MM)
            P.op("vector", "tensor_copy", dst_ap[:, j0:j0 + jn],
                 self.pR[:, 0:jn], inc=self.EV)

    def load_resid(self):
        P = self.P
        src = self.io["x_in"]
        xtm = self.sb["wtmp"]
        identb = self.sb["c_ident_b"]
        for b in range(B):
            for tt in range(T // 128):
                P.wait("gpsimd", self.MM)
                P.dma("gpsimd", xtm[:, 0:D],
                      src[b, tt * 128:(tt + 1) * 128, :], self.LD)
                P.wait("tensor", self.LD)
                P.wait("tensor", self.EV)
                for dXi in range(DX):
                    P.op("tensor", "transpose",
                         self.pT[:, dXi * 128:(dXi + 1) * 128],
                         xtm[:, dXi * 128:(dXi + 1) * 128], identb[:],
                         inc=self.MM if dXi == DX - 1 else None)
                P.wait("vector", self.MM)
                for dXi in range(DX):
                    c0 = dXi * NT + b * T + tt * 128
                    P.op("vector", "tensor_copy",
                         self.resid[:, c0:c0 + 128],
                         self.pT[:, dXi * 128:(dXi + 1) * 128],
                         inc=self.EV if dXi == DX - 1 else None)
        self.barrier()

    def store_out(self, src_name=None):
        P = self.P
        dst = self.io["x_out"]
        identb = self.sb["c_ident_b"]
        otile = self.sb["otile"]
        src = self.sb[src_name] if src_name else self.resid
        for b in range(B):
            for tt in range(T // 128):
                P.wait("tensor", self.STG)
                P.wait("tensor", self.EV)
                for dXi in range(DX):
                    c0 = dXi * NT + b * T + tt * 128
                    P.op("tensor", "transpose",
                         self.pT[:, dXi * 128:(dXi + 1) * 128],
                         src[:, c0:c0 + 128], identb[:],
                         inc=self.MM if dXi == DX - 1 else None)

                P.wait("vector", self.MM)
                P.wait("vector", self.STG)
                P.op("vector", "tensor_copy", otile[:, 0:D],
                     self.pT[:, 0:1024], inc=self.EV)
                P.wait("gpsimd", self.EV)
                P.dma("gpsimd", dst[b, tt * 128:(tt + 1) * 128, :],
                      otile[:, 0:D], self.STG)
        P.wait("gpsimd", self.STG)
        self.barrier("gpsimd")

    def postln_chunk(self, rcol0, slot_fn, bias_col=None, add_resid=True,
                     extra_fn=None, presummed=False):
        P = self.P
        lnsum, lnsq = self.sb["lnsum"], self.sb["wtmp"]
        onesb = self.sb["c_ones_col_b"]
        ones1f = self.sb["c_ones_row"]
        rows = self.sb["rows"]
        t1f = self.sb["t1f"]
        for dXi in range(DX if not presummed else 0):
            rc = rcol0 + dXi * NT
            d_ap = lnsum[:, dXi * 512:(dXi + 1) * 512]
            s_ap = slot_fn(dXi)
            last = (dXi == DX - 1) and extra_fn is None
            if add_resid:
                if bias_col is not None:
                    P.op("vector", "scalar_tensor_tensor", d_ap, s_ap,
                         bias_col, self.resid[:, rc:rc + 512], ALU.add,
                         ALU.add, inc=self.EV if last else None)
                else:
                    P.op("vector", "tensor_add", d_ap, s_ap,
                         self.resid[:, rc:rc + 512],
                         inc=self.EV if last else None)
            else:
                P.op("vector", "tensor_copy", d_ap, s_ap,
                     inc=self.EV if last else None)
        if extra_fn is not None:
            P.wait("vector", self.EV)
            for dXi in range(DX):
                d_ap = lnsum[:, dXi * 512:(dXi + 1) * 512]
                P.op("vector", "tensor_add", d_ap, d_ap, extra_fn(dXi),
                     inc=self.EV if dXi == DX - 1 else None)
        P.wait("scalar", self.EV)
        for dXi in range(DX):
            P.op("scalar", "activation", lnsq[:, dXi * 512:(dXi + 1) * 512],
                 lnsum[:, dXi * 512:(dXi + 1) * 512], AF.Square,
                 inc=self.AC if dXi == DX - 1 else None)
        P.wait("tensor", self.EV)
        P.wait("tensor", self.AC)
        for dXi in range(DX):
            P.op("tensor", "matmul", self.pB[0:1, 0:512], onesb[:],
                 lnsum[:, dXi * 512:(dXi + 1) * 512],
                 start=(dXi == 0), stop=(dXi == DX - 1))
        for dXi in range(DX):
            P.op("tensor", "matmul", self.pB[0:1, 512:1024], onesb[:],
                 lnsq[:, dXi * 512:(dXi + 1) * 512],
                 start=(dXi == 0), stop=(dXi == DX - 1),
                 inc=self.MM if dXi == DX - 1 else None)
        P.wait("vector", self.MM)
        P.op("vector", "tensor_scalar_mul", rows[0:1, 0:512],
             self.pB[0:1, 0:512], 1.0 / D, inc=self.EV)
        P.wait("scalar", self.EV)
        P.op("scalar", "activation", rows[0:1, 512:1024], rows[0:1, 0:512],
             AF.Square, inc=self.AC)
        P.wait("vector", self.AC)
        P.op("vector", "scalar_tensor_tensor", rows[0:1, 512:1024],
             self.pB[0:1, 512:1024], 1.0 / D, rows[0:1, 512:1024],
             ALU.mult, ALU.subtract, inc=self.EV)
        P.wait("scalar", self.EV)
        P.op("scalar", "activation", rows[0:1, 512:1024],
             rows[0:1, 512:1024], AF.Ln,
             bias=self.sb["c_eps"][0:1, 0:1], inc=self.AC)
        P.wait("scalar", self.AC, self.AC.n)
        P.op("scalar", "activation", rows[0:1, 512:1024],
             rows[0:1, 512:1024], AF.Exp, scale=-0.5, inc=self.AC)
        P.wait("tensor", self.AC)
        P.op("tensor", "matmul", self.pA[:, 0:512], ones1f[0:1, :],
             rows[0:1, 0:512], start=True, stop=True)
        P.op("tensor", "matmul", self.pA[:, 512:1024], ones1f[0:1, :],
             rows[0:1, 512:1024], start=True, stop=True, inc=self.MM)
        P.wait("vector", self.MM)
        for dXi in range(DX):
            P.op("vector", "tensor_sub", lnsq[:, dXi * 512:(dXi + 1) * 512],
                 lnsum[:, dXi * 512:(dXi + 1) * 512], self.pA[:, 0:512],
                 inc=self.EV if dXi == DX - 1 else None)
        P.wait("vector", self.EV)
        for dXi in range(DX):
            rc = rcol0 + dXi * NT
            P.op("vector", "tensor_mul", self.resid[:, rc:rc + 512],
                 lnsq[:, dXi * 512:(dXi + 1) * 512], self.pA[:, 512:1024],
                 inc=self.EV if dXi == DX - 1 else None)
        P.wait("vector", self.EV)

    def slot0(self):
        P = self.P
        if True:
            al = self.arena_alloc()
            wzT = al.take(DX * D)
            whT = al.take(DX * D)
            bcol = self.sb["fcols"][:, 0:2 * DX]
            nbz = self.sb["fcols"][:, 2 * DX:3 * DX]
            SC = 512
            hb = self.hbuf
            Ab = hb[:, 0:DX * SC]
            Bb = hb[:, DX * SC:2 * DX * SC]
            XM = [hb[:, 2 * DX * SC:3 * DX * SC],
                  hb[:, 3 * DX * SC:4 * DX * SC]]
            zt = hb[:, 4 * DX * SC:4 * DX * SC + SC]
            sg = hb[:, 4 * DX * SC + SC:4 * DX * SC + 2 * SC]
            hp5 = hb[:, 4 * DX * SC + 2 * SC:4 * DX * SC + 3 * SC]
            gt = hb[:, 4 * DX * SC + 3 * SC:4 * DX * SC + 4 * SC]

            self.prep_transposed(wzT, 0, self.io["L0_attn_Wz"], D, D)
            self.prep_transposed(whT, 0, self.io["L0_attn_Wh"], D, D)
            self.load_bias_cols(bcol[:, 0:DX], self.io["L0_attn_bz"], DX)
            self.load_bias_cols(bcol[:, DX:2 * DX], self.io["L0_attn_bh"], DX)
            P.wait("vector", self.EV)
            P.op("vector", "tensor_scalar_mul", nbz[:], bcol[:, 0:DX], -1.0,
                 inc=self.EV)
            self.barrier()

            for b in range(B):
                for ci in range(T // SC):
                    c0 = ci * SC
                    par = ci % 2
                    for hXi in range(DX):
                        pk = self.pA[:, (hXi % 2) * 512:(hXi % 2) * 512 + 512]
                        ph = self.pB[:, (hXi % 2) * 512:(hXi % 2) * 512 + 512]
                        P.wait("tensor", self.EV)
                        P.wait("tensor", self.AC)
                        for (w, p) in ((wzT, pk), (whT, ph)):
                            for dXi in range(DX):
                                rhs = self.resid[:, dXi * NT + b * T + c0:
                                                 dXi * NT + b * T + c0 + SC]
                                lhsT = w[:, dXi * D + hXi * 128:
                                         dXi * D + (hXi + 1) * 128]
                                for n in range(SC // 512):
                                    P.op("tensor", "matmul",
                                         p[:, n * 512:(n + 1) * 512], lhsT,
                                         rhs[:, n * 512:(n + 1) * 512],
                                         start=(dXi == 0), stop=(dXi == DX - 1),
                                         inc=self.MM if (w is whT
                                                         and dXi == DX - 1
                                                         and n == SC // 512 - 1)
                                         else None)
                        P.wait("scalar", self.MM)
                        hcol = slice(hXi * SC, (hXi + 1) * SC)
                        P.op("scalar", "activation", Ab[:, hcol], pk[:],
                             AF.Sigmoid, bias=nbz[:, hXi:hXi + 1], scale=-1.0,
                             inc=self.AC)
                        P.op("scalar", "activation", zt[:], pk[:], AF.Sigmoid,
                             bias=bcol[:, hXi:hXi + 1], inc=self.AC)
                        P.op("scalar", "activation", sg[:], ph[:], AF.Sigmoid,
                             bias=bcol[:, DX + hXi:DX + hXi + 1], inc=self.AC)
                        P.wait("vector", self.MM)
                        P.op("vector", "tensor_scalar", hp5[:], ph[:],
                             bcol[:, DX + hXi:DX + hXi + 1], 0.5, ALU.add,
                             ALU.add, inc=self.EV)
                        P.wait("vector", self.EV)
                        P.wait("vector", self.AC)
                        P.op("vector", "tensor_max", gt[:], hp5[:], sg[:],
                             inc=self.EV)
                        P.wait("vector", self.EV)
                        P.op("vector", "tensor_mul", Bb[:, hcol], zt[:], gt[:],
                             inc=self.EV)
                        P.wait("vector", self.EV)
                        init = 0.0 if ci == 0 else \
                            XM[1 - par][:, hXi * SC + SC - 1: hXi * SC + SC]
                        P.op("vector", "tensor_tensor_scan", XM[par][:, hcol],
                             Ab[:, hcol], Bb[:, hcol], init, ALU.mult, ALU.add,
                             inc=self.EV)
                        P.wait("vector", self.EV)
                    self.barrier()
                    self.postln_chunk(
                        b * T + c0,
                        lambda dXi, par=par:
                            XM[par][:, dXi * SC: dXi * SC + 512])
                    self.barrier()


    def slot1(self):
        P = self.P
        al = self.arena_alloc()
        h1 = al.take(HX * 512)
        h2 = self.hbuf[:, 0:HX * 512]
        wb = [self.hbuf[:, HX * 512:HX * 512 + 8192],
              self.hbuf[:, HX * 512 + 8192:HX * 512 + 16384]]
        silu_s = al.take(512)
        fc = self.sb["fcols"]
        s1c, s2c = fc[:, 0:32], fc[:, 32:64]
        sh1c, sh2c = fc[:, 64:96], fc[:, 96:128]
        s3c = fc[:, 128:136]

        scr = {
            "w1": self.dram_scratch("scr_w1", (HX, 128, 1024)),
            "w2": self.dram_scratch("scr_w2", (HX, 128, 1024)),
            "h1": self.dram_scratch("scr_h1", (HX, 128, 4096)),
            "h2": self.dram_scratch("scr_h2", (HX, 128, 4096)),
            "w3": self.dram_scratch("scr_w3", (DX, 128, 4096)),
        }
        self.load_bias_cols(s1c, self.io["L0_ffn_w1_s"], 32)
        self.load_bias_cols(s2c, self.io["L0_ffn_w2_s"], 32)
        self.load_bias_cols(sh1c, self.io["L0_ffn_h1_s"], 32)
        self.load_bias_cols(sh2c, self.io["L0_ffn_h2_s"], 32)
        self.load_bias_cols(s3c, self.io["L0_ffn_w3_s"], DX)
        self.barrier()

        lastmm = {}

        def get_block(g, bufs, par, tag, src_name, m, nk, blkcols):
            wap = bufs[par][:, 0:blkcols]
            if g == 0:
                P.wait("sync", self.MM, lastmm.get((tag, par), 0))
                self.prep_transposed(wap, 0, self.io[src_name], 128,
                                     nk * 128, row0=m * 128, dst_stride=128)
                P.wait("sync", self.EV)
                P.dma("sync", scr[tag][m], wap, self.ST)
            else:
                P.wait("sync", self.MM, lastmm.get((tag, par), 0))
                P.dma("sync", wap, scr[tag][m], self.LS)
            return wap

        def stageA(g, gc):
            for m in range(HX):
                par = m % 2
                w1ap = get_block(g, [wb[0][:, 0:1024], wb[1][:, 0:1024]],
                                 par, "w1", "L0_ffn_w1_w", m, DX, 1024)
                w2ap = get_block(g, [wb[0][:, 1024:2048], wb[1][:, 1024:2048]],
                                 par, "w2", "L0_ffn_w2_w", m, DX, 1024)
                pa = self.pA[:, par * 1024: par * 1024 + 512]
                pb = self.pA[:, par * 1024 + 512: par * 1024 + 1024]
                P.wait("tensor", self.LS)
                P.wait("tensor", self.EV, lastmm.get(("psA", par), 0))
                P.wait("tensor", self.AC, lastmm.get(("psAa", par), 0))
                for k in range(DX):
                    rhs = self.resid[:, k * NT + gc: k * NT + gc + 512]
                    P.op("tensor", "matmul", pa, w1ap[:, k * 128:(k + 1) * 128],
                         rhs, start=(k == 0), stop=(k == DX - 1))
                for k in range(DX):
                    rhs = self.resid[:, k * NT + gc: k * NT + gc + 512]
                    P.op("tensor", "matmul", pb, w2ap[:, k * 128:(k + 1) * 128],
                         rhs, start=(k == 0), stop=(k == DX - 1),
                         inc=self.MM if k == DX - 1 else None)
                lastmm[("w1", par)] = self.MM.n
                lastmm[("w2", par)] = self.MM.n
                P.wait("scalar", self.MM)
                P.wait("scalar", self.EV)
                P.op("scalar", "activation", silu_s[:, 0:512], pa, AF.Silu,
                     scale=s1c[:, m:m + 1], inc=self.AC)
                lastmm[("psAa", par)] = self.AC.n
                P.wait("vector", self.AC)
                P.op("vector", "scalar_tensor_tensor",
                     h1[:, m * 512:(m + 1) * 512], pb, s2c[:, m:m + 1],
                     silu_s[:, 0:512], ALU.mult, ALU.mult, inc=self.EV)
                lastmm[("psA", par)] = self.EV.n

        def stageMM(g, gc, tag, src_name, src_buf, dst_buf, nm, nk, scol,
                    act):
            for mg in range(nm // 2):
                par = mg % 2
                if g == 0:
                    P.wait("vector", self.MM, lastmm.get((tag, par), 0))
                    for mm_ in range(2):
                        self.prep_transposed(
                            wap_sub(wb[par], mm_, nk), 0, self.io[src_name],
                            128, nk * 128, row0=(mg * 2 + mm_) * 128,
                            dst_stride=128)
                    P.wait("sync", self.EV)
                    P.dma("sync", scr[tag][mg * 2:(mg + 1) * 2].transpose(
                        [1, 0, 2]),
                        wb[par][:, 0:2 * nk * 128].rearrange(
                            "p (m c) -> p m c", m=2), self.ST)
                else:
                    P.wait("sync", self.MM, lastmm.get((tag, par), 0))
                    P.dma("sync", wb[par][:, 0:2 * nk * 128].rearrange(
                        "p (m c) -> p m c", m=2),
                        scr[tag][mg * 2:(mg + 1) * 2].transpose([1, 0, 2]),
                        self.LS)
                P.wait("tensor", self.LS)
                for mm_ in range(2):
                    m = mg * 2 + mm_
                    p2 = m % 2
                    pc = self.pB[:, p2 * 512: p2 * 512 + 512]
                    P.wait("tensor", self.AC,
                           lastmm.get(("psB" + act.name, p2), 0))
                    for k in range(nk):
                        P.op("tensor", "matmul", pc,
                             wb[par][:, mm_ * nk * 128 + k * 128:
                                     mm_ * nk * 128 + (k + 1) * 128],
                             src_buf[:, k * 512:(k + 1) * 512],
                             start=(k == 0), stop=(k == nk - 1),
                             inc=self.MM if k == nk - 1 else None)
                    P.wait("scalar", self.MM)
                    P.op("scalar", "activation",
                         dst_buf[:, m * 512:(m + 1) * 512],
                         pc, act, scale=scol[:, m:m + 1], inc=self.AC)
                    lastmm[("psB" + act.name, p2)] = self.AC.n
                lastmm[(tag, par)] = self.MM.n

        def wap_sub(wbuf, mm_, nk):
            return wbuf[:, mm_ * nk * 128:(mm_ + 1) * nk * 128]

        for g in range(NT // 512):
            gc = g * 512
            stageA(g, gc)
            self.barrier()
            stageMM(g, gc, "h1", "L0_ffn_h1_w", h1, h2, HX, HX, sh1c, AF.Gelu)
            self.barrier()
            stageMM(g, gc, "h2", "L0_ffn_h2_w", h2, h1, HX, HX, sh2c, AF.Gelu)
            self.barrier()
            for m in range(DX):
                par = m % 2
                wap = get_block(g, wb, par, "w3", "L0_ffn_w3_w", m, HX, 4096)
                pc = self.pB[:, par * 512: par * 512 + 512]
                P.wait("tensor", self.LS)
                P.wait("tensor", self.EV, lastmm.get(("psD", par), 0))
                for k in range(HX):
                    P.op("tensor", "matmul", pc, wap[:, k * 128:(k + 1) * 128],
                         h1[:, k * 512:(k + 1) * 512],
                         start=(k == 0), stop=(k == HX - 1),
                         inc=self.MM if k == HX - 1 else None)
                lastmm[("w3", par)] = self.MM.n
                P.wait("vector", self.MM)
                P.op("vector", "scalar_tensor_tensor",
                     self.sb["lnsum"][:, m * 512:(m + 1) * 512], pc,
                     s3c[:, m:m + 1], self.resid[:, m * NT + gc:
                                                 m * NT + gc + 512],
                     ALU.mult, ALU.add, inc=self.EV)
                lastmm[("psD", par)] = self.EV.n
            self.barrier()
            self.postln_chunk(gc, None, presummed=True)
            self.barrier()


    def col64_to_128(self, dst_col, src_row_ap, n64=64):
        P = self.P
        rowbuf = self.sb["rowbuf"]
        onesr = self.sb["c_ones_row"]
        P.wait("sync", self.MM)
        P.dma("sync", rowbuf[0:1, 0:n64], src_row_ap[0:1, 0:n64], self.LS)
        P.wait("tensor", self.LS)
        P.wait("tensor", self.EV)
        P.op("tensor", "matmul", self.pR[0:n64, 0:1], rowbuf[0:1, 0:n64],
             onesr[0:1, 0:1], start=True, stop=True, inc=self.MM)
        P.wait("vector", self.MM)
        P.op("vector", "tensor_copy", dst_col[0:64, :], self.pR[0:64, 0:1],
             inc=self.EV)
        P.wait("vector", self.EV)
        P.op("vector", "tensor_copy", dst_col[64:128, :], self.pR[0:64, 0:1],
             inc=self.EV)

    def attn_heads(self, Qb, Kb, Vb2, causal, mrow, out_fn, post_fn=None,
                   bound_cols=None):
        P = self.P
        fc = self.sb["fcols"]
        m_all = fc[:, 0:16]
        mh = fc[:, 16:20]
        t1f = self.sb["t1f"]
        ones1f = self.sb["c_ones_row"]
        n8i = self.sb["c_neg8ident"]
        mask_st = self.sb["c_mask_st"]
        mask_ts = self.sb["c_mask_ts"]
        lu = self._attn_lu = {}

        def wl(eng, key):
            v = lu.get(key)
            if v is None:
                return
            if isinstance(v, list):
                for sem, val in v:
                    P.wait(eng, sem, val)
            else:
                P.wait(eng, v[0], v[1])

        lu[("pa", 0)] = (self.EV, self.EV.n)
        lu[("pa", 1)] = (self.EV, self.EV.n)
        for tg_ in ("s", 0, 1):
            lu[("pst", tg_, 0)] = [(self.AC, self.AC.n), (self.EV, self.EV.n)]
            lu[("pst", tg_, 1)] = [(self.AC, self.AC.n), (self.EV, self.EV.n)]
        lu["mrow_free"] = (self.MM, self.MM.n)

        def passA_items(h):
            hb = 64 * h
            items = []
            for i in range(16):
                def item(i=i, hb=hb, h=h):
                    ncols = (i + 1) * 128 if causal else 2048
                    sl = i % 2
                    base = sl * 1024
                    msl = mh[:, sl * 2:sl * 2 + 2]
                    nhalf = min(2, (ncols + 1023) // 1024)
                    wl("vector", ("mh", sl))
                    for hf in range(nhalf):
                        w = min(1024, ncols - hf * 1024)
                        wl("tensor", ("pa", sl))
                        nch = (w + 511) // 512
                        for cc in range(nch):
                            cw = min(512, w - cc * 512)
                            P.op("tensor", "matmul",
                                 self.pA[:, base + cc * 512:
                                         base + cc * 512 + cw],
                                 Qb[hb:hb + 64, i * 128:(i + 1) * 128],
                                 Kb[hb:hb + 64, hf * 1024 + cc * 512:
                                    hf * 1024 + cc * 512 + cw],
                                 start=True, stop=True,
                                 inc=self.MM if cc == nch - 1 else None)
                        mmc = self.MM.n
                        P.wait("vector", self.MM, mmc)
                        if causal and hf * 1024 <= i * 128 < hf * 1024 + w:
                            off = i * 128 - hf * 1024
                            P.op("vector", "tensor_add",
                                 self.pA[:, base + off:base + off + 128],
                                 self.pA[:, base + off:base + off + 128],
                                 mask_st[:], inc=self.EV)
                            P.wait("vector", self.EV, self.EV.n)
                        P.op("vector", "reduce_max", msl[:, hf:hf + 1],
                             self.pA[:, base:base + w], axis=AX.X,
                             negate=True, inc=self.EV)
                        lu[("pa", sl)] = (self.EV, self.EV.n)
                    P.wait("vector", self.EV, self.EV.n)
                    if nhalf == 2:
                        P.op("vector", "tensor_tensor", msl[:, 0:1],
                             msl[:, 0:1], msl[:, 1:2], ALU.min, inc=self.EV)
                        P.wait("vector", self.EV, self.EV.n)
                    P.op("vector", "tensor_scalar_mul", m_all[:, i:i + 1],
                         msl[:, 0:1], 0.125, inc=self.EV)
                    lu[("mh", sl)] = (self.EV, self.EV.n)
                    lu[("mcol", h, i)] = (self.EV, self.EV.n)
                items.append(item)
            return items

        def emit_mrows(h):
            wl("tensor", ("pa", 0))
            for i in range(16):
                wl("tensor", ("mcol", h, i))
            wl("tensor", "mrow_free")
            for i in range(16):
                P.op("tensor", "matmul",
                     self.pA[0:1, (i % 4) * 128:(i % 4 + 1) * 128],
                     m_all[:, i:i + 1], n8i[:], start=True, stop=True,
                     inc=self.MM if i % 4 == 3 else None)
                if i % 4 == 3:
                    mmc = self.MM.n
                    P.wait("vector", self.MM, mmc)
                    P.op("vector", "tensor_copy",
                         mrow[0:1, (i - 3) * 128:(i + 1) * 128],
                         self.pA[0:1, 0:512], inc=self.EV)
                    if i < 15:
                        P.wait("tensor", self.EV)
            lu[("pa", 0)] = (self.EV, self.EV.n)
            lu["mrow_ready"] = (self.EV, self.EV.n)

        def passB_items(h):
            hb = 64 * h
            res = (self._attn_res or {}).get(h) if getattr(
                self, "_attn_res", None) else None
            if res is None:
                res = dict(pst=(self.pB[:, 0:512], self.pB[:, 512:1024]),
                           av=self.pR[0:65, 0:512],
                           ptb=(self._attn_ptb[:, 0:512],
                                self._attn_ptb[:, 512:1024]),
                           rep=self.pB[:, 512:1024], repk=("pst", "s", 1),
                           avs=t1f[0:65, :], tag="s")
            else:
                res = dict(res)
                res["tag"] = h
            tg = res["tag"]
            items = []
            for c in range(4):
                jstop = min(15, (c * 512 + 511) // 128) if causal else 15
                jlist = list(range(jstop + 1))
                for j in jlist:
                    def item(j=j, c=c, hb=hb, h=h, res=res, tg=tg,
                             first=(j == 0), last=(j == jlist[-1])):
                        pj = j % 2
                        bank = res["pst"][pj]
                        sc0 = c * 512
                        wl("tensor", ("pst", tg, pj))
                        if bound_cols is None:
                            wl("tensor", "mrow_ready")
                            P.op("tensor", "matmul", bank,
                                 Kb[hb:hb + 64, j * 128:(j + 1) * 128],
                                 Qb[hb:hb + 64, sc0:sc0 + 512],
                                 start=True, stop=False)
                            P.op("tensor", "matmul", bank,
                                 self.sb["c_ones_row_b"][0:1, :],
                                 mrow[0:1, sc0:sc0 + 512],
                                 start=False, stop=True, inc=self.MM)
                            lu["mrow_free"] = (self.MM, self.MM.n)
                        else:
                            P.op("tensor", "matmul", bank,
                                 Kb[hb:hb + 64, j * 128:(j + 1) * 128],
                                 Qb[hb:hb + 64, sc0:sc0 + 512],
                                 start=True, stop=True, inc=self.MM)
                        mmc = self.MM.n
                        nev = None
                        if causal and sc0 <= j * 128 < sc0 + 512:
                            off = j * 128 - sc0
                            P.wait("vector", self.MM, mmc)
                            if off > 0:
                                P.op("vector", "tensor_scalar_add",
                                     bank[:, 0:off], bank[:, 0:off],
                                     -1e30, inc=self.EV)
                            P.op("vector", "tensor_add",
                                 bank[:, off:off + 128],
                                 bank[:, off:off + 128], mask_ts[:],
                                 inc=self.EV)
                            nev = self.EV.n
                        P.wait("scalar", self.MM, mmc)
                        if nev:
                            P.wait("scalar", self.EV, nev)
                        wl("scalar", ("ptb", tg, pj))
                        ptb_ap = res["ptb"][pj]
                        if bound_cols is None:
                            P.op("scalar", "activation", ptb_ap, bank,
                                 AF.Exp, scale=0.125, inc=self.AC)
                        else:
                            P.op("scalar", "activation", ptb_ap, bank,
                                 AF.Exp, scale=0.125, bias=bound_cols[h],
                                 inc=self.AC)
                        lu[("pst", tg, pj)] = (self.AC, self.AC.n)
                        acc = self.AC.n

                        def av_part(acc=acc, ptb_ap=ptb_ap, j=j, c=c, h=h,
                                    res=res, tg=tg, first=first, last=last):
                            P.wait("tensor", self.AC, acc)
                            if first:
                                wl("tensor", ("prav", tg))
                            P.op("tensor", "matmul", res["av"],
                                 Vb2[:, j * 130 + h * 65:
                                     j * 130 + h * 65 + 65],
                                 ptb_ap, start=first, stop=last,
                                 inc=self.MM if last else None)
                            lu[("ptb", tg, pj)] = (self.MM, self.MM.n)
                            if last:
                                self._finish_chunk(h, c, res, tg)
                        return av_part
                    items.append(item)
            return items

        def finish_chunk(h, c, res, tg):
            avs = res["avs"]
            mmc = self.MM.n
            P.wait("vector", self.MM, mmc)
            wl("vector", ("avsfree", tg))
            P.op("vector", "tensor_copy", avs, res["av"], inc=self.EV)
            lu[("prav", tg)] = (self.EV, self.EV.n)
            P.wait("scalar", self.EV, self.EV.n)
            wl("scalar", "rowd")
            P.op("scalar", "activation", avs[64:65, :], avs[64:65, :],
                 AF.Ln, inc=self.AC)
            P.wait("scalar", self.AC, self.AC.n)
            P.op("scalar", "activation", self.sb["rowd"][0:1, 0:512],
                 avs[64:65, :], AF.Exp, scale=-1.0, inc=self.AC)
            rc = self.AC.n
            P.wait("tensor", self.AC, rc)
            if res["repk"] is not None:
                wl("tensor", res["repk"])
            else:
                wl("tensor", ("repfree", tg))
            P.op("tensor", "matmul", res["rep"],
                 self.sb["c_ones_row_b"][0:1, :],
                 self.sb["rowd"][0:1, 0:512], start=True, stop=True,
                 inc=self.MM)
            lu["rowd"] = (self.MM, self.MM.n)
            repc = self.MM.n
            P.wait("vector", self.MM, repc)
            dst = out_fn(h, c)
            P.op("vector", "tensor_mul", dst, avs[0:64, :],
                 res["rep"][0:64, :], inc=self.EV)
            lu[("pst", tg, 1)] = (self.EV, self.EV.n)
            lu[("repfree", tg)] = (self.EV, self.EV.n)
            lu[("avsfree", tg)] = (self.EV, self.EV.n)
            if post_fn is not None:
                post_fn(h, c)
        self._finish_chunk = finish_chunk

        if bound_cols is not None:
            res_h = {
                0: dict(pst=(self.pB[:, 0:512], self.pB[:, 512:1024]),
                        av=self.pR[0:65, 0:512],
                        ptb=(self._attn_ptb[:, 0:512],
                             self._attn_ptb[:, 512:1024]),
                        rep=self.pB[:, 512:1024], repk=("pst", 0, 1),
                        avs=t1f[0:65, :]),
                1: dict(pst=(self.pA[:, 0:512], self.pA[:, 512:1024]),
                        av=self.pA[0:65, 1024:1536],
                        ptb=(self._attn_ptb[:, 1024:1536],
                             self._attn_ptb[:, 1536:2048]),
                        rep=self.pA[:, 1536:2048], repk=None,
                        avs=self.sb["wtmp"][0:65, 0:512]),
            }
            self._attn_res = res_h
            b0 = passB_items(0)
            b1 = passB_items(1)
            n = max(len(b0), len(b1))
            for i in range(n):
                avs_ = []
                if i < len(b0):
                    avs_.append(b0[i]())
                if i < len(b1):
                    avs_.append(b1[i]())
                for a_ in avs_:
                    a_()
            self._attn_res = None
        else:
            for it in passA_items(0):
                it()
            emit_mrows(0)
            b0 = passB_items(0)
            a1 = passA_items(1)
            k = 0
            prev_av = None
            for i, it in enumerate(b0):
                av_ = it()
                if i % 2 == 1 and k < len(a1):
                    a1[k]()
                    k += 1
                if prev_av is not None:
                    prev_av()
                prev_av = av_
            if prev_av is not None:
                prev_av()
            while k < len(a1):
                a1[k]()
                k += 1
            emit_mrows(1)
            prev_av = None
            for it in passB_items(1):
                av_ = it()
                if prev_av is not None:
                    prev_av()
                prev_av = av_
            if prev_av is not None:
                prev_av()

    def slot2(self):
        P = self.P
        al = self.arena_alloc()
        wqb = [al.take(3072), al.take(3072)]
        Qb = al.take(2048)
        Kb = al.take(2048)
        Vb2 = al.take(16 * 130)
        sqs = al.take(2048)
        ptb = al.take(1024)
        avT = self.hbuf[:, 0:DX * 2048]
        wpT = self.hbuf[:, DX * 2048:DX * 2048 + 8192]
        fc = self.sb["fcols"]
        m_all = fc[:, 0:16]
        den_all = fc[:, 16:32]
        g_all = fc[:, 32:48]
        nscal = fc[:, 48:49]
        bq_cols = fc[:, 50:58]
        bk_cols = fc[:, 58:66]
        bp_cols = fc[:, 66:74]
        qn_col = fc[:, 74:75]
        kn_col = fc[:, 75:76]
        rows2 = self.sb["rows2"]
        rowb = self.sb["rowb"]
        blockones = self.sb["c_blockones"]
        blockonesT = self.sb["c_blockonesT"]
        ones1f = self.sb["c_ones_row"]
        n8i = self.sb["c_neg8ident"]
        eps6 = self.sb["c_eps"][0:1, 1:2]
        eps6r = self.sb["c_eps"][0:2, 1:2]

        self.load_bias_cols(bq_cols, self.io["L1_attn_bqkv"][:, 0:1024], 8)
        self.load_bias_cols(bk_cols, self.io["L1_attn_bqkv"][:, 1024:2048], 8)
        self.load_bias_cols(bp_cols, self.io["L1_attn_bproj"], 8)
        self.col64_to_128(qn_col, self.io["L1_attn_qn_w"])
        self.col64_to_128(kn_col, self.io["L1_attn_kn_w"])
        self.prep_transposed(wpT, 0, self.io["L1_attn_Wproj"], D, D)
        P.wait("vector", self.LD)
        for tt in range(16):
            for hh in range(2):
                P.op("vector", "tensor_copy",
                     Vb2[:, tt * 130 + hh * 65 + 64:tt * 130 + hh * 65 + 65],
                     self.sb["c_ones_col_b"][:],
                     inc=self.EV if (tt == 15 and hh == 1) else None)
        self.barrier()

        lastu = {}
        for b in range(B):
            bc = b * T
            for p in range(NH // 2):
                par = p % 2
                wq = wqb[par][:, 0:1024]
                wk = wqb[par][:, 1024:2048]
                wv = wqb[par][:, 2048:3072]
                if b == 0:
                    P.wait("vector", self.MM, lastu.get(("wq", par), 0))
                    self.prep_transposed(wq, 0, self.io["L1_attn_Wqkv"],
                                         128, D, row0=p * 128, dst_stride=128)
                    self.prep_transposed(wk, 0, self.io["L1_attn_Wqkv"],
                                         128, D, row0=1024 + p * 128,
                                         dst_stride=128)
                    self.prep_transposed(wv, 0, self.io["L1_attn_Wqkv"],
                                         128, D, row0=2048 + p * 128,
                                         dst_stride=128)
                    if p == 0:
                        scr_qkv = self.dram_scratch("scr_qkv",
                                                    (NH // 2, 128, 3072))
                        self.scr_qkv = scr_qkv
                    P.wait("sync", self.EV)
                    P.dma("sync", self.scr_qkv[p], wqb[par][:, 0:3072],
                          self.ST)
                else:
                    P.wait("sync", self.MM, lastu.get(("wq", par), 0))
                    P.dma("sync", wqb[par][:, 0:3072], self.scr_qkv[p],
                          self.LS)
                for (wtile, dstb, bcols) in ((wq, Qb, bq_cols),
                                             (wk, Kb, bk_cols)):
                    P.wait("tensor", self.LS)
                    P.wait("tensor", self.ST)
                    P.wait("tensor", self.EV)
                    for k in range(DX):
                        for n in range(4):
                            P.op("tensor", "matmul",
                                 self.pA[:, n * 512:(n + 1) * 512],
                                 wtile[:, k * 128:(k + 1) * 128],
                                 self.resid[:, k * NT + bc + n * 512:
                                            k * NT + bc + n * 512 + 512],
                                 start=(k == 0), stop=(k == DX - 1),
                                 inc=self.MM if (k == DX - 1 and n == 3)
                                 else None)
                    lastu[("wq", par)] = self.MM.n
                    P.wait("vector", self.MM)
                    P.op("vector", "tensor_scalar_add", dstb[:, 0:2048],
                         self.pA[:, 0:2048], bcols[:, p:p + 1], inc=self.EV)
                P.wait("gpsimd", self.MM, lastu.get("rowb", 0))
                P.dma("gpsimd", rowb[0:1, 0:128],
                      self.io["L1_attn_bqkv"][:, 2048 + p * 128:
                                              2048 + (p + 1) * 128], self.LD)
                P.wait("tensor", self.EV)
                P.wait("tensor", self.LD)
                for tg in range(4):
                    pvb = self.pB[:, (tg % 2) * 512:(tg % 2) * 512 + 512]
                    P.wait("tensor", self.EV, lastu.get(("pv", tg % 2), 0))
                    for t2 in range(4):
                        tt = tg * 4 + t2
                        pv = pvb[:, t2 * 128:(t2 + 1) * 128]
                        for k in range(DX):
                            P.op("tensor", "matmul", pv,
                                 self.resid[:, k * NT + bc + tt * 128:
                                            k * NT + bc + (tt + 1) * 128],
                                 wv[:, k * 128:(k + 1) * 128],
                                 start=(k == 0), stop=False)
                        P.op("tensor", "matmul", pv,
                             self.sb["c_ones_row_b"][0:1, :],
                             rowb[0:1, 0:128],
                             start=False, stop=True,
                             inc=self.MM if t2 == 3 else None)
                    mmc = self.MM.n
                    P.wait("vector", self.MM, mmc)
                    t0c = tg * 4
                    P.op("vector", "tensor_copy",
                         Vb2[:, t0c * 130:(t0c + 4) * 130].rearrange(
                             "p (t c) -> p t c", t=4)[:, :, 0:64],
                         pvb.rearrange("p (t c) -> p t c", t=4)[:, :, 0:64],
                         inc=self.EV)
                    P.op("vector", "tensor_copy",
                         Vb2[:, t0c * 130:(t0c + 4) * 130].rearrange(
                             "p (t c) -> p t c", t=4)[:, :, 65:129],
                         pvb.rearrange("p (t c) -> p t c", t=4)[:, :, 64:128],
                         inc=self.EV)
                    lastu[("pv", tg % 2)] = self.EV.n
                lastu["rowb"] = self.MM.n
                lastu[("wq", par)] = self.MM.n
                for (buf, ncol) in ((Qb, qn_col), (Kb, kn_col)):
                    P.wait("scalar", self.EV)
                    P.op("scalar", "activation", sqs[:, 0:2048], buf[:, 0:2048],
                         AF.Square, inc=self.AC)
                    P.wait("tensor", self.AC)
                    P.wait("tensor", self.EV)
                    for n in range(4):
                        P.op("tensor", "matmul",
                             self.pA[0:2, n * 512:(n + 1) * 512], blockones[:],
                             sqs[:, n * 512:(n + 1) * 512],
                             start=True, stop=True,
                             inc=self.MM if n == 3 else None)
                    P.wait("vector", self.MM)
                    P.op("vector", "tensor_scalar_mul", rows2[0:2, 0:2048],
                         self.pA[0:2, 0:2048], 1.0 / HD, inc=self.EV)
                    P.wait("scalar", self.EV)
                    P.op("scalar", "activation", rows2[0:2, 0:2048],
                         rows2[0:2, 0:2048], AF.Ln, bias=eps6r, inc=self.AC)
                    P.wait("scalar", self.AC, self.AC.n)
                    P.op("scalar", "activation", rows2[0:2, 0:2048],
                         rows2[0:2, 0:2048], AF.Exp, scale=-0.5, inc=self.AC)
                    P.wait("tensor", self.AC)
                    for n in range(4):
                        P.op("tensor", "matmul",
                             self.pA[:, n * 512:(n + 1) * 512],
                             blockonesT[0:2, :], rows2[0:2, n * 512:
                                                       (n + 1) * 512],
                             start=True, stop=True,
                             inc=self.MM if n == 3 else None)
                    P.wait("vector", self.MM)
                    P.op("vector", "scalar_tensor_tensor", buf[:, 0:2048],
                         buf[:, 0:2048], ncol, self.pA[:, 0:2048],
                         ALU.mult, ALU.mult, inc=self.EV)
                self.barrier()
                self._attn_ptb = ptb
                self.attn_heads(
                    Qb, Kb, Vb2, causal=False, mrow=sqs[0:1, 0:2048],
                    out_fn=lambda h, c, p=p: avT[64 * h:64 * h + 64,
                                                 p * 2048 + c * 512:
                                                 p * 2048 + c * 512 + 512])
                self.barrier()
            for n in range(4):
                for m in range(DX):
                    par = m % 2
                    pc = self.pB[:, par * 512:par * 512 + 512]
                    P.wait("tensor", self.EV, lastu.get(("pprj", par), 0))
                    for k in range(DX):
                        P.op("tensor", "matmul", pc,
                             wpT[:, k * 1024 + m * 128:k * 1024 + m * 128
                                 + 128],
                             avT[:, k * 2048 + n * 512:k * 2048 + n * 512
                                 + 512],
                             start=(k == 0), stop=(k == DX - 1),
                             inc=self.MM if k == DX - 1 else None)
                    P.wait("vector", self.MM)
                    P.op("vector", "scalar_tensor_tensor",
                         self.sb["lnsum"][:, m * 512:(m + 1) * 512], pc,
                         bp_cols[:, m:m + 1],
                         self.resid[:, m * NT + bc + n * 512:
                                    m * NT + bc + n * 512 + 512],
                         ALU.add, ALU.add, inc=self.EV)
                    lastu[("pprj", par)] = self.EV.n
                self.barrier()
                self.postln_chunk(bc + n * 512, None, presummed=True)
                self.barrier()


    def slot3(self):
        P = self.P
        al = self.arena_alloc()
        wqb = [al.take(3072)]
        Qb = al.take(2048)
        Kb = al.take(2048)
        Vb2 = al.take(16 * 130)
        shf = al.take(2048)
        sqs = shf
        ptb = al.take(2048)
        ropc = al.take(2048)
        rops = al.take(2048)
        avstg = ptb
        woT = self.arena[:, 0:8192]
        projstg = self.arena[:, 8192:12288]
        av_scr = self.dram_scratch("scr_av", (DX, 128, 2048))
        h_scr = self.dram_scratch("scr_h", (DX, 128, NT))
        fc = self.sb["fcols"]
        m_all = fc[:, 0:16]
        den_all = fc[:, 16:32]
        g_all = fc[:, 32:48]
        nscal = fc[:, 48:49]
        wn1c = fc[:, 49:57]
        wn2c = fc[:, 57:65]
        rstd_tt = fc[:, 65:97]
        rows2 = self.sb["rows2"]
        rows = self.sb["rows"]
        t1f = self.sb["t1f"]
        ones1f = self.sb["c_ones_row"]
        onescb = self.sb["c_ones_col_b"]
        n8i = self.sb["c_neg8ident"]
        identf = self.sb["c_ident_f"]
        mask_st = self.sb["c_mask_st"]
        mask_ts = self.sb["c_mask_ts"]
        iota8 = self.sb["c_iota8"]
        eps6 = self.sb["c_eps"][0:1, 1:2]
        rowbuf = self.sb["rowbuf"]

        P.dma("gpsimd", ropc, self.io["c_rope_cos"], self.LD)
        P.dma("gpsimd", rops, self.io["c_rope_sin"], self.LD)
        self.load_bias_cols(wn1c, self.io["L1_ffn_attn_norm_w"], DX)
        self.load_bias_cols(wn2c, self.io["L1_ffn_moe_norm_w"], DX)
        scr_a = self.dram_scratch("scr_l1a", (NH // 2, 128, 3072))
        self.barrier()
        lastu = {}

        for b in range(B):
            bc = b * T
            for n in range(4):
                P.wait("scalar", self.EV)
                P.wait("scalar", self.MM)
                for dXi in range(DX):
                    P.op("scalar", "activation", sqs[:, 0:512],
                         self.resid[:, dXi * NT + bc + n * 512:
                                    dXi * NT + bc + n * 512 + 512],
                         AF.Square, inc=self.AC)
                    P.wait("tensor", self.AC)
                    P.op("tensor", "matmul", self.pR[0:1, 0:512], onescb[:],
                         sqs[:, 0:512], start=(dXi == 0), stop=(dXi == DX - 1),
                         inc=self.MM)
                    if dXi < DX - 1:
                        P.wait("scalar", self.MM)
                P.wait("vector", self.MM)
                P.op("vector", "tensor_scalar_mul", rows[0:1, 0:512],
                     self.pR[0:1, 0:512], 1.0 / D, inc=self.EV)
                P.wait("scalar", self.EV)
                P.op("scalar", "activation", rows[0:1, 0:512],
                     rows[0:1, 0:512], AF.Sqrt, bias=eps6, inc=self.AC)
                P.wait("vector", self.AC)
                P.op("vector", "reciprocal", rows[0:1, 0:512],
                     rows[0:1, 0:512], inc=self.EV)
                P.wait("tensor", self.EV)
                for i in range(4):
                    P.op("tensor", "matmul", self.pR[:, 4 + i:5 + i],
                         rows[0:1, i * 128:(i + 1) * 128], ones1f[0:1, 0:1],
                         start=True, stop=True,
                         inc=self.MM if i == 3 else None)
                P.wait("vector", self.MM)
                P.op("vector", "tensor_copy",
                     rstd_tt[:, b * 16 + n * 4:b * 16 + n * 4 + 4],
                     self.pR[:, 4:8], inc=self.EV)
                P.wait("tensor", self.EV)
            P.wait("vector", self.LD)
            for tt in range(16):
                for hh in range(2):
                    P.op("vector", "tensor_copy",
                         Vb2[:, tt * 130 + hh * 65 + 64:
                             tt * 130 + hh * 65 + 65],
                         self.sb["c_ones_col_b"][:],
                         inc=self.EV if (tt == 15 and hh == 1) else None)
            self.barrier()

            for p in range(NH // 2):
                par = 0
                wq = wqb[par][:, 0:1024]
                wk = wqb[par][:, 1024:2048]
                wv = wqb[par][:, 2048:3072]
                if b == 0:
                    P.wait("vector", self.MM, lastu.get(("w3a", par), 0))
                    for (wname, dst) in (("L1_ffn_wq", wq), ("L1_ffn_wk", wk),
                                         ("L1_ffn_wv", wv)):
                        if wname == "L1_ffn_wv":
                            self.prep_transposed(dst, 0, self.io[wname],
                                                 128, D, row0=p * 128,
                                                 dst_stride=128,
                                                 scale_col=lambda kX:
                                                 wn1c[:, kX:kX + 1])
                        else:
                            for blk in range(8):
                                hh = blk // 4
                                sub = ROPE_PERM[(blk % 4) * 16:
                                                (blk % 4) * 16 + 16]
                                r0 = int(sub[0])
                                self.prep_16rows(dst, blk * 16,
                                                 self.io[wname],
                                                 p * 128 + hh * 64 + r0,
                                                 wn1c)
                    P.wait("sync", self.EV)
                    P.dma("sync", scr_a[p], wqb[par][:, 0:3072], self.ST)
                else:
                    P.wait("sync", self.MM, lastu.get(("w3a", par), 0))
                    P.dma("sync", wqb[par][:, 0:3072], scr_a[p], self.LS)
                for (wtile, dstb) in ((wq, Qb), (wk, Kb)):
                    P.wait("tensor", self.LS)
                    P.wait("tensor", self.ST)
                    P.wait("tensor", self.EV)
                    for k in range(DX):
                        for n in range(4):
                            P.op("tensor", "matmul",
                                 self.pA[:, n * 512:(n + 1) * 512],
                                 wtile[:, k * 128:(k + 1) * 128],
                                 self.resid[:, k * NT + bc + n * 512:
                                            k * NT + bc + n * 512 + 512],
                                 start=(k == 0), stop=(k == DX - 1),
                                 inc=self.MM if (k == DX - 1 and n == 3)
                                 else None)
                    P.wait("vector", self.MM)
                    P.op("vector", "tensor_copy", dstb[:, 0:2048],
                         self.pA[:, 0:2048], inc=self.EV)
                for n in range(4):
                    P.wait("tensor", self.EV)
                    for i in range(4):
                        P.op("tensor", "matmul",
                             self.pR[0:1, i * 128:(i + 1) * 128],
                             rstd_tt[:, b * 16 + n * 4 + i:
                                     b * 16 + n * 4 + i + 1], identf[:],
                             start=True, stop=True,
                             inc=self.MM if i == 3 else None)
                    P.wait("vector", self.MM)
                    P.op("vector", "tensor_copy", rows[0:1, 512:1024],
                         self.pR[0:1, 0:512], inc=self.EV)
                    P.wait("tensor", self.EV)
                    P.op("tensor", "matmul",
                         self.pA[:, n * 512:(n + 1) * 512], ones1f[0:1, :],
                         rows[0:1, 512:1024],
                         start=True, stop=True,
                         inc=self.MM if n == 3 else None)
                P.wait("vector", self.MM)
                for dstb in (Qb, Kb):
                    P.op("vector", "tensor_mul", dstb[:, 0:2048],
                         dstb[:, 0:2048], self.pA[:, 0:2048], inc=self.EV)
                for dstb in (Qb, Kb):
                    P.wait("vector", self.EV)
                    P.op("vector", "stream_shuffle", shf[:, 0:2048],
                         dstb[:, 0:2048], ROPE_SHUF_MASK, inc=self.EV)
                    P.wait("vector", self.EV)
                    P.op("vector", "tensor_mul", shf[:, 0:2048], shf[:, 0:2048],
                         rops, inc=self.EV)
                    P.wait("vector", self.EV)
                    P.op("vector", "tensor_mul", dstb[:, 0:2048],
                         dstb[:, 0:2048], ropc, inc=self.EV)
                    P.wait("vector", self.EV)
                    P.op("vector", "tensor_add", dstb[:, 0:2048],
                         dstb[:, 0:2048], shf[:, 0:2048], inc=self.EV)
                P.wait("tensor", self.EV)
                for tg in range(4):
                    pvb = self.pB[:, (tg % 2) * 512:(tg % 2) * 512 + 512]
                    P.wait("tensor", self.EV, lastu.get(("pv3", tg % 2), 0))
                    for t2 in range(4):
                        tt = tg * 4 + t2
                        pv = pvb[:, t2 * 128:(t2 + 1) * 128]
                        for k in range(DX):
                            P.op("tensor", "matmul", pv,
                                 self.resid[:, k * NT + bc + tt * 128:
                                            k * NT + bc + (tt + 1) * 128],
                                 wv[:, k * 128:(k + 1) * 128],
                                 start=(k == 0), stop=(k == DX - 1),
                                 inc=self.MM if (k == DX - 1 and t2 == 3)
                                 else None)
                    mmc = self.MM.n
                    P.wait("vector", self.MM, mmc)
                    for t2 in range(tg * 4, tg * 4 + 4):
                        P.op("vector", "tensor_scalar_mul",
                             Vb2[:, t2 * 130:t2 * 130 + 64],
                             pvb[:, (t2 % 4) * 128:(t2 % 4) * 128 + 64],
                             rstd_tt[:, b * 16 + t2:b * 16 + t2 + 1])
                        P.op("vector", "tensor_scalar_mul",
                             Vb2[:, t2 * 130 + 65:t2 * 130 + 129],
                             pvb[:, (t2 % 4) * 128 + 64:
                                 (t2 % 4) * 128 + 128],
                             rstd_tt[:, b * 16 + t2:b * 16 + t2 + 1],
                             inc=self.EV if t2 % 4 == 3 else None)
                    lastu[("pv3", tg % 2)] = self.EV.n
                lastu[("w3a", par)] = self.MM.n
                self.barrier()
                fcb = self.sb["fcols"]
                rows_ = self.sb["rows"]
                blockones = self.sb["c_blockones"]
                for (buf, dst) in ((Qb, fcb[0:2, 22:23]), (Kb, fcb[0:2, 23:24])):
                    P.wait("scalar", self.EV)
                    P.wait("scalar", self.MM)
                    P.op("scalar", "activation", shf[:, 0:2048], buf[:, 0:2048],
                         AF.Square, inc=self.AC)
                    P.wait("tensor", self.AC)
                    P.wait("tensor", self.EV)
                    for n in range(4):
                        P.op("tensor", "matmul",
                             self.pA[0:2, n * 512:(n + 1) * 512], blockones[:],
                             shf[:, n * 512:(n + 1) * 512],
                             start=True, stop=True,
                             inc=self.MM if n == 3 else None)
                    P.wait("vector", self.MM)
                    P.op("vector", "reduce_max", dst, self.pA[0:2, 0:2048],
                         axis=AX.X, inc=self.EV)
                P.wait("vector", self.EV, self.EV.n)
                P.op("vector", "tensor_mul", fcb[0:2, 24:25], fcb[0:2, 22:23],
                     fcb[0:2, 23:24], inc=self.EV)
                P.wait("scalar", self.EV)
                P.op("scalar", "activation", fcb[0:2, 24:25], fcb[0:2, 24:25],
                     AF.Ln, inc=self.AC)
                P.wait("scalar", self.AC, self.AC.n)
                P.op("scalar", "activation", fcb[0:2, 24:25], fcb[0:2, 24:25],
                     AF.Exp, scale=0.5, inc=self.AC)
                P.wait("vector", self.AC)
                P.op("vector", "tensor_scalar_mul", fcb[0:2, 24:25],
                     fcb[0:2, 24:25], -0.125, inc=self.EV)
                P.wait("tensor", self.EV)
                P.op("tensor", "matmul", self.pR[0:1, 0:2], fcb[0:2, 24:25],
                     self.sb["c_ident_f"][0:2, 0:2], start=True, stop=True,
                     inc=self.MM)
                P.wait("vector", self.MM)
                P.op("vector", "tensor_copy", rows_[0:1, 0:2],
                     self.pR[0:1, 0:2], inc=self.EV)
                P.wait("tensor", self.EV)
                for hh in range(2):
                    P.op("tensor", "matmul", self.pR[:, 4 + hh:5 + hh],
                         self.sb["c_ones_row"][0:1, :],
                         rows_[0:1, hh:hh + 1], start=True, stop=True,
                         inc=self.MM if hh == 1 else None)
                P.wait("vector", self.MM)
                P.op("vector", "tensor_copy", fcb[:, 25:27], self.pR[:, 4:6],
                     inc=self.EV)
                P.wait("tensor", self.EV)
                P.wait("scalar", self.EV)
                self._attn_ptb = ptb

                def s3_out(h, c, p=p):
                    return shf[64 * h:64 * h + 64, c * 512:(c + 1) * 512]

                def s3_post(h, c, p=p, b=b):
                    lu = self._attn_lu
                    P.wait("sync", self.EV, self.EV.n)
                    P.dma("sync", av_scr[p, 64 * h:64 * h + 64,
                                         c * 512:(c + 1) * 512],
                          shf[64 * h:64 * h + 64, c * 512:(c + 1) * 512],
                          self.ST)
                    lu[("avsfree", h)] = (self.ST, self.ST.n)

                self.attn_heads(Qb, Kb, Vb2, causal=True, mrow=None,
                                out_fn=s3_out, post_fn=s3_post,
                                bound_cols=(fcb[:, 25:26], fcb[:, 26:27]))
                self.barrier()
            self.prep_transposed(woT, 0, self.io["L1_ffn_wo"], D, D)
            self.barrier()
            for n in range(4):
                P.wait("sync", self.MM, lastu.get("projstg", 0))
                for k in range(DX):
                    P.dma("sync", projstg[:, k * 512:(k + 1) * 512],
                          av_scr[k][:, n * 512:(n + 1) * 512], self.LS)
                for m in range(DX):
                    par = m % 2
                    pc = self.pB[:, par * 512:par * 512 + 512]
                    P.wait("tensor", self.LS)
                    P.wait("tensor", self.EV, lastu.get(("pprj3", par), 0))
                    for k in range(DX):
                        P.op("tensor", "matmul", pc,
                             woT[:, k * 1024 + m * 128:k * 1024 + m * 128
                                 + 128],
                             projstg[:, k * 512:(k + 1) * 512],
                             start=(k == 0), stop=(k == DX - 1),
                             inc=self.MM if k == DX - 1 else None)
                    if m == DX - 1:
                        lastu["projstg"] = self.MM.n
                    P.wait("vector", self.MM)
                    if m == 0:
                        P.wait("vector", self.ST)
                    P.op("vector", "tensor_add",
                         self.sb["lnsum"][:, m * 512:(m + 1) * 512], pc,
                         self.resid[:, m * NT + bc + n * 512:
                                    m * NT + bc + n * 512 + 512],
                         inc=self.EV)
                    lastu[("pprj3", par)] = self.EV.n
                    if m == DX - 1:
                        P.wait("sync", self.EV)
                        P.dma("sync",
                              h_scr[:, :, bc + n * 512:bc + n * 512 + 512]
                              .transpose([1, 0, 2]),
                              self.sb["lnsum"][:, 0:DX * 512].rearrange(
                                  "p (m c) -> p m c", m=DX), self.ST)
            self.barrier()

        al2 = self.arena_alloc()
        hid = self.hbuf[:, 0:HX * 512]
        tn2 = self.hbuf[:, HX * 512:HX * 512 + 4096]
        wb3 = [self.hbuf[:, HX * 512 + 4096:HX * 512 + 8192],
               self.hbuf[:, HX * 512 + 8192:HX * 512 + 12288]]
        hgb = self.hbuf[:, HX * 512 + 12288:HX * 512 + 16384]
        gateT = al2.take(64)
        b2row = al2.take(1024)[0:1, :]
        b1a = fc[:, 256:512]
        logt = fc[:, 129:161]
        top2 = fc[:, 161:169]
        fidx = fc[:, 169:171]
        rw12 = fc[:, 171:173]
        msk = fc[:, 173:181]
        w8s = fc[:, 181:213]
        idxu = self.sbuf("idxu", [128, 8], U32)
        scr_m1 = self.dram_scratch("scr_m1", (NE, HX, 128, 1024))
        scr_m2 = self.dram_scratch("scr_m2", (NE, DX, 128, 4096))
        wtmp = self.sb["wtmp"]
        identb = self.sb["c_ident_b"]
        P.wait("vector", self.ST)
        P.wait("gpsimd", self.MM)
        P.dma("gpsimd", wtmp[0:NE, 0:D], self.io["L1_ffn_gate_w"], self.LD)
        P.wait("tensor", self.LD)
        P.wait("tensor", self.EV)
        for kX in range(DX):
            P.op("tensor", "transpose", self.pT[:, kX * 8:(kX + 1) * 8],
                 wtmp[0:NE, kX * 128:(kX + 1) * 128], identb[0:NE, 0:NE],
                 inc=self.MM if kX == DX - 1 else None)
        P.wait("vector", self.MM)
        for kX in range(DX):
            P.op("vector", "tensor_scalar_mul", gateT[:, kX * 8:(kX + 1) * 8],
                 self.pT[:, kX * 8:(kX + 1) * 8], wn2c[:, kX:kX + 1],
                 inc=self.EV if kX == DX - 1 else None)
        for e in range(NE):
            self.load_bias_cols(b1a[:, e * 32:(e + 1) * 32],
                                self.io["L1_ffn_b1"][e:e + 1, :], 32)
        self.barrier()

        for g in range(NT // 512):
            gc = g * 512
            P.wait("sync", self.EV, lastu.get("hgb", 0))
            P.wait("sync", self.AC, lastu.get("hgb_a", 0))
            P.dma("sync", hgb[:, 0:DX * 512].rearrange("p (m c) -> p m c",
                                                       m=DX),
                  h_scr[:, :, gc:gc + 512].transpose([1, 0, 2]), self.LS)
            P.wait("scalar", self.LS)
            P.wait("vector", self.LS)
            for dXi in range(DX):
                P.wait("scalar", self.EV)
                P.wait("scalar", self.MM)
                P.op("scalar", "activation", sqs[:, 0:512],
                     hgb[:, dXi * 512:(dXi + 1) * 512],
                     AF.Square, inc=self.AC)
                P.wait("tensor", self.AC)
                P.op("tensor", "matmul", self.pR[0:1, 0:512], onescb[:],
                     sqs[:, 0:512], start=(dXi == 0), stop=(dXi == DX - 1),
                     inc=self.MM)
            P.wait("vector", self.MM)
            P.op("vector", "tensor_scalar_mul", rows[0:1, 0:512],
                 self.pR[0:1, 0:512], 1.0 / D, inc=self.EV)
            P.wait("scalar", self.EV)
            P.op("scalar", "activation", rows[0:1, 0:512], rows[0:1, 0:512],
                 AF.Sqrt, bias=eps6, inc=self.AC)
            P.wait("vector", self.AC)
            P.op("vector", "reciprocal", rows[0:1, 0:512], rows[0:1, 0:512],
                 inc=self.EV)
            P.wait("tensor", self.EV)
            P.op("tensor", "matmul", self.pA[:, 0:512], ones1f[0:1, :],
                 rows[0:1, 0:512], start=True, stop=True, inc=self.MM)
            P.wait("vector", self.MM)
            for dXi in range(DX):
                P.op("vector", "tensor_mul",
                     tn2[:, dXi * 512:(dXi + 1) * 512],
                     hgb[:, dXi * 512:(dXi + 1) * 512],
                     self.pA[:, 0:512], inc=self.EV)
            P.wait("tensor", self.EV)
            for k in range(DX):
                P.op("tensor", "matmul", self.pB[0:NE, 0:512],
                     gateT[:, k * NE:(k + 1) * NE],
                     tn2[:, k * 512:(k + 1) * 512],
                     start=(k == 0), stop=(k == DX - 1), inc=self.MM
                     if k == DX - 1 else None)
            P.wait("vector", self.MM)
            P.op("vector", "tensor_copy", t1f[0:NE, 0:512],
                 self.pB[0:NE, 0:512], inc=self.EV)
            P.wait("tensor", self.EV)
            for tt in range(4):
                P.op("tensor", "transpose", self.pA[:, tt * 8:(tt + 1) * 8],
                     t1f[0:NE, tt * 128:(tt + 1) * 128], identf[0:NE, 0:NE],
                     inc=self.MM if tt == 3 else None)
            P.wait("vector", self.MM)
            P.op("vector", "tensor_copy", logt, self.pA[:, 0:32], inc=self.EV)
            for tt in range(4):
                lg = logt[:, tt * 8:(tt + 1) * 8]
                P.wait("vector", self.EV)
                P.op("vector", "max", top2, lg, inc=self.EV)
                P.wait("vector", self.EV)
                P.op("vector", "max_index", idxu[:], top2, lg, inc=self.EV)
                P.wait("vector", self.EV)
                P.op("vector", "tensor_copy", fidx, idxu[:, 0:2], inc=self.EV)
                P.wait("vector", self.EV)
                P.op("vector", "tensor_sub", rw12[:, 0:1], top2[:, 1:2],
                     top2[:, 0:1], inc=self.EV)
                P.wait("scalar", self.EV)
                P.op("scalar", "activation", rw12[:, 0:1], rw12[:, 0:1],
                     AF.Sigmoid, scale=-1.0, inc=self.AC)
                P.wait("vector", self.AC)
                P.op("vector", "tensor_scalar", rw12[:, 1:2], rw12[:, 0:1],
                     -1.0, 1.0, ALU.mult, ALU.add, inc=self.EV)
                P.wait("vector", self.EV)
                P.op("vector", "tensor_scalar", msk, iota8[:],
                     fidx[:, 0:1], None, ALU.is_equal, inc=self.EV)
                P.wait("vector", self.EV)
                P.op("vector", "tensor_scalar_mul",
                     w8s[:, tt * 8:(tt + 1) * 8], msk, rw12[:, 0:1],
                     inc=self.EV)
                P.wait("vector", self.EV)
                P.op("vector", "tensor_scalar", msk, iota8[:],
                     fidx[:, 1:2], None, ALU.is_equal, inc=self.EV)
                P.wait("vector", self.EV)
                P.op("vector", "scalar_tensor_tensor",
                     w8s[:, tt * 8:(tt + 1) * 8], msk, rw12[:, 1:2],
                     w8s[:, tt * 8:(tt + 1) * 8], ALU.mult, ALU.add,
                     inc=self.EV)
            self.barrier()
            for e in range(NE):
                par = e % 2
                P.wait("tensor", self.EV)
                for tt in range(4):
                    P.op("tensor", "matmul",
                         self.pR[0:1, tt * 128:(tt + 1) * 128],
                         w8s[:, tt * 8 + e:tt * 8 + e + 1], identf[:],
                         start=True, stop=True,
                         inc=self.MM if tt == 3 else None)
                P.wait("vector", self.MM)
                P.op("vector", "tensor_copy", rows[0:1, 512:1024],
                     self.pR[0:1, 0:512], inc=self.EV)
                P.wait("tensor", self.EV)
                P.op("tensor", "matmul", self.pA[:, 1536:2048],
                     ones1f[0:1, :], rows[0:1, 512:1024],
                     start=True, stop=True, inc=self.MM)
                P.wait("vector", self.MM)
                P.op("vector", "tensor_copy", t1f[:], self.pA[:, 1536:2048],
                     inc=self.EV)
                P.wait("gpsimd", self.MM)
                P.dma("gpsimd", b2row,
                      self.io["L1_ffn_b2"][0:1, e * 1024:(e + 1) * 1024],
                      self.LD)
                P.wait("scalar", self.EV)
                P.wait("vector", self.EV)
                for mg in range(HX // 4):
                    wap = wb3[par][:, 0:4096]
                    if g == 0:
                        P.wait("vector", self.MM,
                               lastu.get(("wm1", par), 0))
                        for mm_ in range(4):
                            self.prep_transposed(
                                wap[:, mm_ * 1024:(mm_ + 1) * 1024], 0,
                                self.io["L1_ffn_w1"][e],
                                128, D, row0=(mg * 4 + mm_) * 128,
                                dst_stride=128,
                                scale_col=lambda kX: wn2c[:, kX:kX + 1])
                        P.wait("sync", self.EV)
                        P.dma("sync",
                              scr_m1[e, mg * 4:(mg + 1) * 4].transpose([1, 0, 2]),
                              wap.rearrange("p (m c) -> p m c", m=4), self.ST)
                    else:
                        P.wait("sync", self.MM, lastu.get(("wm1", par), 0))
                        P.dma("sync",
                              wap.rearrange("p (m c) -> p m c", m=4),
                              scr_m1[e, mg * 4:(mg + 1) * 4].transpose([1, 0, 2]),
                              self.LS)
                    P.wait("tensor", self.LS)
                    for mm_ in range(4):
                        m = mg * 4 + mm_
                        ph = self.pB[:, (m % 2) * 512:(m % 2) * 512 + 512]
                        P.wait("tensor", self.AC,
                               lastu.get(("ph3", m % 2), 0))
                        for k in range(DX):
                            P.op("tensor", "matmul", ph,
                                 wap[:, mm_ * 1024 + k * 128:
                                     mm_ * 1024 + (k + 1) * 128],
                                 tn2[:, k * 512:(k + 1) * 512],
                                 start=(k == 0), stop=(k == DX - 1),
                                 inc=self.MM if k == DX - 1 else None)
                        P.wait("scalar", self.MM)
                        P.op("scalar", "activation",
                             hid[:, m * 512:(m + 1) * 512], ph, AF.Gelu,
                             bias=b1a[:, e * 32 + m:e * 32 + m + 1],
                             inc=self.AC)
                        lastu[("ph3", m % 2)] = self.AC.n
                    lastu[("wm1", par)] = self.MM.n
                    par = (par + 1) % 2
                P.wait("tensor", self.LD)
                P.wait("tensor", self.AC)
                for dXi in range(DX):
                    wap = wb3[par][:, 0:4096]
                    if g == 0:
                        P.wait("vector", self.MM,
                               lastu.get(("wm1", par), 0))
                        self.prep_transposed(wap, 0,
                                             self.io["L1_ffn_w2"][e],
                                             128, HID, row0=dXi * 128,
                                             dst_stride=128)
                        P.wait("sync", self.EV)
                        P.dma("sync", scr_m2[e, dXi], wap, self.ST)
                    else:
                        P.wait("sync", self.MM, lastu.get(("wm1", par), 0))
                        P.dma("sync", wap, scr_m2[e, dXi], self.LS)
                    pz = self.pA[:, (dXi % 2) * 512:(dXi % 2) * 512 + 512]
                    P.wait("tensor", self.LS)
                    P.wait("tensor", self.EV, lastu.get(("pz3", dXi % 2), 0))
                    for k in range(HX):
                        P.op("tensor", "matmul", pz,
                             wap[:, k * 128:(k + 1) * 128],
                             hid[:, k * 512:(k + 1) * 512],
                             start=(k == 0), stop=False)
                    P.op("tensor", "matmul", pz,
                         b2row[0:1, dXi * 128:(dXi + 1) * 128],
                         self.sb["c_ones512b"][0:1, :],
                         start=False, stop=True, inc=self.MM)
                    lastu[("wm1", par)] = self.MM.n
                    P.wait("vector", self.MM)
                    P.op("vector", "tensor_mul",
                         self.sb["lnsum"][:, 0:512], pz, t1f[:], inc=self.EV)
                    lastu[("pz3", dXi % 2)] = self.EV.n
                    P.wait("vector", self.EV)
                    P.op("vector", "tensor_add",
                         self.resid[:, dXi * NT + gc:dXi * NT + gc + 512],
                         self.resid[:, dXi * NT + gc:dXi * NT + gc + 512],
                         self.sb["lnsum"][:, 0:512], inc=self.EV)
                    P.wait("vector", self.EV)
                    par = (par + 1) % 2
                P.wait("scalar", self.MM)
                P.wait("vector", self.MM)
                P.wait("tensor", self.EV)
            self.postln_chunk(gc, lambda dXi: hgb[:, dXi * 512:
                                                  (dXi + 1) * 512])
            lastu["hgb"] = self.EV.n
            lastu["hgb_a"] = self.AC.n
            self.barrier()

    def prep_16rows(self, dst, dst_sub0, src_ap, row0, wcol):
        P = self.P
        wtmp = self.sb["wtmp"]
        identb = self.sb["c_ident_b"]
        P.wait("vector", self.ST)
        P.wait("gpsimd", self.MM)
        P.wait("gpsimd", self.EV)
        P.dma("gpsimd", wtmp[0:16, 0:D], src_ap[row0:row0 + 16, :], self.LD)
        P.wait("tensor", self.LD)
        P.wait("tensor", self.EV)
        for kX in range(DX):
            P.op("tensor", "transpose", self.pT[:, kX * 16:(kX + 1) * 16],
                 wtmp[0:16, kX * 128:(kX + 1) * 128], identb[0:16, 0:16],
                 inc=self.MM if kX == DX - 1 else None)
        P.wait("vector", self.MM)
        for kX in range(DX):
            P.op("vector", "tensor_scalar_mul",
                 dst[:, kX * 128 + dst_sub0:kX * 128 + dst_sub0 + 16],
                 self.pT[:, kX * 16:(kX + 1) * 16], wcol[:, kX:kX + 1],
                 inc=self.EV if kX == DX - 1 else None)
        P.wait("tensor", self.EV)


def build_program(slots):
    nc = bass.Bass("TRN2", target_bir_lowering=False)
    with contextlib.ExitStack() as ctx:
        mk = MK(nc, ctx, slots)
        mk.setup()
        mk.load_resid()
        for s in slots:
            getattr(mk, f"slot{s}")()
        mk.store_out()
        block = ctx.enter_context(nc.Block())
        mk.P.run(block)
    return nc


def shard_inputs(inputs, core):
    im = {}
    im["x_in"] = np.ascontiguousarray(inputs["L0_attn_x"][2 * core:2 * core + 2])
    return im


def kernel(**inputs):
    from concourse.bass_utils import run_bass_kernel_spmd
    consts = host_constants()
    slots_chain = [[0, 1, 2, 3]]
    x = np.asarray(inputs["L0_attn_x"], dtype=np.float32)
    cur = x
    for slots in slots_chain:
        nc = build_program(slots)
        in_maps = []
        for core in range(8):
            im = {"x_in": np.ascontiguousarray(cur[2 * core:2 * core + 2])}
            for s in slots:
                for nm in SLOT_WEIGHTS[s]:
                    im[nm] = np.ascontiguousarray(
                        np.asarray(inputs[nm], dtype=np.float32)
                        .reshape(WEIGHT_SHAPES[nm]))
            cns = sorted({n for s in slots for n in CONST_NAMES[s]})
            for nm in cns:
                im[nm] = consts[nm]
            in_maps.append(im)
        res = run_bass_kernel_spmd(nc, in_maps, core_ids=list(range(8)))
        cur = np.concatenate([np.asarray(r["x_out"]) for r in res.results],
                             axis=0)
    return cur
